# version 13
# baseline (speedup 1.0000x reference)
import numpy as np
import concourse.bass as bass
import concourse.mybir as mybir
from concourse.bass_utils import run_bass_kernel_spmd

F32 = mybir.dt.float32
BF16 = mybir.dt.bfloat16
U8 = mybir.dt.uint8
AF = mybir.ActivationFunctionType
ALU = mybir.AluOpType

D = 1024
M1 = 256
T = 4
LR = 1e-3
NCORES = 8
GELU_C = float(1.0 / np.sqrt(2.0 * np.pi))


class Sem:
    def __init__(self, nc, name):
        self.h = nc.alloc_semaphore(name=name)
        self.n = 0

    def inc(self, instr, amt=1):
        instr.then_inc(self.h, amt)
        self.n += amt
        return self.n


class Stream:
    def __init__(self, nc, name):
        self.sems = [Sem(nc, name + "_0"), Sem(nc, name + "_1")]
        self.i = 0


class Builder:
    def __init__(self, R):
        self.R = R
        self.IT = R // 128
        self.NS = 4 if self.IT >= 8 else (2 if self.IT >= 2 else 1)
        self.ITS = self.IT // self.NS
        self.upd_c = float(LR * 2.0 / ((NCORES * R) * D))
        nc = bass.Bass("TRN2", target_bir_lowering=False)
        self.nc = nc

        self.x = nc.dram_tensor("x", [T * R, D], F32, kind="ExternalInput").ap()
        self.y = nc.dram_tensor("y", [T * R, D], F32, kind="ExternalOutput").ap()
        self.tk = nc.dram_tensor("t_k", [D, D], F32, kind="ExternalInput").ap()
        self.tv = nc.dram_tensor("t_v", [D, D], F32, kind="ExternalInput").ap()
        self.tq = nc.dram_tensor("t_q", [D, D], F32, kind="ExternalInput").ap()
        self.W1 = nc.dram_tensor("W1", [M1, D], F32, kind="ExternalInput").ap()
        self.b1 = nc.dram_tensor("b1", [M1], F32, kind="ExternalInput").ap()
        self.W2 = nc.dram_tensor("W2", [D, M1], F32, kind="ExternalInput").ap()
        self.b2 = nc.dram_tensor("b2", [D], F32, kind="ExternalInput").ap()
        self.G1R = nc.dram_tensor("G1R", [1, 32, 4, 32], F32, kind="ExternalInput").ap()
        self.G2R = nc.dram_tensor("G2R", [4, 16, 1, 16], F32, kind="ExternalInput").ap()
        self.G1I = nc.dram_tensor("G1I", [1, 32, 4, 32], F32, kind="ExternalInput").ap()
        self.G2I = nc.dram_tensor("G2I", [4, 16, 1, 16], F32, kind="ExternalInput").ap()
        self.br = nc.dram_tensor("br", [512], F32, kind="ExternalInput").ap()
        self.bi = nc.dram_tensor("bi", [512], F32, kind="ExternalInput").ap()
        self.tk_h = nc.dram_tensor("tk_h", [128, 8 * D], BF16).ap()
        self.tv_h = nc.dram_tensor("tv_h", [128, 8 * D], BF16).ap()
        self.tq_h = nc.dram_tensor("tq_h", [128, 8 * D], BF16).ap()
        self.mt_h = nc.dram_tensor("mt_h", [128, 8 * D], BF16).ap()
        self.wr_h = nc.dram_tensor("wr_h", [512, 512], BF16).ap()
        self.wi_h = nc.dram_tensor("wi_h", [512, 512], BF16).ap()
        self.g_in = nc.dram_tensor("g_in", [128 * 4096 + 1024 + 256], F32).ap()
        self.g_out = nc.dram_tensor("g_out", [128 * 4096 + 1024 + 256], F32).ap()

        self.PE = Sem(nc, "sPE")
        self.DV = Sem(nc, "sDV")
        self.AC = Sem(nc, "sAC")
        self.SY = Sem(nc, "sSY")
        self.SC = Sem(nc, "sSC")
        self.GP = Sem(nc, "sGP")
        self.CC = Sem(nc, "sCC")
        self.sx = Stream(nc, "sx")
        self.sout = Stream(nc, "sout")
        self.sslot = Stream(nc, "sslot")
        self.sg = Stream(nc, "sg")
        self.tlnx = Stream(nc, "tlnx")
        self.tsub = Stream(nc, "tsub")
        self.tder = Stream(nc, "tder")
        self.sgr = Stream(nc, "sgr")
        self.xgate = [0, 0]
        self.lngate = [None, None]
        self.xuse = 0
        self.pe_qdone = 0
        self.pe_ffndone = 0

    def on(self, eng, fn):
        getattr(self.block, eng)(fn)

    def dma_s(self, eng, stream, out, in_, **kw):
        sem = stream.sems[stream.i % 2]
        stream.i += 1
        eng.wait_ge(sem.h, sem.n)
        i = eng.dma_start(out=out, in_=in_, **kw)
        sem.inc(i, 16)
        return (sem, sem.n)

    def wtok(self, eng, tok):
        if tok is not None:
            eng.wait_ge(tok[0].h, tok[1])

    def wstream(self, eng, stream):
        for sem in stream.sems:
            if sem.n:
                eng.wait_ge(sem.h, sem.n)

    def stream_state(self, stream):
        return [(sem, sem.n) for sem in stream.sems if sem.n]

    def dma(self, eng, sem, out, in_, **kw):
        eng.wait_ge(sem.h, sem.n)
        i = eng.dma_start(out=out, in_=in_, **kw)
        sem.inc(i, 16)
        return i

    def barrier(self):
        sems = [self.PE, self.DV, self.AC, self.SY, self.SC, self.GP, self.CC]
        for stx in (self.sx, self.sout, self.sslot, self.sg, self.sgr,
                    self.tlnx, self.tsub, self.tder):
            sems += stx.sems
        vals = [(s.h, s.n) for s in sems if s.n > 0]

        def mk(e):
            for h, n in vals:
                e.wait_ge(h, n)
        for eng in ("tensor", "vector", "scalar", "sync", "gpsimd"):
            self.on(eng, mk)

    def build(self):
        nc = self.nc
        import contextlib
        stack = contextlib.ExitStack()
        sb = lambda name, shape, dt: stack.enter_context(nc.sbuf_tensor(name, shape, dt))
        IT, ITS = self.IT, self.ITS
        self.W1n_f = sb("W1n_f", [128, 2 * D], F32)
        self.W2n_f = sb("W2n_f", [128, 8 * M1], F32)
        self.W2n_b = sb("W2n_b", [128, 8 * M1], BF16)
        self.W1T_b = sb("W1T_b", [128, 8 * M1], BF16)
        self.W2T_b = sb("W2T_b", [128, 2 * D], BF16)
        self.b1r_f = sb("b1r_f", [1, M1], F32)
        self.b2r_f = sb("b2r_f", [1, D], F32)
        self.b1r_b = sb("b1r_b", [1, M1], BF16)
        self.b2r_b = sb("b2r_b", [1, D], BF16)
        self.fbias_b = sb("fbias_b", [1, D], BF16)
        self.ones_r = sb("ones_r", [1, 128], BF16)
        self.ones_c = sb("ones_c", [128, 1], BF16)
        self.epsc = sb("epsc", [128, 1], F32)
        self.epsT = sb("epsT", [128, M1], F32)
        self.halfT = sb("halfT", [128, M1], F32)
        self.slotA = sb("slotA", [128, 8 * D], BF16)
        self.slotB = sb("slotB", [128, 8 * D], BF16)
        self.lnx_fm = sb("lnx_fm", [128, IT * D], BF16)
        self.z_rm = sb("z_rm", [128, IT * D], BF16)
        self.y_rm = sb("y_rm", [128, IT * D], BF16)
        self.q_rm = sb("q_rm", [128, IT * D], BF16)
        self.gW1a = sb("gW1a", [128, 2 * D], F32)
        self.gW2a = sb("gW2a", [128, 8 * M1], F32)
        self.b1ar = sb("b1ar", [1, M1], F32)
        self.b2ar = sb("b2ar", [1, D], F32)
        self.st6 = sb("st6", [128, 12], F32)
        self.mv = sb("mv", [128, 2], F32)
        self.rstd = sb("rstd", [128, 1], F32)
        self.x01 = sb("x01", [128, 2 * D], F32)
        self.lnxr = sb("lnxr", [128, 2 * D], BF16)
        self.W1n_b = self.lnxr.ap()
        self.zfm = sb("zfm", [128, ITS * D], BF16)
        self.erm = sb("erm", [128, ITS * D], BF16)
        self.efm = sb("efm", [128, ITS * D], BF16)
        self.hrm = sb("hrm", [128, ITS * M1], BF16)
        self.hfm = sb("hfm", [128, ITS * M1], BF16)
        self.gpb = sb("gpb", [128, ITS * M1], BF16)
        self.dab = sb("dab", [128, ITS * M1], BF16)
        self.xa01 = sb("xa01", [128, 2 * M1], BF16)
        self.sq01 = sb("sq01", [128, 2 * M1], BF16)
        self.ex01 = sb("ex01", [128, 2 * M1], BF16)
        self.ms01 = sb("ms01", [128, M1], U8)
        self.xs01 = sb("xs01", [128, M1], F32)
        self.ph01 = sb("ph01", [128, M1], F32)
        self.x2s = sb("x2s", [128, ITS * D], F32)
        self.ln2r = sb("ln2r", [128, ITS * D], BF16)
        self.ln2f = sb("ln2f", [128, ITS * D], BF16)
        self.p1sb = self.erm[:, 0:2048]
        self.g1sb = self.efm[0:4, 0:1024]
        self.g2sb = self.efm[0:4, 1024:1280]

        import os
        nphase = int(os.environ.get("ATH_PHASES", "99"))
        with nc.Block() as block:
            self.block = block
            self.init_phase()
            if nphase >= 1:
                self._init_gates = list(getattr(self, "_stage_gates", []))
                nT = min(T, nphase)
                self.emit_A(0)
                self._mt_gate_once = getattr(self, "_mt_store_gate", None)
                self.emit_q(0)
                p1g = (self.stream_state(self.sg) + self.stream_state(self.sout)
                       + [(self.GP, self.GP.n)])

                def pre_chain_gate(e, p1g=p1g):
                    for sem, v in p1g:
                        e.wait_ge(sem.h, v)
                self.on("vector", pre_chain_gate)
                self.on("scalar", pre_chain_gate)
                for t in range(nT):
                    for s_i in range(self.NS):
                        self.chain_subtile(t, s_i)
                    self.ar_start(t)
                    if t + 1 < nT:
                        self.emit_A(t + 1)
                    self.ar_finish(t)
                    self.emit_D(t)
                    if t + 1 < nT:
                        self.emit_q(t + 1)
            flush = [(self.SY, self.SY.n)]
            for stx in (self.sx, self.sout, self.sslot, self.sg, self.sgr,
                        self.tlnx, self.tsub, self.tder):
                flush += self.stream_state(stx)

            def fin(s, flush=flush):
                for sem, v in flush:
                    if v:
                        s.wait_ge(sem.h, v)
            self.on("sync", fin)
        stack.close()
        return nc

    def init_phase(self):
        import os
        lvl = int(os.environ.get("ATH_INIT", "99"))
        nc = self.nc

        def gp(g):
            self.dma(g, self.GP, out=self.W1n_f.ap().rearrange("p (c f) -> p c f", c=2),
                     in_=self.W1.rearrange("(c p) f -> p c f", p=128))
            self.dma(g, self.GP, out=self.W2n_f.ap().rearrange("p (c f) -> p c f", c=8),
                        in_=self.W2.rearrange("(c p) f -> p c f", p=128))
            self.dma(g, self.GP, out=self.b1r_f[:, :], in_=self.b1[None, :])
            self.dma(g, self.GP, out=self.b2r_f[:, :], in_=self.b2[None, :])
            self.dma(g, self.GP, out=self.fbias_b[:, 0:512], in_=self.br[None, :])
            self.dma(g, self.GP, out=self.fbias_b[:, 512:1024], in_=self.bi[None, :])
        self.on("gpsimd", gp)

        def dv(v):
            v.memset(self.ones_r[:, :], 1.0)
            v.memset(self.ones_c[:, :], 1.0)
            v.memset(self.epsc[:, :], 1e-5)
            v.memset(self.epsT[:, :], 1e-12)
            i = v.memset(self.halfT[:, :], 0.5)
            self.DV.inc(i)
        self.on("vector", dv)

        if lvl < 2:
            return
        for wi, (src, dst) in enumerate([(self.tk, self.tk_h), (self.tv, self.tv_h),
                                         (self.tq, self.tq_h)]):
            st_before = self.stream_state(self.tsub)

            def gpl(g, src=src, wi=wi, st_before=st_before):
                if wi > 0:
                    for tk_ in st_before:
                        g.wait_ge(tk_[0].h, tk_[1])
                self.dma(g, self.GP, out=self.slotA.ap().rearrange("p (c f) -> p c f", c=8),
                            in_=src.rearrange("(c p) f -> p c f", p=128))
            self.on("gpsimd", gpl)
            gp_now = self.GP.n

            def scl(s, gp_now=gp_now):
                s.wait_ge(self.GP.h, gp_now)
                for hc in range(8):
                    self.dma_s(s, self.tsub,
                        out=self.slotB.ap().rearrange("p (dc m) -> p dc m", dc=8)[:, :, hc * 128:(hc + 1) * 128],
                        in_=self.slotA[:, hc * D:(hc + 1) * D], transpose=True)
            self.on("scalar", scl)
            st_t = self.stream_state(self.tsub)

            def syl(s, dst=dst, st_t=st_t):
                for tk_ in st_t:
                    s.wait_ge(tk_[0].h, tk_[1])
                self.dma(s, self.SY, out=dst, in_=self.slotB[:, :])
            self.on("sync", syl)
            sy_now = self.SY.n

            def scw(s, sy_now=sy_now):
                s.wait_ge(self.SY.h, sy_now)
            self.on("scalar", scw)

        self._stage_gates = ([(self.SY, self.SY.n)]
                             + self.stream_state(self.tsub))
        if lvl < 3:
            return
        for part, wdst in enumerate([self.wr_h, self.wi_h]):
            if part == 0:
                def gpl0(g):
                    self.dma(g, self.GP, out=self.g1sb.rearrange("p (b d) -> p b d", b=32),
                             in_=self.G1R.rearrange("a b c d -> (a c) b d"))
                    self.dma(g, self.GP, out=self.g2sb.rearrange("p (e g) -> p e g", e=16),
                             in_=self.G2R.rearrange("c e f g -> (c f) e g"))
                self.on("gpsimd", gpl0)
            if part == 1:
                sy_prev = self.SY.n

                st_p = self.stream_state(self.sg) + self.stream_state(self.sout)

                def gpl2(g, st_p=st_p):
                    for tk_ in st_p:
                        g.wait_ge(tk_[0].h, tk_[1])
                    self.dma(g, self.GP, out=self.g1sb.rearrange("p (b d) -> p b d", b=32),
                                in_=self.G1I.rearrange("a b c d -> (a c) b d"))
                    self.dma(g, self.GP, out=self.g2sb.rearrange("p (e g) -> p e g", e=16),
                                in_=self.G2I.rearrange("c e f g -> (c f) e g"))
                self.on("gpsimd", gpl2)
            gp_now = self.GP.n
            with nc.psum_tensor("ps_init%d" % part, [128, 1024], F32) as ps_init:
                ac_base = self.AC.n
                pe_base = self.PE.n

                def pe(t, gp_now=gp_now, ac_base=ac_base, ps_init=ps_init):
                    t.wait_ge(self.GP.h, gp_now)
                    for m in range(8):
                        if m >= 2:
                            t.wait_ge(self.AC.h, ac_base + m - 1)
                        i = t.matmul(ps_init[:, (m % 2) * 512:(m % 2) * 512 + 256],
                                     self.g1sb[:, m * 128:(m + 1) * 128], self.g2sb[:, :],
                                     start=True, stop=True)
                        self.PE.inc(i)
                self.on("tensor", pe)

                def ac(s, pe_base=pe_base, ps_init=ps_init):
                    for m in range(8):
                        s.wait_ge(self.PE.h, pe_base + m + 1)
                        i = s.copy(self.p1sb[:, m * 256:(m + 1) * 256],
                                   ps_init[:, (m % 2) * 512:(m % 2) * 512 + 256])
                        self.AC.inc(i)
                self.on("scalar", ac)
            ac_now = self.AC.n

            def syp(s, wdst=wdst, ac_now=ac_now):
                import os
                s.wait_ge(self.AC.h, ac_now)
                if os.environ.get("ATH_NOSYP"):
                    return
                wt4 = wdst.rearrange("(b e) (d g) -> b d e g", e=16, g=16)
                k = 0
                for m in range(8):
                    for bl in range(4):
                        b = m * 4 + bl
                        self.dma_s(s, (self.sg, self.sout)[k % 2],
                            out=wt4[b, :, :, :],
                            in_=self.p1sb[bl * 32:(bl + 1) * 32, m * 256:(m + 1) * 256]
                            .rearrange("p (e g) -> p e g", e=16))
                        k += 1
            self.on("sync", syp)

        if lvl < 4:
            return
        st_p2 = self.stream_state(self.sg) + self.stream_state(self.sout)

        def sym(s, st_p2=st_p2):
            for tk_ in st_p2:
                s.wait_ge(tk_[0].h, tk_[1])
            mtb = self.q_rm if self.IT >= 8 else self.slotB
            v = mtb.ap().rearrange("p (kc c) -> p kc c", kc=8)
            for (w, kgrp, half) in ((self.wr_h, 0, 0), (self.wi_h, 1, 0),
                                    (self.wi_h, 0, 1), (self.wr_h, 1, 1)):
                for r in range(4):
                    self.dma_s(s, self.tsub,
                        out=v[:, kgrp * 4:(kgrp + 1) * 4,
                              half * 512 + r * 128: half * 512 + (r + 1) * 128],
                        in_=w[r * 128:(r + 1) * 128, :], transpose=True)
        self.on("scalar", sym)
        st_mt = self.stream_state(self.tsub)

        def dvn(v, st_mt=st_mt):
            for tk_ in st_mt:
                v.wait_ge(tk_[0].h, tk_[1])
            mtb = self.q_rm if self.IT >= 8 else self.slotB
            for kc in range(4, 8):
                i = v.tensor_scalar_mul(mtb[:, kc * D:kc * D + 512],
                                        mtb[:, kc * D:kc * D + 512], -1.0)
                self.DV.inc(i)
        self.on("vector", dvn)
        dv_now = self.DV.n

        def sym2(s, dv_now=dv_now):
            s.wait_ge(self.DV.h, dv_now)
            self.dma(s, self.SY, out=self.mt_h,
                     in_=(self.q_rm if self.IT >= 8 else self.slotB)[:, :])
        self.on("sync", sym2)
        self._mt_store_gate = (self.SY, self.SY.n)
        if self.IT < 8:
            self._stage_gates = ([(self.SY, self.SY.n)]
                                 + self.stream_state(self.tsub)
                                 + self.stream_state(self.sg)
                                 + self.stream_state(self.sout))

        if lvl < 5:
            return
        def dvd(v):
            v.wait_ge(self.GP.h, 16 * 4)
            self.emit_derive_dv(v)
        self.on("vector", dvd)
        dv_now = self.DV.n

        def scd(s, dv_now=dv_now):
            s.wait_ge(self.DV.h, dv_now)
            self.emit_derive_sc(s)
        self.on("scalar", scd)

    def emit_derive_dv(self, v):
        i = v.tensor_copy(self.W1n_b[:, :], self.W1n_f[:, :]); self.DV.inc(i)
        i = v.tensor_copy(self.W2n_b[:, :], self.W2n_f[:, :]); self.DV.inc(i)
        i = v.tensor_copy(self.b1r_b[:, :], self.b1r_f[:, :]); self.DV.inc(i)
        i = v.tensor_copy(self.b2r_b[:, :], self.b2r_f[:, :]); self.DV.inc(i)

    def emit_derive_sc(self, s, init=False, part="all"):
        dma = (lambda *a, **k: self.dma(s, self.SC, *a, **k)) if init else \
              (lambda *a, **k: self.dma_s(s, self.tder, *a, **k))
        if part in ("all", "w1"):
            w1toks = []
            for mc in range(2):
                w1toks.append(dma(
                    out=self.W1T_b.ap().rearrange("p (hc m) -> p hc m", hc=8)[:, :, mc * 128:(mc + 1) * 128],
                    in_=self.W1n_b[:, mc * D:(mc + 1) * D], transpose=True))
            if not init:
                self._w1toks = w1toks
        if part in ("all", "w2"):
            w2toks = []
            for jc in range(8):
                w2toks.append(dma(
                    out=self.W2T_b.ap().rearrange("p (m2c m) -> p m2c m", m2c=2)[:, :, jc * 128:(jc + 1) * 128],
                    in_=self.W2n_b[:, jc * 256:(jc + 1) * 256], transpose=True))
            if not init:
                self._w2toks = w2toks

    def emit_A(self, t):
        nc = self.nc
        R, IT, NS, ITS = self.R, self.IT, self.NS, self.ITS

        pe_slotA = self.pe_qdone
        pe_slotB = self.pe_ffndone

        toks = []

        gates = list(getattr(self, "_init_gates", []) or [])
        self._init_gates = None

        def syA(s, pe_slotA=pe_slotA, pe_slotB=pe_slotB, gates=gates):
            s.wait_ge(self.PE.h, pe_slotA)
            if pe_slotB:
                s.wait_ge(self.PE.h, pe_slotB)
            for sem, v in gates:
                s.wait_ge(sem.h, v)
            toks.append(self.dma_s(s, self.sslot, out=self.slotA[:, :], in_=self.tk_h))
            toks.append(self.dma_s(s, self.sslot, out=self.slotB[:, :], in_=self.tv_h))
        self.on("sync", syA)
        slot_toks = list(toks)

        with nc.psum_tensor("zps0_%d" % t, [128, D], F32) as zps0, \
             nc.psum_tensor("zps1_%d" % t, [128, D], F32) as zps1, \
             nc.psum_tensor("yps0_%d" % t, [128, D], F32) as yps0, \
             nc.psum_tensor("yps1_%d" % t, [128, D], F32) as yps1:
            zps = [zps0, zps1]
            yps = [yps0, yps1]
            a_ac0 = self.AC.n
            evacs = {}
            for it in range(IT):
                self.emit_ln_x(t, it, self.lnxr, it % 2)

                dv_lnc = self.DV.n
                ttok = []

                def sct(s, it=it, dv_lnc=dv_lnc):
                    s.wait_ge(self.DV.h, dv_lnc)
                    ttok.append(self.dma_s(s, self.tlnx,
                             out=self.lnx_fm.ap().rearrange("p (c m) -> p c m", m=128)[:, it * 8:(it + 1) * 8, :],
                             in_=self.lnxr[:, (it % 2) * D:(it % 2 + 1) * D], transpose=True))
                self.on("scalar", sct)
                self.lngate[it % 2] = ttok[0]
                lnx_tok = ttok[0]

                prev_evac = evacs.get(it - 2)

                def pezy(tt, it=it, lnx_tok=lnx_tok, prev_evac=prev_evac,
                         slot_toks=slot_toks, a_ac0=a_ac0):
                    self.wtok(tt, lnx_tok)
                    for tk_ in slot_toks:
                        self.wtok(tt, tk_)
                    if prev_evac is not None:
                        tt.wait_ge(self.AC.h, prev_evac[0])
                        tt.wait_ge(self.DV.h, prev_evac[1])
                    else:
                        tt.wait_ge(self.AC.h, a_ac0)
                    for dst, w in ((zps[it % 2], self.slotA), (yps[it % 2], self.slotB)):
                        for h in range(2):
                            for dc in range(8):
                                i = tt.matmul(dst[:, h * 512:(h + 1) * 512],
                                              self.lnx_fm[:, (it * 8 + dc) * 128:(it * 8 + dc + 1) * 128],
                                              w[:, dc * D + h * 512: dc * D + (h + 1) * 512],
                                              start=(dc == 0), stop=(dc == 7))
                            self.PE.inc(i)
                self.on("tensor", pezy)
                pe_n = self.PE.n

                def acz(s, it=it, pe_n=pe_n):
                    s.wait_ge(self.PE.h, pe_n - 2)
                    i = s.copy(self.z_rm[:, it * D:(it + 1) * D], zps[it % 2][:, :])
                    self.AC.inc(i)
                self.on("scalar", acz)
                ac_z = self.AC.n

                def dvy(v, it=it, pe_n=pe_n, ac_z=ac_z):
                    v.wait_ge(self.PE.h, pe_n)
                    v.wait_ge(self.AC.h, ac_z)
                    i = v.tensor_tensor(self.y_rm[:, it * D:(it + 1) * D],
                                        self.z_rm[:, it * D:(it + 1) * D],
                                        yps[it % 2][:, :], ALU.subtract)
                    self.DV.inc(i)
                self.on("vector", dvy)
                evacs[it] = (self.AC.n, self.DV.n)

    def emit_q(self, t):
        nc = self.nc
        R, IT = self.R, self.IT
        pe_zy = self.PE.n

        qtk = []

        mtg = getattr(self, "_mt_gate_once", None)
        self._mt_gate_once = None

        def syq(s, pe_zy=pe_zy, mtg=mtg):
            if mtg is not None:
                s.wait_ge(mtg[0].h, mtg[1])
            s.wait_ge(self.PE.h, pe_zy)
            qtk.append(self.dma_s(s, self.sslot, out=self.slotA[:, :], in_=self.tq_h))
        self.on("sync", syq)
        slot_tok = qtk[0]
        lnx_state = self.stream_state(self.tlnx)
        a_ac = self.AC.n
        a_dv = self.DV.n

        with nc.psum_tensor("qps0_%d" % t, [128, D], F32) as qps0, \
             nc.psum_tensor("qps1_%d" % t, [128, D], F32) as qps1:
            qps = [qps0, qps1]
            qevac = {}
            for it in range(IT):
                prev = qevac.get(it - 2)

                def peq(tt, it=it, prev=prev, slot_tok=slot_tok, a_ac=a_ac, a_dv=a_dv,
                        lnx_state=lnx_state):
                    self.wtok(tt, slot_tok)
                    for tk_ in lnx_state:
                        tt.wait_ge(tk_[0].h, tk_[1])
                    if prev is not None:
                        tt.wait_ge(self.AC.h, prev)
                    else:
                        tt.wait_ge(self.AC.h, a_ac)
                        tt.wait_ge(self.DV.h, a_dv)
                    for h in range(2):
                        for dc in range(8):
                            i = tt.matmul(qps[it % 2][:, h * 512:(h + 1) * 512],
                                          self.lnx_fm[:, (it * 8 + dc) * 128:(it * 8 + dc + 1) * 128],
                                          self.slotA[:, dc * D + h * 512: dc * D + (h + 1) * 512],
                                          start=(dc == 0), stop=(dc == 7))
                        self.PE.inc(i)
                self.on("tensor", peq)
                pe_n = self.PE.n

                def acq(s, it=it, pe_n=pe_n):
                    s.wait_ge(self.PE.h, pe_n)
                    i = s.copy(self.q_rm[:, it * D:(it + 1) * D], qps[it % 2][:, :])
                    self.AC.inc(i)
                self.on("scalar", acq)
                qevac[it] = self.AC.n
        self.pe_qdone = self.PE.n

    def emit_D(self, t):
        nc = self.nc
        pe_b = self.PE.n

        mtk = []

        def syd(s, pe_b=pe_b):
            s.wait_ge(self.PE.h, pe_b)
            mtk.append(self.dma_s(s, self.sslot, out=self.slotB[:, :], in_=self.mt_h))
        self.on("sync", syd)
        self.mt_tok = mtk[0]
        for s_i in range(self.NS):
            self.recon_subtile(t, s_i)
        self.pe_ffndone = self.PE.n

    def emit_ln_x(self, t, it, dstbuf, slot):
        R = self.R
        gate = self.xgate[self.xuse % 2]
        xslot = self.xuse % 2
        self.xuse += 1

        xt = []

        def syx(s, t=t, it=it, gate=gate, xslot=xslot):
            if gate:
                s.wait_ge(self.DV.h, gate)
            xt.append(self.dma_s(s, self.sx, out=self.x01[:, xslot * D:(xslot + 1) * D],
                      in_=self.x[t * R + it * 128: t * R + (it + 1) * 128, :]))
        self.on("sync", syx)
        x_tok = xt[0]
        lngate = self.lngate[slot]

        def dvln(v, x_tok=x_tok, xslot=xslot):
            self.wtok(v, x_tok)
            sl = self.x01[:, xslot * D:(xslot + 1) * D]
            v.bn_stats(self.st6[:, 0:6], sl[:, 0:512])
            i = v.bn_stats(self.st6[:, 6:12], sl[:, 512:1024]); self.DV.inc(i)
            v.wait_ge(self.DV.h, self.DV.n)
            i = v.bn_aggr(self.mv[:, :], self.st6[:, :]); self.DV.inc(i)
        self.on("vector", dvln)
        dv_n = self.DV.n

        def acln(s, dv_n=dv_n):
            s.wait_ge(self.DV.h, dv_n)
            i = s.activation(self.rstd[:, :], self.mv[:, 1:2], AF.Sqrt,
                             bias=self.epsc[:, 0:1], scale=1.0)
            self.AC.inc(i)
        self.on("scalar", acln)
        ac_n = self.AC.n

        def dvln2(v, ac_n=ac_n, xslot=xslot, slot=slot, lngate=lngate):
            v.wait_ge(self.AC.h, ac_n)
            self.wtok(v, lngate)
            i = v.reciprocal(self.rstd[:, :], self.rstd[:, :]); self.DV.inc(i)
            v.wait_ge(self.DV.h, self.DV.n)
            sl = self.x01[:, xslot * D:(xslot + 1) * D]
            i = v.tensor_scalar(dstbuf[:, slot * D:(slot + 1) * D],
                                sl, self.mv[:, 0:1], self.rstd[:, 0:1],
                                ALU.subtract, ALU.mult)
            self.DV.inc(i)
        self.on("vector", dvln2)
        self.xgate[xslot] = self.DV.n

    def chain_subtile(self, t, s_i):
        nc = self.nc
        ITS = self.ITS
        base_it = s_i * ITS
        ac_pre = self.AC.n
        dv_pre = self.DV.n
        pe_pre = self.PE.n

        ztoks = []

        def scz(s, base_it=base_it, ac_pre=ac_pre, pe_pre=pe_pre):
            s.wait_ge(self.AC.h, ac_pre)
            s.wait_ge(self.PE.h, pe_pre)
            for il in range(ITS):
                it = base_it + il
                ztoks.append(self.dma_s(s, self.tsub,
                    out=self.zfm.ap().rearrange("p (c m) -> p c m", m=128)[:, il * 8:(il + 1) * 8, :],
                    in_=self.z_rm[:, it * D:(it + 1) * D], transpose=True))
        self.on("scalar", scz)
        st_der = self.stream_state(self.tder)

        with nc.psum_tensor("aps0_%d_%d" % (t, s_i), [128, 512], F32) as aps0, \
             nc.psum_tensor("aps1_%d_%d" % (t, s_i), [128, 512], F32) as aps1, \
             nc.psum_tensor("rps0_%d_%d" % (t, s_i), [128, D], F32) as rps0, \
             nc.psum_tensor("rps1_%d_%d" % (t, s_i), [128, D], F32) as rps1, \
             nc.psum_tensor("dps0_%d_%d" % (t, s_i), [128, 512], F32) as dps0, \
             nc.psum_tensor("dps1_%d_%d" % (t, s_i), [128, 512], F32) as dps1:
            aps = [aps0, aps1]
            rps = [rps0, rps1]
            dps = [dps0, dps1]
            h_done = {}
            acw = {}
            htoks = []
            for il in range(ITS):
                prev_ac = acw.get(il - 2, 0)

                def pea(tt, il=il, st_der=st_der, prev_ac=prev_ac,
                        ac_pre=ac_pre, dv_pre=dv_pre):
                    for tk_ in [ztoks[il]] + st_der:
                        tt.wait_ge(tk_[0].h, tk_[1])
                    if prev_ac:
                        tt.wait_ge(self.AC.h, prev_ac)
                    else:
                        tt.wait_ge(self.AC.h, ac_pre)
                        tt.wait_ge(self.DV.h, dv_pre)
                    for hc in range(8):
                        tt.matmul(aps[il % 2][:, 0:256],
                                  self.zfm[:, (il * 8 + hc) * 128:(il * 8 + hc + 1) * 128],
                                  self.W1T_b[:, hc * 256:(hc + 1) * 256],
                                  start=(hc == 0), stop=False)
                    i = tt.matmul(aps[il % 2][:, 0:256], self.ones_r[:, :],
                                  self.b1r_b[:, :], start=False, stop=True)
                    self.PE.inc(i)
                self.on("tensor", pea)
                pe_a = self.PE.n
                dv_gate = h_done.get(il - 1, (0, 0))[1]

                def aca(s, il=il, pe_a=pe_a, dv_gate=dv_gate):
                    s.wait_ge(self.PE.h, pe_a)
                    if dv_gate:
                        s.wait_ge(self.DV.h, dv_gate)
                    ap = aps[il % 2][:, 0:256]
                    sl = slice((il % 2) * 256, (il % 2 + 1) * 256)
                    i = s.activation(self.hrm[:, il * 256:(il + 1) * 256], ap, AF.Gelu)
                    self.AC.inc(i)
                    i = s.copy(self.xa01[:, sl], ap); self.AC.inc(i)
                    i = s.activation(self.sq01[:, sl], ap, AF.Square); self.AC.inc(i)
                self.on("scalar", aca)
                acw[il] = self.AC.n
                ac_hil = self.AC.n

                def schil(s, il=il, ac_hil=ac_hil):
                    s.wait_ge(self.AC.h, ac_hil)
                    htoks.append(self.dma_s(s, self.tsub,
                        out=self.hfm.ap().rearrange("p (c m) -> p c m", m=128)[:, il * 2:(il + 1) * 2, :],
                        in_=self.hrm[:, il * 256:(il + 1) * 256], transpose=True))
                self.on("scalar", schil)

            def acE(s):
                s.wait_ge(self.AC.h, self.AC.n)
                for il in range(ITS):
                    sl = slice((il % 2) * 256, (il % 2 + 1) * 256)
                    i = s.activation(self.ex01[:, sl], self.sq01[:, sl], AF.Exp, scale=-0.5)
                    self.AC.inc(i)
            self.on("scalar", acE)
            ac_a = self.AC.n
            for il in range(ITS):
                def dvg(v, il=il, ac_a=ac_a):
                    v.wait_ge(self.AC.h, ac_a)
                    v.wait_ge(self.DV.h, self.DV.n)
                    sl = slice((il % 2) * 256, (il % 2 + 1) * 256)
                    i = v.tensor_single_scalar(self.ms01[:, :], self.sq01[:, sl], 1e-24, ALU.is_lt)
                    self.DV.inc(i)
                    i = v.tensor_copy(self.xs01[:, :], self.xa01[:, sl]); self.DV.inc(i)
                    v.wait_ge(self.DV.h, self.DV.n)
                    i = v.copy_predicated(self.xs01[:, :], self.ms01[:, :], self.epsT[:, :])
                    self.DV.inc(i)
                    v.wait_ge(self.DV.h, self.DV.n)
                    i = v.reciprocal(self.xs01[:, :], self.xs01[:, :]); self.DV.inc(i)
                    v.wait_ge(self.DV.h, self.DV.n)
                    i = v.tensor_tensor(self.ph01[:, :], self.hrm[:, il * 256:(il + 1) * 256],
                                        self.xs01[:, :], ALU.mult)
                    self.DV.inc(i)
                    v.wait_ge(self.DV.h, self.DV.n)
                    i = v.copy_predicated(self.ph01[:, :], self.ms01[:, :], self.halfT[:, :])
                    self.DV.inc(i)
                    i = v.tensor_tensor(self.xs01[:, :], self.xa01[:, sl], self.ex01[:, sl], ALU.mult)
                    self.DV.inc(i)
                    v.wait_ge(self.DV.h, self.DV.n)
                    i = v.scalar_tensor_tensor(self.gpb[:, il * 256:(il + 1) * 256],
                                               self.xs01[:, :], GELU_C, self.ph01[:, :],
                                               ALU.mult, ALU.add)
                    self.DV.inc(i)
                self.on("vector", dvg)
                h_done[il] = (self.AC.n, self.DV.n)



            edv = {}
            etoks = []
            for il in range(ITS):
                it = base_it + il
                prev_dv = edv.get(il - 2, 0)

                def per(tt, il=il, prev_dv=prev_dv):
                    self.wtok(tt, htoks[il])
                    if prev_dv:
                        tt.wait_ge(self.DV.h, prev_dv)
                    for h in range(2):
                        for m2c in range(2):
                            tt.matmul(rps[il % 2][:, h * 512:(h + 1) * 512],
                                      self.hfm[:, (il * 2 + m2c) * 128:(il * 2 + m2c + 1) * 128],
                                      self.W2T_b[:, m2c * D + h * 512:m2c * D + (h + 1) * 512],
                                      start=(m2c == 0), stop=False)
                        i = tt.matmul(rps[il % 2][:, h * 512:(h + 1) * 512],
                                      self.ones_r[:, :], self.b2r_b[:, h * 512:(h + 1) * 512],
                                      start=False, stop=True)
                        self.PE.inc(i)
                self.on("tensor", per)
                pe_r = self.PE.n

                def dve(v, il=il, it=it, pe_r=pe_r):
                    v.wait_ge(self.PE.h, pe_r)
                    i = v.tensor_tensor(self.erm[:, il * D:(il + 1) * D], rps[il % 2][:, :],
                                        self.y_rm[:, it * D:(it + 1) * D], ALU.add)
                    self.DV.inc(i)
                self.on("vector", dve)
                edv[il] = self.DV.n
                dv_eil = self.DV.n

                def sceil(s, il=il, dv_eil=dv_eil):
                    s.wait_ge(self.DV.h, dv_eil)
                    etoks.append(self.dma_s(s, self.tsub,
                        out=self.efm.ap().rearrange("p (c m) -> p c m", m=128)[:, il * 8:(il + 1) * 8, :],
                        in_=self.erm[:, il * D:(il + 1) * D], transpose=True))
                self.on("scalar", sceil)

            dadv = {}
            for il in range(ITS):
                prev_dv = dadv.get(il - 2, 0)

                def ped(tt, il=il, prev_dv=prev_dv):
                    self.wtok(tt, etoks[il])
                    if prev_dv:
                        tt.wait_ge(self.DV.h, prev_dv)
                    for jc in range(8):
                        i = tt.matmul(dps[il % 2][:, 0:256],
                                      self.efm[:, (il * 8 + jc) * 128:(il * 8 + jc + 1) * 128],
                                      self.W2n_b[:, jc * 256:(jc + 1) * 256],
                                      start=(jc == 0), stop=(jc == 7))
                    self.PE.inc(i)
                self.on("tensor", ped)
                pe_d = self.PE.n

                def dvda(v, il=il, pe_d=pe_d):
                    v.wait_ge(self.PE.h, pe_d)
                    i = v.tensor_tensor(self.dab[:, il * 256:(il + 1) * 256],
                                        dps[il % 2][:, 0:256],
                                        self.gpb[:, il * 256:(il + 1) * 256], ALU.mult)
                    self.DV.inc(i)
                self.on("vector", dvda)
                dadv[il] = self.DV.n
            loops_dv = self.DV.n
            loops_ac = self.AC.n

        with nc.psum_tensor("gw2ps_%d_%d" % (t, s_i), [128, 8 * 256], F32) as gw2ps, \
             nc.psum_tensor("gw1ps_%d_%d" % (t, s_i), [128, 2 * D], F32) as gw1ps:

            def peg(tt, base_it=base_it, loops_dv=loops_dv, loops_ac=loops_ac,
                    gw2ps=gw2ps, gw1ps=gw1ps):
                tt.wait_ge(self.DV.h, loops_dv)
                tt.wait_ge(self.AC.h, loops_ac)
                for jc in range(8):
                    for il in range(ITS):
                        i = tt.matmul(gw2ps[:, jc * 256:(jc + 1) * 256],
                                      self.erm[:, il * D + jc * 128: il * D + (jc + 1) * 128],
                                      self.hrm[:, il * 256:(il + 1) * 256],
                                      start=(il == 0), stop=(il == ITS - 1))
                    self.PE.inc(i)
                for m1c in range(2):
                    for h in range(2):
                        for il in range(ITS):
                            it = base_it + il
                            i = tt.matmul(gw1ps[:, m1c * D + h * 512: m1c * D + (h + 1) * 512],
                                          self.dab[:, il * 256 + m1c * 128: il * 256 + (m1c + 1) * 128],
                                          self.z_rm[:, it * D + h * 512: it * D + (h + 1) * 512],
                                          start=(il == 0), stop=(il == ITS - 1))
                        self.PE.inc(i)
            self.on("tensor", peg)
            pe_g = self.PE.n

            def dvgw(v, s_i=s_i, pe_g=pe_g, gw2ps=gw2ps, gw1ps=gw1ps):
                v.wait_ge(self.PE.h, pe_g)
                if s_i == 0:
                    i = v.tensor_copy(self.gW2a[:, :], gw2ps[:, :]); self.DV.inc(i)
                    i = v.tensor_copy(self.gW1a[:, :], gw1ps[:, :]); self.DV.inc(i)
                else:
                    i = v.tensor_tensor(self.gW2a[:, :], self.gW2a[:, :], gw2ps[:, :], ALU.add)
                    self.DV.inc(i)
                    i = v.tensor_tensor(self.gW1a[:, :], self.gW1a[:, :], gw1ps[:, :], ALU.add)
                    self.DV.inc(i)
            self.on("vector", dvgw)
            gw_dv = self.DV.n

        with nc.psum_tensor("b2ps_%d_%d" % (t, s_i), [1, D], F32) as b2ps, \
             nc.psum_tensor("b1ps_%d_%d" % (t, s_i), [1, M1], F32) as b1ps:

            def peb(tt, gw_dv=gw_dv, b2ps=b2ps, b1ps=b1ps):
                tt.wait_ge(self.DV.h, gw_dv)
                for h in range(2):
                    for il in range(ITS):
                        i = tt.matmul(b2ps[:, h * 512:(h + 1) * 512], self.ones_c[:, :],
                                      self.erm[:, il * D + h * 512: il * D + (h + 1) * 512],
                                      start=(il == 0), stop=(il == ITS - 1))
                    self.PE.inc(i)
                for il in range(ITS):
                    i = tt.matmul(b1ps[:, :], self.ones_c[:, :],
                                  self.dab[:, il * 256:(il + 1) * 256],
                                  start=(il == 0), stop=(il == ITS - 1))
                self.PE.inc(i)
            self.on("tensor", peb)
            pe_bb = self.PE.n

            def dvb(v, s_i=s_i, pe_bb=pe_bb, b2ps=b2ps, b1ps=b1ps):
                v.wait_ge(self.PE.h, pe_bb)
                if s_i == 0:
                    i = v.tensor_copy(self.b2ar[:, :], b2ps[:, :]); self.DV.inc(i)
                    i = v.tensor_copy(self.b1ar[:, :], b1ps[:, :]); self.DV.inc(i)
                else:
                    i = v.tensor_tensor(self.b2ar[:, :], self.b2ar[:, :], b2ps[:, :], ALU.add)
                    self.DV.inc(i)
                    i = v.tensor_tensor(self.b1ar[:, :], self.b1ar[:, :], b1ps[:, :], ALU.add)
                    self.DV.inc(i)
            self.on("vector", dvb)

    def ar_start(self, t):
        dv_g = self.DV.n
        g2d = self.g_in[0:128 * 4096].rearrange("(p c) -> p c", c=2048)

        def sya(s, dv_g=dv_g):
            s.wait_ge(self.DV.h, dv_g)
            self.dma_s(s, self.sg, out=g2d[0:128, :], in_=self.gW1a[:, :])
            self.dma_s(s, self.sg, out=g2d[128:256, :], in_=self.gW2a[:, :])
            self.dma_s(s, self.sg, out=self.g_in[128 * 4096:128 * 4096 + 1024][None, :],
                       in_=self.b2ar[:, :])
            self.dma_s(s, self.sg, out=self.g_in[128 * 4096 + 1024:128 * 4096 + 1280][None, :],
                       in_=self.b1ar[:, :])
        self.on("sync", sya)
        st_g = self.stream_state(self.sg)

        def gpc(g, st_g=st_g):
            for tk_ in st_g:
                g.wait_ge(tk_[0].h, tk_[1])
            i = g.collective_compute("AllReduce", ALU.add,
                                     replica_groups=[list(range(NCORES))],
                                     ins=[self.g_in], outs=[self.g_out])
            self.CC.inc(i)
        self.on("gpsimd", gpc)

    def ar_finish(self, t):
        g2o = self.g_out[0:128 * 4096].rearrange("(p c) -> p c", c=2048)
        cc_n = self.CC.n
        rtoks = []

        def syb(g, cc_n=cc_n):
            g.wait_ge(self.CC.h, cc_n)
            for tk_ in self.stream_state(self.sg):
                g.wait_ge(tk_[0].h, tk_[1])
            rtoks.append(self.dma_s(g, self.sgr, out=self.gW1a[:, :], in_=g2o[0:128, :]))
            rtoks.append(self.dma_s(g, self.sgr, out=self.gW2a[:, :], in_=g2o[128:256, :]))
            rtoks.append(self.dma_s(g, self.sgr, out=self.b2ar[:, :],
                         in_=self.g_out[128 * 4096:128 * 4096 + 1024][None, :]))
            rtoks.append(self.dma_s(g, self.sgr, out=self.b1ar[:, :],
                         in_=self.g_out[128 * 4096 + 1024:128 * 4096 + 1280][None, :]))
        self.on("gpsimd", syb)
        c = self.upd_c

        st_ln = self.stream_state(self.tlnx)

        def dvu1(v, c=c, st_ln=st_ln, rtoks=rtoks):
            for tk_ in rtoks:
                v.wait_ge(tk_[0].h, tk_[1])
            for tk_ in st_ln:
                v.wait_ge(tk_[0].h, tk_[1])
            i = v.scalar_tensor_tensor(self.W1n_f[:, :], self.gW1a[:, :], -c,
                                       self.W1n_f[:, :], ALU.mult, ALU.add)
            self.DV.inc(i)
            i = v.scalar_tensor_tensor(self.b1r_f[:, :], self.b1ar[:, :], -c,
                                       self.b1r_f[:, :], ALU.mult, ALU.add)
            self.DV.inc(i)
            i = v.scalar_tensor_tensor(self.b2r_f[:, :], self.b2ar[:, :], -c,
                                       self.b2r_f[:, :], ALU.mult, ALU.add)
            self.DV.inc(i)
            v.wait_ge(self.DV.h, self.DV.n)
            i = v.tensor_copy(self.W1n_b[:, :], self.W1n_f[:, :]); self.DV.inc(i)
            i = v.tensor_copy(self.b1r_b[:, :], self.b1r_f[:, :]); self.DV.inc(i)
            i = v.tensor_copy(self.b2r_b[:, :], self.b2r_f[:, :]); self.DV.inc(i)
        self.on("vector", dvu1)
        dv_u1 = self.DV.n

        def scu1(s, dv_u1=dv_u1):
            s.wait_ge(self.DV.h, dv_u1)
            self.emit_derive_sc(s, part="w1")
        self.on("scalar", scu1)

        def dvu2(v, c=c):
            i = v.scalar_tensor_tensor(self.W2n_f[:, :], self.gW2a[:, :], -c,
                                       self.W2n_f[:, :], ALU.mult, ALU.add)
            self.DV.inc(i)
            v.wait_ge(self.DV.h, self.DV.n)
            i = v.tensor_copy(self.W2n_b[:, :], self.W2n_f[:, :]); self.DV.inc(i)
        self.on("vector", dvu2)
        dv_u2 = self.DV.n

        def scu2(s, dv_u2=dv_u2):
            s.wait_ge(self.DV.h, dv_u2)
            self.emit_derive_sc(s, part="w2")
        self.on("scalar", scu2)
        self.lngate = [self._w1toks[0], self._w1toks[1]]

    def recon_subtile(self, t, s_i):
        nc = self.nc
        R, ITS = self.R, self.ITS
        base_it = s_i * ITS
        st_der = list(getattr(self, "_w1toks", []))
        st_der2 = list(getattr(self, "_w2toks", []))
        dv_pre = self.DV.n
        st_out_pre = self.stream_state(self.sout)

        qtoks = []

        def scq(s, base_it=base_it):
            for il in range(ITS):
                it = base_it + il
                qtoks.append(self.dma_s(s, self.tsub,
                    out=self.zfm.ap().rearrange("p (c m) -> p c m", m=128)[:, il * 8:(il + 1) * 8, :],
                    in_=self.q_rm[:, it * D:(it + 1) * D], transpose=True))
        self.on("scalar", scq)

        h2toks = []
        with nc.psum_tensor("a2ps0_%d_%d" % (t, s_i), [128, 512], F32) as a2ps0, \
             nc.psum_tensor("a2ps1_%d_%d" % (t, s_i), [128, 512], F32) as a2ps1:
            a2ps = [a2ps0, a2ps1]
            h2ac = {}
            for il in range(ITS):
                prev = h2ac.get(il - 2, 0)

                def pea2(tt, il=il, st_der=st_der, prev=prev, dv_pre=dv_pre,
                         a2ps=a2ps):
                    for tk_ in [qtoks[il]] + st_der:
                        tt.wait_ge(tk_[0].h, tk_[1])
                    if prev:
                        tt.wait_ge(self.AC.h, prev)
                    else:
                        tt.wait_ge(self.DV.h, dv_pre)
                    for hc in range(8):
                        tt.matmul(a2ps[il % 2][:, 0:256],
                                  self.zfm[:, (il * 8 + hc) * 128:(il * 8 + hc + 1) * 128],
                                  self.W1T_b[:, hc * 256:(hc + 1) * 256],
                                  start=(hc == 0), stop=False)
                    i = tt.matmul(a2ps[il % 2][:, 0:256], self.ones_r[:, :],
                                  self.b1r_b[:, :], start=False, stop=True)
                    self.PE.inc(i)
                self.on("tensor", pea2)
                pe_a2 = self.PE.n

                def ach2(s, il=il, pe_a2=pe_a2, a2ps=a2ps):
                    s.wait_ge(self.PE.h, pe_a2)
                    i = s.activation(self.hrm[:, il * 256:(il + 1) * 256],
                                     a2ps[il % 2][:, 0:256], AF.Gelu)
                    self.AC.inc(i)
                self.on("scalar", ach2)
                h2ac[il] = self.AC.n
                ac_h2il = self.AC.n

                def sch2il(s, il=il, ac_h2il=ac_h2il):
                    s.wait_ge(self.AC.h, ac_h2il)
                    h2toks.append(self.dma_s(s, self.tsub,
                        out=self.hfm.ap().rearrange("p (c m) -> p c m", m=128)[:, il * 2:(il + 1) * 2, :],
                        in_=self.hrm[:, il * 256:(il + 1) * 256], transpose=True))
                self.on("scalar", sch2il)

        ac_h2all = self.AC.n
        with nc.psum_tensor("ops0_%d_%d" % (t, s_i), [128, D], F32) as ops0, \
             nc.psum_tensor("ops1_%d_%d" % (t, s_i), [128, D], F32) as ops1, \
             nc.psum_tensor("fps0_%d_%d" % (t, s_i), [128, D], F32) as fps0, \
             nc.psum_tensor("fps1_%d_%d" % (t, s_i), [128, D], F32) as fps1:
            ops = [ops0, ops1]
            fps = [fps0, fps1]
            ln2dv = {}
            l2toks = []
            for il in range(ITS):
                it = base_it + il
                gate = self.xgate[self.xuse % 2]
                xslot = self.xuse % 2
                self.xuse += 1

                xrt = []

                def syxr(s, t=t, it=it, gate=gate, xslot=xslot):
                    if gate:
                        s.wait_ge(self.DV.h, gate)
                    xrt.append(self.dma_s(s, self.sx, out=self.x01[:, xslot * D:(xslot + 1) * D],
                               in_=self.x[t * R + it * 128: t * R + (it + 1) * 128, :]))
                self.on("sync", syxr)
                xr_tok = xrt[0]
                prev = ln2dv.get(il - 2, 0)

                def peo(tt, il=il, prev=prev, ops=ops, st_der2=st_der2,
                        ac_h2all=ac_h2all):
                    for tk_ in [h2toks[il]] + st_der2:
                        tt.wait_ge(tk_[0].h, tk_[1])
                    if prev:
                        tt.wait_ge(self.DV.h, prev)
                    else:
                        tt.wait_ge(self.AC.h, ac_h2all)
                    for h in range(2):
                        for m2c in range(2):
                            tt.matmul(ops[il % 2][:, h * 512:(h + 1) * 512],
                                      self.hfm[:, (il * 2 + m2c) * 128:(il * 2 + m2c + 1) * 128],
                                      self.W2T_b[:, m2c * D + h * 512:m2c * D + (h + 1) * 512],
                                      start=(m2c == 0), stop=False)
                        i = tt.matmul(ops[il % 2][:, h * 512:(h + 1) * 512],
                                      self.ones_r[:, :], self.b2r_b[:, h * 512:(h + 1) * 512],
                                      start=False, stop=True)
                        self.PE.inc(i)
                self.on("tensor", peo)
                pe_o = self.PE.n

                def dvx2(v, il=il, it=it, pe_o=pe_o, xr_tok=xr_tok, xslot=xslot,
                         st_out_pre=st_out_pre, ops=ops):
                    v.wait_ge(self.PE.h, pe_o)
                    for tk_ in st_out_pre:
                        v.wait_ge(tk_[0].h, tk_[1])
                    i = v.tensor_tensor(self.x2s[:, il * D:(il + 1) * D], ops[il % 2][:, :],
                                        self.q_rm[:, it * D:(it + 1) * D], ALU.add)
                    self.DV.inc(i)
                    self.wtok(v, xr_tok)
                    v.wait_ge(self.DV.h, self.DV.n)
                    i = v.tensor_tensor(self.x2s[:, il * D:(il + 1) * D],
                                        self.x2s[:, il * D:(il + 1) * D],
                                        self.x01[:, xslot * D:(xslot + 1) * D], ALU.add)
                    self.DV.inc(i)
                    sl = self.x2s[:, il * D:(il + 1) * D]
                    v.wait_ge(self.DV.h, self.DV.n)
                    v.bn_stats(self.st6[:, 0:6], sl[:, 0:512])
                    i = v.bn_stats(self.st6[:, 6:12], sl[:, 512:1024]); self.DV.inc(i)
                    v.wait_ge(self.DV.h, self.DV.n)
                    i = v.bn_aggr(self.mv[:, :], self.st6[:, :]); self.DV.inc(i)
                self.on("vector", dvx2)
                self.xgate[xslot] = self.DV.n
                dv_st = self.DV.n

                def acl2(s, dv_st=dv_st):
                    s.wait_ge(self.DV.h, dv_st)
                    i = s.activation(self.rstd[:, :], self.mv[:, 1:2], AF.Sqrt,
                                     bias=self.epsc[:, 0:1], scale=1.0)
                    self.AC.inc(i)
                self.on("scalar", acl2)
                ac_l2 = self.AC.n

                def dvl2(v, il=il, ac_l2=ac_l2):
                    v.wait_ge(self.AC.h, ac_l2)
                    i = v.reciprocal(self.rstd[:, :], self.rstd[:, :]); self.DV.inc(i)
                    v.wait_ge(self.DV.h, self.DV.n)
                    i = v.tensor_scalar(self.ln2r[:, il * D:(il + 1) * D],
                                        self.x2s[:, il * D:(il + 1) * D],
                                        self.mv[:, 0:1], self.rstd[:, 0:1],
                                        ALU.subtract, ALU.mult)
                    self.DV.inc(i)
                self.on("vector", dvl2)
                ln2dv[il] = self.DV.n
                dv_l2il = self.DV.n

                def scl2il(s, il=il, dv_l2il=dv_l2il):
                    s.wait_ge(self.DV.h, dv_l2il)
                    l2toks.append(self.dma_s(s, self.tsub,
                        out=self.ln2f.ap().rearrange("p (c m) -> p c m", m=128)[:, il * 8:(il + 1) * 8, :],
                        in_=self.ln2r[:, il * D:(il + 1) * D], transpose=True))
                self.on("scalar", scl2il)

            outdv = {}
            for il in range(ITS):
                it = base_it + il
                prev = outdv.get(il - 2, 0)

                def pef(tt, il=il, prev=prev, mt_tok=self.mt_tok, fps=fps,
                        ac_h2all=ac_h2all):
                    self.wtok(tt, l2toks[il])
                    self.wtok(tt, mt_tok)
                    if prev:
                        tt.wait_ge(self.DV.h, prev)
                    else:
                        tt.wait_ge(self.AC.h, ac_h2all)
                    for h in range(2):
                        for kc in range(8):
                            tt.matmul(fps[il % 2][:, h * 512:(h + 1) * 512],
                                      self.ln2f[:, (il * 8 + kc) * 128:(il * 8 + kc + 1) * 128],
                                      self.slotB[:, kc * D + h * 512: kc * D + (h + 1) * 512],
                                      start=(kc == 0), stop=False)
                        i = tt.matmul(fps[il % 2][:, h * 512:(h + 1) * 512],
                                      self.ones_r[:, :], self.fbias_b[:, h * 512:(h + 1) * 512],
                                      start=False, stop=True)
                        self.PE.inc(i)
                self.on("tensor", pef)
                pe_f = self.PE.n

                def dvo(v, il=il, pe_f=pe_f, fps=fps):
                    v.wait_ge(self.PE.h, pe_f)
                    i = v.tensor_tensor(self.x2s[:, il * D:(il + 1) * D],
                                        fps[il % 2][:, :],
                                        self.x2s[:, il * D:(il + 1) * D], ALU.add)
                    self.DV.inc(i)
                self.on("vector", dvo)
                dv_o = self.DV.n
                outdv[il] = dv_o

                def syo(s, t=t, it=it, il=il, dv_o=dv_o):
                    s.wait_ge(self.DV.h, dv_o)
                    self.dma_s(s, self.sout, out=self.y[t * R + it * 128: t * R + (it + 1) * 128, :],
                               in_=self.x2s[:, il * D:(il + 1) * D])
                self.on("sync", syo)


def build_program(R):
    return Builder(R).build()


def kernel(L0_attn_in_seq, L0_attn_t_k, L0_attn_t_v, L0_attn_t_q, L0_attn_W1,
           L0_attn_b1, L0_attn_W2, L0_attn_b2, L0_ffn_G1_R, L0_ffn_G2_R,
           L0_ffn_G1_I, L0_ffn_G2_I, L0_ffn_bias_real, L0_ffn_bias_imag):
    x = np.asarray(L0_attn_in_seq)
    Tn, N, Dn = x.shape
    R = N // NCORES
    nc = build_program(R)
    shared = {
        "t_k": np.ascontiguousarray(L0_attn_t_k),
        "t_v": np.ascontiguousarray(L0_attn_t_v),
        "t_q": np.ascontiguousarray(L0_attn_t_q),
        "W1": np.ascontiguousarray(L0_attn_W1),
        "b1": np.ascontiguousarray(L0_attn_b1),
        "W2": np.ascontiguousarray(L0_attn_W2),
        "b2": np.ascontiguousarray(L0_attn_b2),
        "G1R": np.ascontiguousarray(L0_ffn_G1_R),
        "G2R": np.ascontiguousarray(L0_ffn_G2_R),
        "G1I": np.ascontiguousarray(L0_ffn_G1_I),
        "G2I": np.ascontiguousarray(L0_ffn_G2_I),
        "br": np.ascontiguousarray(L0_ffn_bias_real),
        "bi": np.ascontiguousarray(L0_ffn_bias_imag),
    }
    in_maps = []
    for c in range(NCORES):
        m = dict(shared)
        m["x"] = np.ascontiguousarray(x[:, c * R:(c + 1) * R, :]).reshape(Tn * R, Dn)
        in_maps.append(m)
    res = run_bass_kernel_spmd(nc, in_maps, core_ids=list(range(NCORES)))
    out = np.empty((Tn, N, Dn), dtype=np.float32)
    for c in range(NCORES):
        out[:, c * R:(c + 1) * R, :] = res.results[c]["y"].reshape(Tn, R, Dn)
    return out


# version 14
# speedup vs baseline: 1.0116x; 1.0116x over previous
import numpy as np
import concourse.bass as bass
import concourse.mybir as mybir
from concourse.bass_utils import run_bass_kernel_spmd

F32 = mybir.dt.float32
BF16 = mybir.dt.bfloat16
U8 = mybir.dt.uint8
AF = mybir.ActivationFunctionType
ALU = mybir.AluOpType

D = 1024
M1 = 256
T = 4
LR = 1e-3
NCORES = 8
GELU_C = float(1.0 / np.sqrt(2.0 * np.pi))


class Sem:
    def __init__(self, nc, name):
        self.h = nc.alloc_semaphore(name=name)
        self.n = 0

    def inc(self, instr, amt=1):
        instr.then_inc(self.h, amt)
        self.n += amt
        return self.n


class Stream:
    def __init__(self, nc, name):
        self.sems = [Sem(nc, name + "_0"), Sem(nc, name + "_1")]
        self.i = 0


class Builder:
    def __init__(self, R):
        self.R = R
        self.IT = R // 128
        self.NS = 4 if self.IT >= 8 else (2 if self.IT >= 2 else 1)
        self.ITS = self.IT // self.NS
        self.upd_c = float(LR * 2.0 / ((NCORES * R) * D))
        nc = bass.Bass("TRN2", target_bir_lowering=False)
        self.nc = nc

        self.x = nc.dram_tensor("x", [T * R, D], F32, kind="ExternalInput").ap()
        self.y = nc.dram_tensor("y", [T * R, D], F32, kind="ExternalOutput").ap()
        self.tk = nc.dram_tensor("t_k", [D, D], F32, kind="ExternalInput").ap()
        self.tv = nc.dram_tensor("t_v", [D, D], F32, kind="ExternalInput").ap()
        self.tq = nc.dram_tensor("t_q", [D, D], F32, kind="ExternalInput").ap()
        self.W1 = nc.dram_tensor("W1", [M1, D], F32, kind="ExternalInput").ap()
        self.b1 = nc.dram_tensor("b1", [M1], F32, kind="ExternalInput").ap()
        self.W2 = nc.dram_tensor("W2", [D, M1], F32, kind="ExternalInput").ap()
        self.b2 = nc.dram_tensor("b2", [D], F32, kind="ExternalInput").ap()
        self.G1R = nc.dram_tensor("G1R", [1, 32, 4, 32], F32, kind="ExternalInput").ap()
        self.G2R = nc.dram_tensor("G2R", [4, 16, 1, 16], F32, kind="ExternalInput").ap()
        self.G1I = nc.dram_tensor("G1I", [1, 32, 4, 32], F32, kind="ExternalInput").ap()
        self.G2I = nc.dram_tensor("G2I", [4, 16, 1, 16], F32, kind="ExternalInput").ap()
        self.br = nc.dram_tensor("br", [512], F32, kind="ExternalInput").ap()
        self.bi = nc.dram_tensor("bi", [512], F32, kind="ExternalInput").ap()
        self.tk_h = nc.dram_tensor("tk_h", [128, 8 * D], BF16).ap()
        self.tv_h = nc.dram_tensor("tv_h", [128, 8 * D], BF16).ap()
        self.tq_h = nc.dram_tensor("tq_h", [128, 8 * D], BF16).ap()
        self.mt_h = nc.dram_tensor("mt_h", [128, 8 * D], BF16).ap()
        self.wr_h = nc.dram_tensor("wr_h", [512, 512], BF16).ap()
        self.wi_h = nc.dram_tensor("wi_h", [512, 512], BF16).ap()
        self.g_in = nc.dram_tensor("g_in", [128 * 4096 + 1024 + 256], F32).ap()
        self.g_out = nc.dram_tensor("g_out", [128 * 4096 + 1024 + 256], F32).ap()

        self.PE = Sem(nc, "sPE")
        self.DV = Sem(nc, "sDV")
        self.AC = Sem(nc, "sAC")
        self.SY = Sem(nc, "sSY")
        self.SC = Sem(nc, "sSC")
        self.GP = Sem(nc, "sGP")
        self.CC = Sem(nc, "sCC")
        self.sx = Stream(nc, "sx")
        self.sout = Stream(nc, "sout")
        self.sslot = Stream(nc, "sslot")
        self.sg = Stream(nc, "sg")
        self.tlnx = Stream(nc, "tlnx")
        self.tsub = Stream(nc, "tsub")
        self.tder = Stream(nc, "tder")
        self.sgr = Stream(nc, "sgr")
        self.xgate = [0, 0]
        self.lngate = [None, None]
        self.xuse = 0
        self.pe_qdone = 0
        self.pe_ffndone = 0

    def on(self, eng, fn):
        getattr(self.block, eng)(fn)

    def dma_s(self, eng, stream, out, in_, **kw):
        sem = stream.sems[stream.i % 2]
        stream.i += 1
        eng.wait_ge(sem.h, sem.n)
        i = eng.dma_start(out=out, in_=in_, **kw)
        sem.inc(i, 16)
        return (sem, sem.n)

    def wtok(self, eng, tok):
        if tok is not None:
            eng.wait_ge(tok[0].h, tok[1])

    def wstream(self, eng, stream):
        for sem in stream.sems:
            if sem.n:
                eng.wait_ge(sem.h, sem.n)

    def stream_state(self, stream):
        return [(sem, sem.n) for sem in stream.sems if sem.n]

    def dma(self, eng, sem, out, in_, **kw):
        eng.wait_ge(sem.h, sem.n)
        i = eng.dma_start(out=out, in_=in_, **kw)
        sem.inc(i, 16)
        return i

    def barrier(self):
        sems = [self.PE, self.DV, self.AC, self.SY, self.SC, self.GP, self.CC]
        for stx in (self.sx, self.sout, self.sslot, self.sg, self.sgr,
                    self.tlnx, self.tsub, self.tder):
            sems += stx.sems
        vals = [(s.h, s.n) for s in sems if s.n > 0]

        def mk(e):
            for h, n in vals:
                e.wait_ge(h, n)
        for eng in ("tensor", "vector", "scalar", "sync", "gpsimd"):
            self.on(eng, mk)

    def build(self):
        nc = self.nc
        import contextlib
        stack = contextlib.ExitStack()
        sb = lambda name, shape, dt: stack.enter_context(nc.sbuf_tensor(name, shape, dt))
        IT, ITS = self.IT, self.ITS
        self.W1n_f = sb("W1n_f", [128, 2 * D], F32)
        self.W2n_f = sb("W2n_f", [128, 8 * M1], F32)
        self.W2n_b = sb("W2n_b", [128, 8 * M1], BF16)
        self.W1T_b = sb("W1T_b", [128, 8 * M1], BF16)
        self.W2T_b = sb("W2T_b", [128, 2 * D], BF16)
        self.b1r_f = sb("b1r_f", [1, M1], F32)
        self.b2r_f = sb("b2r_f", [1, D], F32)
        self.b1r_b = sb("b1r_b", [1, M1], BF16)
        self.b2r_b = sb("b2r_b", [1, D], BF16)
        self.fbias_b = sb("fbias_b", [1, D], BF16)
        self.ones_r = sb("ones_r", [1, 128], BF16)
        self.ones_c = sb("ones_c", [128, 1], BF16)
        self.epsc = sb("epsc", [128, 1], F32)
        self.epsT = sb("epsT", [128, M1], F32)
        self.halfT = sb("halfT", [128, M1], F32)
        self.slotA = sb("slotA", [128, 8 * D], BF16)
        self.slotB = sb("slotB", [128, 8 * D], BF16)
        self.lnx_fm = sb("lnx_fm", [128, IT * D], BF16)
        self.z_rm = sb("z_rm", [128, IT * D], BF16)
        self.y_rm = sb("y_rm", [128, IT * D], BF16)
        self.q_rm = sb("q_rm", [128, IT * D], BF16)
        self.gW1a = sb("gW1a", [128, 2 * D], F32)
        self.gW2a = sb("gW2a", [128, 8 * M1], F32)
        self.b1ar = sb("b1ar", [1, M1], F32)
        self.b2ar = sb("b2ar", [1, D], F32)
        self.st6 = sb("st6", [128, 12], F32)
        self.mv = sb("mv", [128, 2], F32)
        self.rstd = sb("rstd", [128, 1], F32)
        self.x01 = sb("x01", [128, 2 * D], F32)
        self.lnxr = sb("lnxr", [128, 2 * D], BF16)
        self.W1n_b = self.lnxr.ap()
        self.zfm = sb("zfm", [128, ITS * D], BF16)
        self.erm = sb("erm", [128, ITS * D], BF16)
        self.efm = sb("efm", [128, ITS * D], BF16)
        self.hrm = sb("hrm", [128, ITS * M1], BF16)
        self.hfm = sb("hfm", [128, ITS * M1], BF16)
        self.gpb = sb("gpb", [128, ITS * M1], BF16)
        self.dab = sb("dab", [128, ITS * M1], BF16)
        self.xa01 = sb("xa01", [128, 2 * M1], BF16)
        self.sq01 = sb("sq01", [128, 2 * M1], BF16)
        self.ex01 = sb("ex01", [128, 2 * M1], BF16)
        self.ms01 = sb("ms01", [128, M1], U8)
        self.xs01 = sb("xs01", [128, M1], F32)
        self.ph01 = sb("ph01", [128, M1], F32)
        self.x2s = sb("x2s", [128, ITS * D], F32)
        self.ln2r = sb("ln2r", [128, ITS * D], BF16)
        self.ln2f = sb("ln2f", [128, ITS * D], BF16)
        self.p1sb = self.erm[:, 0:2048]
        self.g1sb = self.efm[0:4, 0:1024]
        self.g2sb = self.efm[0:4, 1024:1280]

        import os
        nphase = int(os.environ.get("ATH_PHASES", "99"))
        with nc.Block() as block:
            self.block = block
            self.init_phase()
            if nphase >= 1:
                self._init_gates = list(getattr(self, "_stage_gates", []))
                nT = min(T, nphase)
                self.emit_A(0)
                self._mt_gate_once = getattr(self, "_mt_store_gate", None)
                self.emit_q(0)
                p1g = (self.stream_state(self.sg) + self.stream_state(self.sout)
                       + [(self.GP, self.GP.n)])

                def pre_chain_gate(e, p1g=p1g):
                    for sem, v in p1g:
                        e.wait_ge(sem.h, v)
                self.on("vector", pre_chain_gate)
                self.on("scalar", pre_chain_gate)
                for t in range(nT):
                    for s_i in range(self.NS):
                        self.chain_subtile(t, s_i)
                    self.ar_start(t)
                    if t + 1 < nT:
                        self.emit_A(t + 1)
                    self.ar_finish(t)
                    self.emit_D(t)
                    if t + 1 < nT:
                        self.emit_q(t + 1)
            flush = [(self.SY, self.SY.n)]
            for stx in (self.sx, self.sout, self.sslot, self.sg, self.sgr,
                        self.tlnx, self.tsub, self.tder):
                flush += self.stream_state(stx)

            def fin(s, flush=flush):
                for sem, v in flush:
                    if v:
                        s.wait_ge(sem.h, v)
            self.on("sync", fin)
        stack.close()
        return nc

    def init_phase(self):
        import os
        lvl = int(os.environ.get("ATH_INIT", "99"))
        nc = self.nc

        def gp(g):
            self.dma(g, self.GP, out=self.W1n_f.ap().rearrange("p (c f) -> p c f", c=2),
                     in_=self.W1.rearrange("(c p) f -> p c f", p=128))
            self.dma(g, self.GP, out=self.W2n_f.ap().rearrange("p (c f) -> p c f", c=8),
                        in_=self.W2.rearrange("(c p) f -> p c f", p=128))
            self.dma(g, self.GP, out=self.b1r_f[:, :], in_=self.b1[None, :])
            self.dma(g, self.GP, out=self.b2r_f[:, :], in_=self.b2[None, :])
            self.dma(g, self.GP, out=self.fbias_b[:, 0:512], in_=self.br[None, :])
            self.dma(g, self.GP, out=self.fbias_b[:, 512:1024], in_=self.bi[None, :])
        self.on("gpsimd", gp)

        def dv(v):
            v.memset(self.ones_r[:, :], 1.0)
            v.memset(self.ones_c[:, :], 1.0)
            v.memset(self.epsc[:, :], 1e-5)
            v.memset(self.epsT[:, :], 1e-12)
            i = v.memset(self.halfT[:, :], 0.5)
            self.DV.inc(i)
        self.on("vector", dv)

        if lvl < 2:
            return
        for wi, (src, dst) in enumerate([(self.tk, self.tk_h), (self.tv, self.tv_h),
                                         (self.tq, self.tq_h)]):
            st_before = self.stream_state(self.tsub)

            def gpl(g, src=src, wi=wi, st_before=st_before):
                if wi > 0:
                    for tk_ in st_before:
                        g.wait_ge(tk_[0].h, tk_[1])
                self.dma(g, self.GP, out=self.slotA.ap().rearrange("p (c f) -> p c f", c=8),
                            in_=src.rearrange("(c p) f -> p c f", p=128))
            self.on("gpsimd", gpl)
            gp_now = self.GP.n

            def scl(s, gp_now=gp_now):
                s.wait_ge(self.GP.h, gp_now)
                for hc in range(8):
                    self.dma_s(s, self.tsub,
                        out=self.slotB.ap().rearrange("p (dc m) -> p dc m", dc=8)[:, :, hc * 128:(hc + 1) * 128],
                        in_=self.slotA[:, hc * D:(hc + 1) * D], transpose=True)
            self.on("scalar", scl)
            st_t = self.stream_state(self.tsub)

            def syl(s, dst=dst, st_t=st_t):
                for tk_ in st_t:
                    s.wait_ge(tk_[0].h, tk_[1])
                self.dma(s, self.SY, out=dst, in_=self.slotB[:, :])
            self.on("sync", syl)
            sy_now = self.SY.n

            def scw(s, sy_now=sy_now):
                s.wait_ge(self.SY.h, sy_now)
            self.on("scalar", scw)

        self._stage_gates = ([(self.SY, self.SY.n)]
                             + self.stream_state(self.tsub))
        if lvl < 3:
            return
        for part, wdst in enumerate([self.wr_h, self.wi_h]):
            if part == 0:
                def gpl0(g):
                    self.dma(g, self.GP, out=self.g1sb.rearrange("p (b d) -> p b d", b=32),
                             in_=self.G1R.rearrange("a b c d -> (a c) b d"))
                    self.dma(g, self.GP, out=self.g2sb.rearrange("p (e g) -> p e g", e=16),
                             in_=self.G2R.rearrange("c e f g -> (c f) e g"))
                self.on("gpsimd", gpl0)
            if part == 1:
                sy_prev = self.SY.n

                st_p = self.stream_state(self.sg) + self.stream_state(self.sout)

                def gpl2(g, st_p=st_p):
                    for tk_ in st_p:
                        g.wait_ge(tk_[0].h, tk_[1])
                    self.dma(g, self.GP, out=self.g1sb.rearrange("p (b d) -> p b d", b=32),
                                in_=self.G1I.rearrange("a b c d -> (a c) b d"))
                    self.dma(g, self.GP, out=self.g2sb.rearrange("p (e g) -> p e g", e=16),
                                in_=self.G2I.rearrange("c e f g -> (c f) e g"))
                self.on("gpsimd", gpl2)
            gp_now = self.GP.n
            with nc.psum_tensor("ps_init%d" % part, [128, 1024], F32) as ps_init:
                ac_base = self.AC.n
                pe_base = self.PE.n

                def pe(t, gp_now=gp_now, ac_base=ac_base, ps_init=ps_init):
                    t.wait_ge(self.GP.h, gp_now)
                    for m in range(8):
                        if m >= 2:
                            t.wait_ge(self.AC.h, ac_base + m - 1)
                        i = t.matmul(ps_init[:, (m % 2) * 512:(m % 2) * 512 + 256],
                                     self.g1sb[:, m * 128:(m + 1) * 128], self.g2sb[:, :],
                                     start=True, stop=True)
                        self.PE.inc(i)
                self.on("tensor", pe)

                def ac(s, pe_base=pe_base, ps_init=ps_init):
                    for m in range(8):
                        s.wait_ge(self.PE.h, pe_base + m + 1)
                        i = s.copy(self.p1sb[:, m * 256:(m + 1) * 256],
                                   ps_init[:, (m % 2) * 512:(m % 2) * 512 + 256])
                        self.AC.inc(i)
                self.on("scalar", ac)
            ac_now = self.AC.n

            def syp(s, wdst=wdst, ac_now=ac_now):
                import os
                s.wait_ge(self.AC.h, ac_now)
                if os.environ.get("ATH_NOSYP"):
                    return
                wt4 = wdst.rearrange("(b e) (d g) -> b d e g", e=16, g=16)
                k = 0
                for m in range(8):
                    for bl in range(4):
                        b = m * 4 + bl
                        self.dma_s(s, (self.sg, self.sout)[k % 2],
                            out=wt4[b, :, :, :],
                            in_=self.p1sb[bl * 32:(bl + 1) * 32, m * 256:(m + 1) * 256]
                            .rearrange("p (e g) -> p e g", e=16))
                        k += 1
            self.on("sync", syp)

        if lvl < 4:
            return
        st_p2 = self.stream_state(self.sg) + self.stream_state(self.sout)

        def sym(s, st_p2=st_p2):
            for tk_ in st_p2:
                s.wait_ge(tk_[0].h, tk_[1])
            mtb = self.q_rm if self.IT >= 8 else self.slotB
            v = mtb.ap().rearrange("p (kc c) -> p kc c", kc=8)
            for (w, kgrp, half) in ((self.wr_h, 0, 0), (self.wi_h, 1, 0),
                                    (self.wi_h, 0, 1), (self.wr_h, 1, 1)):
                for r in range(4):
                    self.dma_s(s, self.tsub,
                        out=v[:, kgrp * 4:(kgrp + 1) * 4,
                              half * 512 + r * 128: half * 512 + (r + 1) * 128],
                        in_=w[r * 128:(r + 1) * 128, :], transpose=True)
        self.on("scalar", sym)
        st_mt = self.stream_state(self.tsub)

        def dvn(v, st_mt=st_mt):
            for tk_ in st_mt:
                v.wait_ge(tk_[0].h, tk_[1])
            mtb = self.q_rm if self.IT >= 8 else self.slotB
            for kc in range(4, 8):
                i = v.tensor_scalar_mul(mtb[:, kc * D:kc * D + 512],
                                        mtb[:, kc * D:kc * D + 512], -1.0)
                self.DV.inc(i)
        self.on("vector", dvn)
        dv_now = self.DV.n

        def sym2(s, dv_now=dv_now):
            s.wait_ge(self.DV.h, dv_now)
            self.dma(s, self.SY, out=self.mt_h,
                     in_=(self.q_rm if self.IT >= 8 else self.slotB)[:, :])
        self.on("sync", sym2)
        self._mt_store_gate = (self.SY, self.SY.n)
        if self.IT < 8:
            self._stage_gates = ([(self.SY, self.SY.n)]
                                 + self.stream_state(self.tsub)
                                 + self.stream_state(self.sg)
                                 + self.stream_state(self.sout))

        if lvl < 5:
            return
        def dvd(v):
            v.wait_ge(self.GP.h, 16 * 4)
            self.emit_derive_dv(v)
        self.on("vector", dvd)
        dv_now = self.DV.n

        def scd(s, dv_now=dv_now):
            s.wait_ge(self.DV.h, dv_now)
            self.emit_derive_sc(s)
        self.on("scalar", scd)

    def emit_derive_dv(self, v):
        i = v.tensor_copy(self.W1n_b[:, :], self.W1n_f[:, :]); self.DV.inc(i)
        i = v.tensor_copy(self.W2n_b[:, :], self.W2n_f[:, :]); self.DV.inc(i)
        i = v.tensor_copy(self.b1r_b[:, :], self.b1r_f[:, :]); self.DV.inc(i)
        i = v.tensor_copy(self.b2r_b[:, :], self.b2r_f[:, :]); self.DV.inc(i)

    def emit_derive_sc(self, s, init=False, part="all"):
        dma = (lambda *a, **k: self.dma(s, self.SC, *a, **k)) if init else \
              (lambda *a, **k: self.dma_s(s, self.tder, *a, **k))
        if part in ("all", "w1"):
            w1toks = []
            for mc in range(2):
                w1toks.append(dma(
                    out=self.W1T_b.ap().rearrange("p (hc m) -> p hc m", hc=8)[:, :, mc * 128:(mc + 1) * 128],
                    in_=self.W1n_b[:, mc * D:(mc + 1) * D], transpose=True))
            if not init:
                self._w1toks = w1toks
        if part in ("all", "w2"):
            w2toks = []
            for jc in range(8):
                w2toks.append(dma(
                    out=self.W2T_b.ap().rearrange("p (m2c m) -> p m2c m", m2c=2)[:, :, jc * 128:(jc + 1) * 128],
                    in_=self.W2n_b[:, jc * 256:(jc + 1) * 256], transpose=True))
            if not init:
                self._w2toks = w2toks

    def emit_A(self, t):
        nc = self.nc
        R, IT, NS, ITS = self.R, self.IT, self.NS, self.ITS

        pe_slotA = self.pe_qdone
        pe_slotB = self.pe_ffndone

        toks = []

        gates = list(getattr(self, "_init_gates", []) or [])
        self._init_gates = None

        def syA(s, pe_slotA=pe_slotA, pe_slotB=pe_slotB, gates=gates):
            s.wait_ge(self.PE.h, pe_slotA)
            if pe_slotB:
                s.wait_ge(self.PE.h, pe_slotB)
            for sem, v in gates:
                s.wait_ge(sem.h, v)
            toks.append(self.dma_s(s, self.sslot, out=self.slotA[:, :], in_=self.tk_h))
            toks.append(self.dma_s(s, self.sslot, out=self.slotB[:, :], in_=self.tv_h))
        self.on("sync", syA)
        slot_toks = list(toks)

        with nc.psum_tensor("zps0_%d" % t, [128, D], F32) as zps0, \
             nc.psum_tensor("zps1_%d" % t, [128, D], F32) as zps1, \
             nc.psum_tensor("yps0_%d" % t, [128, D], F32) as yps0, \
             nc.psum_tensor("yps1_%d" % t, [128, D], F32) as yps1:
            zps = [zps0, zps1]
            yps = [yps0, yps1]
            a_ac0 = self.AC.n
            evacs = {}
            for it in range(IT):
                self.emit_ln_x(t, it, self.lnxr, it % 2)

                dv_lnc = self.DV.n
                ttok = []

                def sct(s, it=it, dv_lnc=dv_lnc):
                    s.wait_ge(self.DV.h, dv_lnc)
                    ttok.append(self.dma_s(s, self.tlnx,
                             out=self.lnx_fm.ap().rearrange("p (c m) -> p c m", m=128)[:, it * 8:(it + 1) * 8, :],
                             in_=self.lnxr[:, (it % 2) * D:(it % 2 + 1) * D], transpose=True))
                self.on("scalar", sct)
                self.lngate[it % 2] = ttok[0]
                lnx_tok = ttok[0]

                prev_evac = evacs.get(it - 2)

                def pezy(tt, it=it, lnx_tok=lnx_tok, prev_evac=prev_evac,
                         slot_toks=slot_toks, a_ac0=a_ac0):
                    self.wtok(tt, lnx_tok)
                    for tk_ in slot_toks:
                        self.wtok(tt, tk_)
                    if prev_evac is not None:
                        tt.wait_ge(self.AC.h, prev_evac[0])
                        tt.wait_ge(self.DV.h, prev_evac[1])
                    else:
                        tt.wait_ge(self.AC.h, a_ac0)
                    for dst, w in ((zps[it % 2], self.slotA), (yps[it % 2], self.slotB)):
                        for h in range(2):
                            for dc in range(8):
                                i = tt.matmul(dst[:, h * 512:(h + 1) * 512],
                                              self.lnx_fm[:, (it * 8 + dc) * 128:(it * 8 + dc + 1) * 128],
                                              w[:, dc * D + h * 512: dc * D + (h + 1) * 512],
                                              start=(dc == 0), stop=(dc == 7))
                            self.PE.inc(i)
                self.on("tensor", pezy)
                pe_n = self.PE.n

                def acz(s, it=it, pe_n=pe_n):
                    s.wait_ge(self.PE.h, pe_n - 2)
                    i = s.copy(self.z_rm[:, it * D:(it + 1) * D], zps[it % 2][:, :])
                    self.AC.inc(i)
                self.on("scalar", acz)
                ac_z = self.AC.n

                def dvy(v, it=it, pe_n=pe_n, ac_z=ac_z):
                    v.wait_ge(self.PE.h, pe_n)
                    v.wait_ge(self.AC.h, ac_z)
                    i = v.tensor_tensor(self.y_rm[:, it * D:(it + 1) * D],
                                        self.z_rm[:, it * D:(it + 1) * D],
                                        yps[it % 2][:, :], ALU.subtract)
                    self.DV.inc(i)
                self.on("vector", dvy)
                evacs[it] = (self.AC.n, self.DV.n)

    def emit_q(self, t):
        nc = self.nc
        R, IT = self.R, self.IT
        pe_zy = self.PE.n

        qtk = []

        mtg = getattr(self, "_mt_gate_once", None)
        self._mt_gate_once = None

        def syq(s, pe_zy=pe_zy, mtg=mtg):
            if mtg is not None:
                s.wait_ge(mtg[0].h, mtg[1])
            s.wait_ge(self.PE.h, pe_zy)
            qtk.append(self.dma_s(s, self.sslot, out=self.slotA[:, :], in_=self.tq_h))
        self.on("sync", syq)
        slot_tok = qtk[0]
        lnx_state = self.stream_state(self.tlnx)
        a_ac = self.AC.n
        a_dv = self.DV.n

        with nc.psum_tensor("qps0_%d" % t, [128, D], F32) as qps0, \
             nc.psum_tensor("qps1_%d" % t, [128, D], F32) as qps1:
            qps = [qps0, qps1]
            qevac = {}
            for it in range(IT):
                prev = qevac.get(it - 2)

                def peq(tt, it=it, prev=prev, slot_tok=slot_tok, a_ac=a_ac, a_dv=a_dv,
                        lnx_state=lnx_state):
                    self.wtok(tt, slot_tok)
                    for tk_ in lnx_state:
                        tt.wait_ge(tk_[0].h, tk_[1])
                    if prev is not None:
                        tt.wait_ge(self.AC.h, prev)
                    else:
                        tt.wait_ge(self.AC.h, a_ac)
                        tt.wait_ge(self.DV.h, a_dv)
                    for h in range(2):
                        for dc in range(8):
                            i = tt.matmul(qps[it % 2][:, h * 512:(h + 1) * 512],
                                          self.lnx_fm[:, (it * 8 + dc) * 128:(it * 8 + dc + 1) * 128],
                                          self.slotA[:, dc * D + h * 512: dc * D + (h + 1) * 512],
                                          start=(dc == 0), stop=(dc == 7))
                        self.PE.inc(i)
                self.on("tensor", peq)
                pe_n = self.PE.n

                def acq(s, it=it, pe_n=pe_n):
                    s.wait_ge(self.PE.h, pe_n)
                    i = s.copy(self.q_rm[:, it * D:(it + 1) * D], qps[it % 2][:, :])
                    self.AC.inc(i)
                self.on("scalar", acq)
                qevac[it] = self.AC.n
        self.pe_qdone = self.PE.n

    def emit_D(self, t):
        nc = self.nc
        pe_b = self.PE.n

        mtk = []

        def syd(s, pe_b=pe_b):
            s.wait_ge(self.PE.h, pe_b)
            mtk.append(self.dma_s(s, self.sslot, out=self.slotB[:, :], in_=self.mt_h))
        self.on("sync", syd)
        self.mt_tok = mtk[0]
        for s_i in range(self.NS):
            self.recon_subtile(t, s_i)
        self.pe_ffndone = self.PE.n

    def emit_ln_x(self, t, it, dstbuf, slot):
        R = self.R
        gate = self.xgate[self.xuse % 2]
        xslot = self.xuse % 2
        self.xuse += 1

        xt = []

        def syx(s, t=t, it=it, gate=gate, xslot=xslot):
            if gate:
                s.wait_ge(self.DV.h, gate)
            xt.append(self.dma_s(s, self.sx, out=self.x01[:, xslot * D:(xslot + 1) * D],
                      in_=self.x[t * R + it * 128: t * R + (it + 1) * 128, :]))
        self.on("sync", syx)
        x_tok = xt[0]
        lngate = self.lngate[slot]

        def dvln(v, x_tok=x_tok, xslot=xslot):
            self.wtok(v, x_tok)
            sl = self.x01[:, xslot * D:(xslot + 1) * D]
            v.bn_stats(self.st6[:, 0:6], sl[:, 0:512])
            i = v.bn_stats(self.st6[:, 6:12], sl[:, 512:1024]); self.DV.inc(i)
            v.wait_ge(self.DV.h, self.DV.n)
            i = v.bn_aggr(self.mv[:, :], self.st6[:, :]); self.DV.inc(i)
        self.on("vector", dvln)
        dv_n = self.DV.n

        def acln(s, dv_n=dv_n):
            s.wait_ge(self.DV.h, dv_n)
            i = s.activation(self.rstd[:, :], self.mv[:, 1:2], AF.Sqrt,
                             bias=self.epsc[:, 0:1], scale=1.0)
            self.AC.inc(i)
        self.on("scalar", acln)
        ac_n = self.AC.n

        def dvln2(v, ac_n=ac_n, xslot=xslot, slot=slot, lngate=lngate):
            v.wait_ge(self.AC.h, ac_n)
            self.wtok(v, lngate)
            i = v.reciprocal(self.rstd[:, :], self.rstd[:, :]); self.DV.inc(i)
            v.wait_ge(self.DV.h, self.DV.n)
            sl = self.x01[:, xslot * D:(xslot + 1) * D]
            i = v.tensor_scalar(dstbuf[:, slot * D:(slot + 1) * D],
                                sl, self.mv[:, 0:1], self.rstd[:, 0:1],
                                ALU.subtract, ALU.mult)
            self.DV.inc(i)
        self.on("vector", dvln2)
        self.xgate[xslot] = self.DV.n

    def chain_subtile(self, t, s_i):
        nc = self.nc
        ITS = self.ITS
        base_it = s_i * ITS
        ac_pre = self.AC.n
        dv_pre = self.DV.n
        pe_pre = self.PE.n

        ztoks = []

        def scz(s, base_it=base_it, ac_pre=ac_pre, pe_pre=pe_pre):
            s.wait_ge(self.AC.h, ac_pre)
            s.wait_ge(self.PE.h, pe_pre)
            for il in range(ITS):
                it = base_it + il
                ztoks.append(self.dma_s(s, self.tsub,
                    out=self.zfm.ap().rearrange("p (c m) -> p c m", m=128)[:, il * 8:(il + 1) * 8, :],
                    in_=self.z_rm[:, it * D:(it + 1) * D], transpose=True))
        self.on("scalar", scz)
        st_der = self.stream_state(self.tder)

        with nc.psum_tensor("aps0_%d_%d" % (t, s_i), [128, 512], F32) as aps0, \
             nc.psum_tensor("aps1_%d_%d" % (t, s_i), [128, 512], F32) as aps1, \
             nc.psum_tensor("rps0_%d_%d" % (t, s_i), [128, D], F32) as rps0, \
             nc.psum_tensor("rps1_%d_%d" % (t, s_i), [128, D], F32) as rps1, \
             nc.psum_tensor("dps0_%d_%d" % (t, s_i), [128, 512], F32) as dps0, \
             nc.psum_tensor("dps1_%d_%d" % (t, s_i), [128, 512], F32) as dps1:
            aps = [aps0, aps1]
            rps = [rps0, rps1]
            dps = [dps0, dps1]
            h_done = {}
            acw = {}
            htoks = []
            for il in range(ITS):
                prev_ac = acw.get(il - 2, 0)

                def pea(tt, il=il, st_der=st_der, prev_ac=prev_ac,
                        ac_pre=ac_pre, dv_pre=dv_pre):
                    for tk_ in [ztoks[il]] + st_der:
                        tt.wait_ge(tk_[0].h, tk_[1])
                    if prev_ac:
                        tt.wait_ge(self.AC.h, prev_ac)
                    else:
                        tt.wait_ge(self.AC.h, ac_pre)
                        tt.wait_ge(self.DV.h, dv_pre)
                    for hc in range(8):
                        tt.matmul(aps[il % 2][:, 0:256],
                                  self.zfm[:, (il * 8 + hc) * 128:(il * 8 + hc + 1) * 128],
                                  self.W1T_b[:, hc * 256:(hc + 1) * 256],
                                  start=(hc == 0), stop=False)
                    i = tt.matmul(aps[il % 2][:, 0:256], self.ones_r[:, :],
                                  self.b1r_b[:, :], start=False, stop=True)
                    self.PE.inc(i)
                self.on("tensor", pea)
                pe_a = self.PE.n
                dv_gate = h_done.get(il - 1, (0, 0))[1]

                def aca(s, il=il, pe_a=pe_a, dv_gate=dv_gate):
                    s.wait_ge(self.PE.h, pe_a)
                    if dv_gate:
                        s.wait_ge(self.DV.h, dv_gate)
                    ap = aps[il % 2][:, 0:256]
                    sl = slice((il % 2) * 256, (il % 2 + 1) * 256)
                    i = s.activation(self.hrm[:, il * 256:(il + 1) * 256], ap, AF.Gelu)
                    self.AC.inc(i)
                    i = s.copy(self.xa01[:, sl], ap); self.AC.inc(i)
                    i = s.activation(self.sq01[:, sl], ap, AF.Square); self.AC.inc(i)
                self.on("scalar", aca)
                acw[il] = self.AC.n
                ac_hil = self.AC.n

                def schil(s, il=il, ac_hil=ac_hil):
                    s.wait_ge(self.AC.h, ac_hil)
                    htoks.append(self.dma_s(s, self.tsub,
                        out=self.hfm.ap().rearrange("p (c m) -> p c m", m=128)[:, il * 2:(il + 1) * 2, :],
                        in_=self.hrm[:, il * 256:(il + 1) * 256], transpose=True))
                self.on("scalar", schil)

            def acE(s):
                s.wait_ge(self.AC.h, self.AC.n)
                for il in range(ITS):
                    sl = slice((il % 2) * 256, (il % 2 + 1) * 256)
                    i = s.activation(self.ex01[:, sl], self.sq01[:, sl], AF.Exp, scale=-0.5)
                    self.AC.inc(i)
            self.on("scalar", acE)
            ac_a = self.AC.n
            for il in range(ITS):
                def dvg(v, il=il, ac_a=ac_a):
                    v.wait_ge(self.AC.h, ac_a)
                    v.wait_ge(self.DV.h, self.DV.n)
                    sl = slice((il % 2) * 256, (il % 2 + 1) * 256)
                    i = v.tensor_single_scalar(self.ms01[:, :], self.sq01[:, sl], 1e-24, ALU.is_lt)
                    self.DV.inc(i)
                    i = v.tensor_copy(self.xs01[:, :], self.xa01[:, sl]); self.DV.inc(i)
                    v.wait_ge(self.DV.h, self.DV.n)
                    i = v.copy_predicated(self.xs01[:, :], self.ms01[:, :], self.epsT[:, :])
                    self.DV.inc(i)
                    v.wait_ge(self.DV.h, self.DV.n)
                    i = v.reciprocal(self.xs01[:, :], self.xs01[:, :]); self.DV.inc(i)
                    v.wait_ge(self.DV.h, self.DV.n)
                    i = v.tensor_tensor(self.ph01[:, :], self.hrm[:, il * 256:(il + 1) * 256],
                                        self.xs01[:, :], ALU.mult)
                    self.DV.inc(i)
                    v.wait_ge(self.DV.h, self.DV.n)
                    i = v.copy_predicated(self.ph01[:, :], self.ms01[:, :], self.halfT[:, :])
                    self.DV.inc(i)
                    i = v.tensor_tensor(self.xs01[:, :], self.xa01[:, sl], self.ex01[:, sl], ALU.mult)
                    self.DV.inc(i)
                    v.wait_ge(self.DV.h, self.DV.n)
                    i = v.scalar_tensor_tensor(self.gpb[:, il * 256:(il + 1) * 256],
                                               self.xs01[:, :], GELU_C, self.ph01[:, :],
                                               ALU.mult, ALU.add)
                    self.DV.inc(i)
                self.on("vector", dvg)
                h_done[il] = (self.AC.n, self.DV.n)



            edv = {}
            etoks = []
            for il in range(ITS):
                it = base_it + il
                prev_dv = edv.get(il - 2, 0)

                def per(tt, il=il, prev_dv=prev_dv):
                    self.wtok(tt, htoks[il])
                    if prev_dv:
                        tt.wait_ge(self.DV.h, prev_dv)
                    for h in range(2):
                        for m2c in range(2):
                            tt.matmul(rps[il % 2][:, h * 512:(h + 1) * 512],
                                      self.hfm[:, (il * 2 + m2c) * 128:(il * 2 + m2c + 1) * 128],
                                      self.W2T_b[:, m2c * D + h * 512:m2c * D + (h + 1) * 512],
                                      start=(m2c == 0), stop=False)
                        i = tt.matmul(rps[il % 2][:, h * 512:(h + 1) * 512],
                                      self.ones_r[:, :], self.b2r_b[:, h * 512:(h + 1) * 512],
                                      start=False, stop=True)
                        self.PE.inc(i)
                self.on("tensor", per)
                pe_r = self.PE.n

                def dve(v, il=il, it=it, pe_r=pe_r):
                    v.wait_ge(self.PE.h, pe_r)
                    i = v.tensor_tensor(self.erm[:, il * D:(il + 1) * D], rps[il % 2][:, :],
                                        self.y_rm[:, it * D:(it + 1) * D], ALU.add)
                    self.DV.inc(i)
                self.on("vector", dve)
                edv[il] = self.DV.n
                dv_eil = self.DV.n

                def sceil(s, il=il, dv_eil=dv_eil):
                    s.wait_ge(self.DV.h, dv_eil)
                    etoks.append(self.dma_s(s, self.tsub,
                        out=self.efm.ap().rearrange("p (c m) -> p c m", m=128)[:, il * 8:(il + 1) * 8, :],
                        in_=self.erm[:, il * D:(il + 1) * D], transpose=True))
                self.on("scalar", sceil)

            dadv = {}
            for il in range(ITS):
                prev_dv = dadv.get(il - 2, 0)

                def ped(tt, il=il, prev_dv=prev_dv):
                    self.wtok(tt, etoks[il])
                    if prev_dv:
                        tt.wait_ge(self.DV.h, prev_dv)
                    for jc in range(8):
                        i = tt.matmul(dps[il % 2][:, 0:256],
                                      self.efm[:, (il * 8 + jc) * 128:(il * 8 + jc + 1) * 128],
                                      self.W2n_b[:, jc * 256:(jc + 1) * 256],
                                      start=(jc == 0), stop=(jc == 7))
                    self.PE.inc(i)
                self.on("tensor", ped)
                pe_d = self.PE.n

                def dvda(v, il=il, pe_d=pe_d):
                    v.wait_ge(self.PE.h, pe_d)
                    i = v.tensor_tensor(self.dab[:, il * 256:(il + 1) * 256],
                                        dps[il % 2][:, 0:256],
                                        self.gpb[:, il * 256:(il + 1) * 256], ALU.mult)
                    self.DV.inc(i)
                self.on("vector", dvda)
                dadv[il] = self.DV.n
            loops_dv = self.DV.n
            loops_ac = self.AC.n

        with nc.psum_tensor("gw2ps_%d_%d" % (t, s_i), [128, 8 * 256], F32) as gw2ps, \
             nc.psum_tensor("gw1ps_%d_%d" % (t, s_i), [128, 2 * D], F32) as gw1ps:

            def peg(tt, base_it=base_it, loops_dv=loops_dv, loops_ac=loops_ac,
                    gw2ps=gw2ps, gw1ps=gw1ps):
                tt.wait_ge(self.DV.h, loops_dv)
                tt.wait_ge(self.AC.h, loops_ac)
                for jc in range(8):
                    for il in range(ITS):
                        i = tt.matmul(gw2ps[:, jc * 256:(jc + 1) * 256],
                                      self.erm[:, il * D + jc * 128: il * D + (jc + 1) * 128],
                                      self.hrm[:, il * 256:(il + 1) * 256],
                                      start=(il == 0), stop=(il == ITS - 1))
                    self.PE.inc(i)
                for m1c in range(2):
                    for h in range(2):
                        for il in range(ITS):
                            it = base_it + il
                            i = tt.matmul(gw1ps[:, m1c * D + h * 512: m1c * D + (h + 1) * 512],
                                          self.dab[:, il * 256 + m1c * 128: il * 256 + (m1c + 1) * 128],
                                          self.z_rm[:, it * D + h * 512: it * D + (h + 1) * 512],
                                          start=(il == 0), stop=(il == ITS - 1))
                        self.PE.inc(i)
            self.on("tensor", peg)
            pe_g = self.PE.n

            def dvgw(v, s_i=s_i, pe_g=pe_g, gw2ps=gw2ps, gw1ps=gw1ps):
                v.wait_ge(self.PE.h, pe_g)
                if s_i == 0:
                    i = v.tensor_copy(self.gW2a[:, :], gw2ps[:, :]); self.DV.inc(i)
                    i = v.tensor_copy(self.gW1a[:, :], gw1ps[:, :]); self.DV.inc(i)
                else:
                    i = v.tensor_tensor(self.gW2a[:, :], self.gW2a[:, :], gw2ps[:, :], ALU.add)
                    self.DV.inc(i)
                    i = v.tensor_tensor(self.gW1a[:, :], self.gW1a[:, :], gw1ps[:, :], ALU.add)
                    self.DV.inc(i)
            self.on("vector", dvgw)
            gw_dv = self.DV.n

        with nc.psum_tensor("b2ps_%d_%d" % (t, s_i), [1, D], F32) as b2ps, \
             nc.psum_tensor("b1ps_%d_%d" % (t, s_i), [1, M1], F32) as b1ps:

            def peb(tt, gw_dv=gw_dv, b2ps=b2ps, b1ps=b1ps):
                tt.wait_ge(self.DV.h, gw_dv)
                for h in range(2):
                    for il in range(ITS):
                        i = tt.matmul(b2ps[:, h * 512:(h + 1) * 512], self.ones_c[:, :],
                                      self.erm[:, il * D + h * 512: il * D + (h + 1) * 512],
                                      start=(il == 0), stop=(il == ITS - 1))
                    self.PE.inc(i)
                for il in range(ITS):
                    i = tt.matmul(b1ps[:, :], self.ones_c[:, :],
                                  self.dab[:, il * 256:(il + 1) * 256],
                                  start=(il == 0), stop=(il == ITS - 1))
                self.PE.inc(i)
            self.on("tensor", peb)
            pe_bb = self.PE.n

            def dvb(v, s_i=s_i, pe_bb=pe_bb, b2ps=b2ps, b1ps=b1ps):
                v.wait_ge(self.PE.h, pe_bb)
                if s_i == 0:
                    i = v.tensor_copy(self.b2ar[:, :], b2ps[:, :]); self.DV.inc(i)
                    i = v.tensor_copy(self.b1ar[:, :], b1ps[:, :]); self.DV.inc(i)
                else:
                    i = v.tensor_tensor(self.b2ar[:, :], self.b2ar[:, :], b2ps[:, :], ALU.add)
                    self.DV.inc(i)
                    i = v.tensor_tensor(self.b1ar[:, :], self.b1ar[:, :], b1ps[:, :], ALU.add)
                    self.DV.inc(i)
            self.on("vector", dvb)

    def ar_start(self, t):
        dv_g = self.DV.n
        g2d = self.g_in[0:128 * 4096].rearrange("(p c) -> p c", c=2048)

        def sya(s, dv_g=dv_g):
            s.wait_ge(self.DV.h, dv_g)
            self.dma_s(s, self.sg, out=g2d[0:128, :], in_=self.gW1a[:, :])
            self.dma_s(s, self.sg, out=g2d[128:256, :], in_=self.gW2a[:, :])
            self.dma_s(s, self.sg, out=self.g_in[128 * 4096:128 * 4096 + 1024][None, :],
                       in_=self.b2ar[:, :])
            self.dma_s(s, self.sg, out=self.g_in[128 * 4096 + 1024:128 * 4096 + 1280][None, :],
                       in_=self.b1ar[:, :])
        self.on("sync", sya)
        st_g = self.stream_state(self.sg)

        def gpc(g, st_g=st_g):
            for tk_ in st_g:
                g.wait_ge(tk_[0].h, tk_[1])
            i = g.collective_compute("AllReduce", ALU.add,
                                     replica_groups=[list(range(NCORES))],
                                     ins=[self.g_in], outs=[self.g_out])
            self.CC.inc(i)
        self.on("gpsimd", gpc)

    def ar_finish(self, t):
        g2o = self.g_out[0:128 * 4096].rearrange("(p c) -> p c", c=2048)
        cc_n = self.CC.n
        rtoks = []

        def syb(g, cc_n=cc_n):
            g.wait_ge(self.CC.h, cc_n)
            for tk_ in self.stream_state(self.sg):
                g.wait_ge(tk_[0].h, tk_[1])
            rtoks.append(self.dma_s(g, self.sgr, out=self.gW1a[:, :], in_=g2o[0:128, :]))
            rtoks.append(self.dma_s(g, self.sgr, out=self.gW2a[:, :], in_=g2o[128:256, :]))
            rtoks.append(self.dma_s(g, self.sgr, out=self.b2ar[:, :],
                         in_=self.g_out[128 * 4096:128 * 4096 + 1024][None, :]))
            rtoks.append(self.dma_s(g, self.sgr, out=self.b1ar[:, :],
                         in_=self.g_out[128 * 4096 + 1024:128 * 4096 + 1280][None, :]))
        self.on("gpsimd", syb)
        c = self.upd_c

        st_ln = self.stream_state(self.tlnx)

        def dvu1(v, c=c, st_ln=st_ln, rtoks=rtoks):
            for tk_ in rtoks:
                v.wait_ge(tk_[0].h, tk_[1])
            for tk_ in st_ln:
                v.wait_ge(tk_[0].h, tk_[1])
            i = v.scalar_tensor_tensor(self.W1n_f[:, :], self.gW1a[:, :], -c,
                                       self.W1n_f[:, :], ALU.mult, ALU.add)
            self.DV.inc(i)
            i = v.scalar_tensor_tensor(self.b1r_f[:, :], self.b1ar[:, :], -c,
                                       self.b1r_f[:, :], ALU.mult, ALU.add)
            self.DV.inc(i)
            i = v.scalar_tensor_tensor(self.b2r_f[:, :], self.b2ar[:, :], -c,
                                       self.b2r_f[:, :], ALU.mult, ALU.add)
            self.DV.inc(i)
            v.wait_ge(self.DV.h, self.DV.n)
            i = v.tensor_copy(self.W1n_b[:, :], self.W1n_f[:, :]); self.DV.inc(i)
            i = v.tensor_copy(self.b1r_b[:, :], self.b1r_f[:, :]); self.DV.inc(i)
            i = v.tensor_copy(self.b2r_b[:, :], self.b2r_f[:, :]); self.DV.inc(i)
        self.on("vector", dvu1)
        dv_u1 = self.DV.n

        def scu1(s, dv_u1=dv_u1):
            s.wait_ge(self.DV.h, dv_u1)
            self.emit_derive_sc(s, part="w1")
        self.on("scalar", scu1)

        def dvu2(v, c=c):
            i = v.scalar_tensor_tensor(self.W2n_f[:, :], self.gW2a[:, :], -c,
                                       self.W2n_f[:, :], ALU.mult, ALU.add)
            self.DV.inc(i)
            v.wait_ge(self.DV.h, self.DV.n)
            i = v.tensor_copy(self.W2n_b[:, :], self.W2n_f[:, :]); self.DV.inc(i)
        self.on("vector", dvu2)
        dv_u2 = self.DV.n

        def scu2(s, dv_u2=dv_u2):
            s.wait_ge(self.DV.h, dv_u2)
            self.emit_derive_sc(s, part="w2")
        self.on("scalar", scu2)
        self.lngate = [self._w1toks[0], self._w1toks[1]]

    def recon_subtile(self, t, s_i):
        nc = self.nc
        R, ITS = self.R, self.ITS
        base_it = s_i * ITS
        st_der = list(getattr(self, "_w1toks", []))
        st_der2 = list(getattr(self, "_w2toks", []))
        dv_pre = self.DV.n
        st_out_pre = self.stream_state(self.sout)

        qtoks = []

        def scq(s, base_it=base_it):
            for il in range(ITS):
                it = base_it + il
                qtoks.append(self.dma_s(s, self.tsub,
                    out=self.zfm.ap().rearrange("p (c m) -> p c m", m=128)[:, il * 8:(il + 1) * 8, :],
                    in_=self.q_rm[:, it * D:(it + 1) * D], transpose=True))
        self.on("scalar", scq)

        h2toks = []
        with nc.psum_tensor("a2ps0_%d_%d" % (t, s_i), [128, 512], F32) as a2ps0, \
             nc.psum_tensor("a2ps1_%d_%d" % (t, s_i), [128, 512], F32) as a2ps1:
            a2ps = [a2ps0, a2ps1]
            h2ac = {}
            for il in range(ITS):
                prev = h2ac.get(il - 2, 0)

                def pea2(tt, il=il, st_der=st_der, prev=prev, dv_pre=dv_pre,
                         a2ps=a2ps):
                    for tk_ in [qtoks[il]] + st_der:
                        tt.wait_ge(tk_[0].h, tk_[1])
                    if prev:
                        tt.wait_ge(self.AC.h, prev)
                    else:
                        tt.wait_ge(self.DV.h, dv_pre)
                    for hc in range(8):
                        tt.matmul(a2ps[il % 2][:, 0:256],
                                  self.zfm[:, (il * 8 + hc) * 128:(il * 8 + hc + 1) * 128],
                                  self.W1T_b[:, hc * 256:(hc + 1) * 256],
                                  start=(hc == 0), stop=False)
                    i = tt.matmul(a2ps[il % 2][:, 0:256], self.ones_r[:, :],
                                  self.b1r_b[:, :], start=False, stop=True)
                    self.PE.inc(i)
                self.on("tensor", pea2)
                pe_a2 = self.PE.n

                def ach2(s, il=il, pe_a2=pe_a2, a2ps=a2ps):
                    s.wait_ge(self.PE.h, pe_a2)
                    i = s.activation(self.hrm[:, il * 256:(il + 1) * 256],
                                     a2ps[il % 2][:, 0:256], AF.Gelu)
                    self.AC.inc(i)
                self.on("scalar", ach2)
                h2ac[il] = self.AC.n
                ac_h2il = self.AC.n

                def sch2il(s, il=il, ac_h2il=ac_h2il):
                    s.wait_ge(self.AC.h, ac_h2il)
                    h2toks.append(self.dma_s(s, self.tsub,
                        out=self.hfm.ap().rearrange("p (c m) -> p c m", m=128)[:, il * 2:(il + 1) * 2, :],
                        in_=self.hrm[:, il * 256:(il + 1) * 256], transpose=True))
                self.on("scalar", sch2il)

        ac_h2all = self.AC.n
        with nc.psum_tensor("ops0_%d_%d" % (t, s_i), [128, D], F32) as ops0, \
             nc.psum_tensor("ops1_%d_%d" % (t, s_i), [128, D], F32) as ops1, \
             nc.psum_tensor("fps0_%d_%d" % (t, s_i), [128, D], F32) as fps0, \
             nc.psum_tensor("fps1_%d_%d" % (t, s_i), [128, D], F32) as fps1:
            ops = [ops0, ops1]
            fps = [fps0, fps1]
            xr_toks = []
            xslots = []
            for il in range(ITS):
                it = base_it + il
                gate = self.xgate[self.xuse % 2]
                xslot = self.xuse % 2
                self.xuse += 1
                xslots.append(xslot)
                xrt = []

                def syxr(s, t=t, it=it, gate=gate, xslot=xslot):
                    if gate:
                        s.wait_ge(self.DV.h, gate)
                    xrt.append(self.dma_s(s, self.sx, out=self.x01[:, xslot * D:(xslot + 1) * D],
                               in_=self.x[t * R + it * 128: t * R + (it + 1) * 128, :]))
                self.on("sync", syxr)
                xr_toks.append(xrt[0])
            ln2dv = {}
            l2toks = []
            for il in range(ITS):
                it = base_it + il
                xr_tok = xr_toks[il]
                xslot = xslots[il]
                prev = ln2dv.get(il - 2, 0)

                def peo(tt, il=il, prev=prev, ops=ops, st_der2=st_der2,
                        ac_h2all=ac_h2all):
                    for tk_ in [h2toks[il]] + st_der2:
                        tt.wait_ge(tk_[0].h, tk_[1])
                    if prev:
                        tt.wait_ge(self.DV.h, prev)
                    else:
                        tt.wait_ge(self.AC.h, ac_h2all)
                    for h in range(2):
                        for m2c in range(2):
                            tt.matmul(ops[il % 2][:, h * 512:(h + 1) * 512],
                                      self.hfm[:, (il * 2 + m2c) * 128:(il * 2 + m2c + 1) * 128],
                                      self.W2T_b[:, m2c * D + h * 512:m2c * D + (h + 1) * 512],
                                      start=(m2c == 0), stop=False)
                        i = tt.matmul(ops[il % 2][:, h * 512:(h + 1) * 512],
                                      self.ones_r[:, :], self.b2r_b[:, h * 512:(h + 1) * 512],
                                      start=False, stop=True)
                        self.PE.inc(i)
                self.on("tensor", peo)
                pe_o = self.PE.n

                def dvx2(v, il=il, it=it, pe_o=pe_o, xr_tok=xr_tok, xslot=xslot,
                         st_out_pre=st_out_pre, ops=ops):
                    v.wait_ge(self.PE.h, pe_o)
                    for tk_ in st_out_pre:
                        v.wait_ge(tk_[0].h, tk_[1])
                    i = v.tensor_tensor(self.x2s[:, il * D:(il + 1) * D], ops[il % 2][:, :],
                                        self.q_rm[:, it * D:(it + 1) * D], ALU.add)
                    self.DV.inc(i)
                    self.wtok(v, xr_tok)
                    v.wait_ge(self.DV.h, self.DV.n)
                    i = v.tensor_tensor(self.x2s[:, il * D:(il + 1) * D],
                                        self.x2s[:, il * D:(il + 1) * D],
                                        self.x01[:, xslot * D:(xslot + 1) * D], ALU.add)
                    self.DV.inc(i)
                    sl = self.x2s[:, il * D:(il + 1) * D]
                    v.wait_ge(self.DV.h, self.DV.n)
                    v.bn_stats(self.st6[:, 0:6], sl[:, 0:512])
                    i = v.bn_stats(self.st6[:, 6:12], sl[:, 512:1024]); self.DV.inc(i)
                    v.wait_ge(self.DV.h, self.DV.n)
                    i = v.bn_aggr(self.mv[:, :], self.st6[:, :]); self.DV.inc(i)
                self.on("vector", dvx2)
                self.xgate[xslot] = self.DV.n
                dv_st = self.DV.n

                def acl2(s, dv_st=dv_st):
                    s.wait_ge(self.DV.h, dv_st)
                    i = s.activation(self.rstd[:, :], self.mv[:, 1:2], AF.Sqrt,
                                     bias=self.epsc[:, 0:1], scale=1.0)
                    self.AC.inc(i)
                self.on("scalar", acl2)
                ac_l2 = self.AC.n

                def dvl2(v, il=il, ac_l2=ac_l2):
                    v.wait_ge(self.AC.h, ac_l2)
                    i = v.reciprocal(self.rstd[:, :], self.rstd[:, :]); self.DV.inc(i)
                    v.wait_ge(self.DV.h, self.DV.n)
                    i = v.tensor_scalar(self.ln2r[:, il * D:(il + 1) * D],
                                        self.x2s[:, il * D:(il + 1) * D],
                                        self.mv[:, 0:1], self.rstd[:, 0:1],
                                        ALU.subtract, ALU.mult)
                    self.DV.inc(i)
                self.on("vector", dvl2)
                ln2dv[il] = self.DV.n
                dv_l2il = self.DV.n

                def scl2il(s, il=il, dv_l2il=dv_l2il):
                    s.wait_ge(self.DV.h, dv_l2il)
                    l2toks.append(self.dma_s(s, self.tsub,
                        out=self.ln2f.ap().rearrange("p (c m) -> p c m", m=128)[:, il * 8:(il + 1) * 8, :],
                        in_=self.ln2r[:, il * D:(il + 1) * D], transpose=True))
                self.on("scalar", scl2il)

            outdv = {}
            for il in range(ITS):
                it = base_it + il
                prev = outdv.get(il - 2, 0)

                def pef(tt, il=il, prev=prev, mt_tok=self.mt_tok, fps=fps,
                        ac_h2all=ac_h2all):
                    self.wtok(tt, l2toks[il])
                    self.wtok(tt, mt_tok)
                    if prev:
                        tt.wait_ge(self.DV.h, prev)
                    else:
                        tt.wait_ge(self.AC.h, ac_h2all)
                    for h in range(2):
                        for kc in range(8):
                            tt.matmul(fps[il % 2][:, h * 512:(h + 1) * 512],
                                      self.ln2f[:, (il * 8 + kc) * 128:(il * 8 + kc + 1) * 128],
                                      self.slotB[:, kc * D + h * 512: kc * D + (h + 1) * 512],
                                      start=(kc == 0), stop=False)
                        i = tt.matmul(fps[il % 2][:, h * 512:(h + 1) * 512],
                                      self.ones_r[:, :], self.fbias_b[:, h * 512:(h + 1) * 512],
                                      start=False, stop=True)
                        self.PE.inc(i)
                self.on("tensor", pef)
                pe_f = self.PE.n

                def dvo(v, il=il, pe_f=pe_f, fps=fps):
                    v.wait_ge(self.PE.h, pe_f)
                    i = v.tensor_tensor(self.x2s[:, il * D:(il + 1) * D],
                                        fps[il % 2][:, :],
                                        self.x2s[:, il * D:(il + 1) * D], ALU.add)
                    self.DV.inc(i)
                self.on("vector", dvo)
                dv_o = self.DV.n
                outdv[il] = dv_o

                def syo(s, t=t, it=it, il=il, dv_o=dv_o):
                    s.wait_ge(self.DV.h, dv_o)
                    self.dma_s(s, self.sout, out=self.y[t * R + it * 128: t * R + (it + 1) * 128, :],
                               in_=self.x2s[:, il * D:(il + 1) * D])
                self.on("sync", syo)


def build_program(R):
    return Builder(R).build()


def kernel(L0_attn_in_seq, L0_attn_t_k, L0_attn_t_v, L0_attn_t_q, L0_attn_W1,
           L0_attn_b1, L0_attn_W2, L0_attn_b2, L0_ffn_G1_R, L0_ffn_G2_R,
           L0_ffn_G1_I, L0_ffn_G2_I, L0_ffn_bias_real, L0_ffn_bias_imag):
    x = np.asarray(L0_attn_in_seq)
    Tn, N, Dn = x.shape
    R = N // NCORES
    nc = build_program(R)
    shared = {
        "t_k": np.ascontiguousarray(L0_attn_t_k),
        "t_v": np.ascontiguousarray(L0_attn_t_v),
        "t_q": np.ascontiguousarray(L0_attn_t_q),
        "W1": np.ascontiguousarray(L0_attn_W1),
        "b1": np.ascontiguousarray(L0_attn_b1),
        "W2": np.ascontiguousarray(L0_attn_W2),
        "b2": np.ascontiguousarray(L0_attn_b2),
        "G1R": np.ascontiguousarray(L0_ffn_G1_R),
        "G2R": np.ascontiguousarray(L0_ffn_G2_R),
        "G1I": np.ascontiguousarray(L0_ffn_G1_I),
        "G2I": np.ascontiguousarray(L0_ffn_G2_I),
        "br": np.ascontiguousarray(L0_ffn_bias_real),
        "bi": np.ascontiguousarray(L0_ffn_bias_imag),
    }
    in_maps = []
    for c in range(NCORES):
        m = dict(shared)
        m["x"] = np.ascontiguousarray(x[:, c * R:(c + 1) * R, :]).reshape(Tn * R, Dn)
        in_maps.append(m)
    res = run_bass_kernel_spmd(nc, in_maps, core_ids=list(range(NCORES)))
    out = np.empty((Tn, N, Dn), dtype=np.float32)
    for c in range(NCORES):
        out[:, c * R:(c + 1) * R, :] = res.results[c]["y"].reshape(Tn, R, Dn)
    return out


# version 15
# speedup vs baseline: 1.0178x; 1.0062x over previous
import numpy as np
import concourse.bass as bass
import concourse.mybir as mybir
from concourse.bass_utils import run_bass_kernel_spmd

F32 = mybir.dt.float32
BF16 = mybir.dt.bfloat16
U8 = mybir.dt.uint8
AF = mybir.ActivationFunctionType
ALU = mybir.AluOpType

D = 1024
M1 = 256
T = 4
LR = 1e-3
NCORES = 8
GELU_C = float(1.0 / np.sqrt(2.0 * np.pi))


class Sem:
    def __init__(self, nc, name):
        self.h = nc.alloc_semaphore(name=name)
        self.n = 0

    def inc(self, instr, amt=1):
        instr.then_inc(self.h, amt)
        self.n += amt
        return self.n


class Stream:
    def __init__(self, nc, name):
        self.sems = [Sem(nc, name + "_0"), Sem(nc, name + "_1")]
        self.i = 0


class Builder:
    def __init__(self, R):
        self.R = R
        self.IT = R // 128
        self.NS = 4 if self.IT >= 8 else (2 if self.IT >= 2 else 1)
        self.ITS = self.IT // self.NS
        self.upd_c = float(LR * 2.0 / ((NCORES * R) * D))
        nc = bass.Bass("TRN2", target_bir_lowering=False)
        self.nc = nc

        self.x = nc.dram_tensor("x", [T * R, D], F32, kind="ExternalInput").ap()
        self.y = nc.dram_tensor("y", [T * R, D], F32, kind="ExternalOutput").ap()
        self.tk = nc.dram_tensor("t_k", [D, D], F32, kind="ExternalInput").ap()
        self.tv = nc.dram_tensor("t_v", [D, D], F32, kind="ExternalInput").ap()
        self.tq = nc.dram_tensor("t_q", [D, D], F32, kind="ExternalInput").ap()
        self.W1 = nc.dram_tensor("W1", [M1, D], F32, kind="ExternalInput").ap()
        self.b1 = nc.dram_tensor("b1", [M1], F32, kind="ExternalInput").ap()
        self.W2 = nc.dram_tensor("W2", [D, M1], F32, kind="ExternalInput").ap()
        self.b2 = nc.dram_tensor("b2", [D], F32, kind="ExternalInput").ap()
        self.G1R = nc.dram_tensor("G1R", [1, 32, 4, 32], F32, kind="ExternalInput").ap()
        self.G2R = nc.dram_tensor("G2R", [4, 16, 1, 16], F32, kind="ExternalInput").ap()
        self.G1I = nc.dram_tensor("G1I", [1, 32, 4, 32], F32, kind="ExternalInput").ap()
        self.G2I = nc.dram_tensor("G2I", [4, 16, 1, 16], F32, kind="ExternalInput").ap()
        self.br = nc.dram_tensor("br", [512], F32, kind="ExternalInput").ap()
        self.bi = nc.dram_tensor("bi", [512], F32, kind="ExternalInput").ap()
        self.tk_h = nc.dram_tensor("tk_h", [128, 8 * D], BF16).ap()
        self.tv_h = nc.dram_tensor("tv_h", [128, 8 * D], BF16).ap()
        self.tq_h = nc.dram_tensor("tq_h", [128, 8 * D], BF16).ap()
        self.mt_h = nc.dram_tensor("mt_h", [128, 8 * D], BF16).ap()
        self.wr_h = nc.dram_tensor("wr_h", [512, 512], BF16).ap()
        self.wi_h = nc.dram_tensor("wi_h", [512, 512], BF16).ap()
        self.g_in = nc.dram_tensor("g_in", [128 * 4096 + 1024 + 256], F32).ap()
        self.g_out = nc.dram_tensor("g_out", [128 * 4096 + 1024 + 256], F32).ap()

        self.PE = Sem(nc, "sPE")
        self.DV = Sem(nc, "sDV")
        self.AC = Sem(nc, "sAC")
        self.SY = Sem(nc, "sSY")
        self.SC = Sem(nc, "sSC")
        self.GP = Sem(nc, "sGP")
        self.CC = Sem(nc, "sCC")
        self.sx = Stream(nc, "sx")
        self.sout = Stream(nc, "sout")
        self.sslot = Stream(nc, "sslot")
        self.sg = Stream(nc, "sg")
        self.tlnx = Stream(nc, "tlnx")
        self.tsub = Stream(nc, "tsub")
        self.tder = Stream(nc, "tder")
        self.sgr = Stream(nc, "sgr")
        self.xgate = [0, 0]
        self.lngate = [None, None]
        self.xuse = 0
        self.pe_qdone = 0
        self.pe_ffndone = 0

    def on(self, eng, fn):
        getattr(self.block, eng)(fn)

    def dma_s(self, eng, stream, out, in_, **kw):
        sem = stream.sems[stream.i % 2]
        stream.i += 1
        eng.wait_ge(sem.h, sem.n)
        i = eng.dma_start(out=out, in_=in_, **kw)
        sem.inc(i, 16)
        return (sem, sem.n)

    def wtok(self, eng, tok):
        if tok is not None:
            eng.wait_ge(tok[0].h, tok[1])

    def wstream(self, eng, stream):
        for sem in stream.sems:
            if sem.n:
                eng.wait_ge(sem.h, sem.n)

    def stream_state(self, stream):
        return [(sem, sem.n) for sem in stream.sems if sem.n]

    def dma(self, eng, sem, out, in_, **kw):
        eng.wait_ge(sem.h, sem.n)
        i = eng.dma_start(out=out, in_=in_, **kw)
        sem.inc(i, 16)
        return i

    def barrier(self):
        sems = [self.PE, self.DV, self.AC, self.SY, self.SC, self.GP, self.CC]
        for stx in (self.sx, self.sout, self.sslot, self.sg, self.sgr,
                    self.tlnx, self.tsub, self.tder):
            sems += stx.sems
        vals = [(s.h, s.n) for s in sems if s.n > 0]

        def mk(e):
            for h, n in vals:
                e.wait_ge(h, n)
        for eng in ("tensor", "vector", "scalar", "sync", "gpsimd"):
            self.on(eng, mk)

    def build(self):
        nc = self.nc
        import contextlib
        stack = contextlib.ExitStack()
        sb = lambda name, shape, dt: stack.enter_context(nc.sbuf_tensor(name, shape, dt))
        IT, ITS = self.IT, self.ITS
        self.W1n_f = sb("W1n_f", [128, 2 * D], F32)
        self.W2n_f = sb("W2n_f", [128, 8 * M1], F32)
        self.W2n_b = sb("W2n_b", [128, 8 * M1], BF16)
        self.W1T_b = sb("W1T_b", [128, 8 * M1], BF16)
        self.W2T_b = sb("W2T_b", [128, 2 * D], BF16)
        self.b1r_f = sb("b1r_f", [1, M1], F32)
        self.b2r_f = sb("b2r_f", [1, D], F32)
        self.b1r_b = sb("b1r_b", [1, M1], BF16)
        self.b2r_b = sb("b2r_b", [1, D], BF16)
        self.fbias_b = sb("fbias_b", [1, D], BF16)
        self.ones_r = sb("ones_r", [1, 128], BF16)
        self.ones_c = sb("ones_c", [128, 1], BF16)
        self.epsc = sb("epsc", [128, 1], F32)
        self.epsT = sb("epsT", [128, M1], F32)
        self.halfT = sb("halfT", [128, M1], F32)
        self.slotA = sb("slotA", [128, 8 * D], BF16)
        self.slotB = sb("slotB", [128, 8 * D], BF16)
        self.lnx_fm = sb("lnx_fm", [128, IT * D], BF16)
        self.z_rm = sb("z_rm", [128, IT * D], BF16)
        self.y_rm = sb("y_rm", [128, IT * D], BF16)
        self.q_rm = sb("q_rm", [128, IT * D], BF16)
        self.gW1a = sb("gW1a", [128, 2 * D], F32)
        self.gW2a = sb("gW2a", [128, 8 * M1], F32)
        self.b1ar = sb("b1ar", [1, M1], F32)
        self.b2ar = sb("b2ar", [1, D], F32)
        self.st6 = sb("st6", [128, 12], F32)
        self.mv = sb("mv", [128, 2], F32)
        self.rstd = sb("rstd", [128, 1], F32)
        self.x01 = sb("x01", [128, 2 * D], F32)
        self.lnxr = sb("lnxr", [128, 2 * D], BF16)
        self.W1n_b = self.lnxr.ap()
        self.zfm = sb("zfm", [128, ITS * D], BF16)
        self.erm = sb("erm", [128, ITS * D], BF16)
        self.efm = sb("efm", [128, ITS * D], BF16)
        self.hrm = sb("hrm", [128, ITS * M1], BF16)
        self.hfm = sb("hfm", [128, ITS * M1], BF16)
        self.gpb = sb("gpb", [128, ITS * M1], BF16)
        self.dab = sb("dab", [128, ITS * M1], BF16)
        self.xa01 = sb("xa01", [128, 2 * M1], BF16)
        self.sq01 = sb("sq01", [128, 2 * M1], BF16)
        self.ex01 = sb("ex01", [128, 2 * M1], BF16)
        self.ms01 = sb("ms01", [128, M1], U8)
        self.xs01 = sb("xs01", [128, M1], F32)
        self.ph01 = sb("ph01", [128, M1], F32)
        self.x2s = sb("x2s", [128, ITS * D], F32)
        self.ln2r = sb("ln2r", [128, ITS * D], BF16)
        self.ln2f = sb("ln2f", [128, ITS * D], BF16)
        self.p1sb = self.erm[:, 0:2048]
        self.g1sb = self.efm[0:4, 0:1024]
        self.g2sb = self.efm[0:4, 1024:1280]

        import os
        nphase = int(os.environ.get("ATH_PHASES", "99"))
        with nc.Block() as block:
            self.block = block
            self.init_phase()
            if nphase >= 1:
                self._init_gates = list(getattr(self, "_stage_gates", []))
                nT = min(T, nphase)
                self.emit_A(0)
                self._mt_gate_once = getattr(self, "_mt_store_gate", None)
                self.emit_q(0)
                p1g = (self.stream_state(self.sg) + self.stream_state(self.sout)
                       + [(self.GP, self.GP.n)])

                def pre_chain_gate(e, p1g=p1g):
                    for sem, v in p1g:
                        e.wait_ge(sem.h, v)
                self.on("vector", pre_chain_gate)
                self.on("scalar", pre_chain_gate)
                for t in range(nT):
                    for s_i in range(self.NS):
                        self.chain_subtile(t, s_i)
                    self.ar_start(t)
                    if t + 1 < nT:
                        self.emit_A(t + 1)
                    self.ar_finish(t)
                    self.emit_D(t)
                    if t + 1 < nT:
                        self.emit_q(t + 1)
            flush = [(self.SY, self.SY.n)]
            for stx in (self.sx, self.sout, self.sslot, self.sg, self.sgr,
                        self.tlnx, self.tsub, self.tder):
                flush += self.stream_state(stx)

            def fin(s, flush=flush):
                for sem, v in flush:
                    if v:
                        s.wait_ge(sem.h, v)
            self.on("sync", fin)
        stack.close()
        return nc

    def init_phase(self):
        import os
        lvl = int(os.environ.get("ATH_INIT", "99"))
        nc = self.nc

        def gp(g):
            self.dma(g, self.GP, out=self.W1n_f.ap().rearrange("p (c f) -> p c f", c=2),
                     in_=self.W1.rearrange("(c p) f -> p c f", p=128))
            self.dma(g, self.GP, out=self.W2n_f.ap().rearrange("p (c f) -> p c f", c=8),
                        in_=self.W2.rearrange("(c p) f -> p c f", p=128))
            self.dma(g, self.GP, out=self.b1r_f[:, :], in_=self.b1[None, :])
            self.dma(g, self.GP, out=self.b2r_f[:, :], in_=self.b2[None, :])
            self.dma(g, self.GP, out=self.fbias_b[:, 0:512], in_=self.br[None, :])
            self.dma(g, self.GP, out=self.fbias_b[:, 512:1024], in_=self.bi[None, :])
        self.on("gpsimd", gp)

        def dv(v):
            v.memset(self.ones_r[:, :], 1.0)
            v.memset(self.ones_c[:, :], 1.0)
            v.memset(self.epsc[:, :], 1e-5)
            v.memset(self.epsT[:, :], 1e-12)
            i = v.memset(self.halfT[:, :], 0.5)
            self.DV.inc(i)
        self.on("vector", dv)

        if lvl < 2:
            return
        for wi, (src, dst) in enumerate([(self.tk, self.tk_h), (self.tv, self.tv_h),
                                         (self.tq, self.tq_h)]):
            st_before = self.stream_state(self.tsub)

            def gpl(g, src=src, wi=wi, st_before=st_before):
                if wi > 0:
                    for tk_ in st_before:
                        g.wait_ge(tk_[0].h, tk_[1])
                self.dma(g, self.GP, out=self.slotA.ap().rearrange("p (c f) -> p c f", c=8),
                            in_=src.rearrange("(c p) f -> p c f", p=128))
            self.on("gpsimd", gpl)
            gp_now = self.GP.n

            def scl(s, gp_now=gp_now):
                s.wait_ge(self.GP.h, gp_now)
                for hc in range(8):
                    self.dma_s(s, self.tsub,
                        out=self.slotB.ap().rearrange("p (dc m) -> p dc m", dc=8)[:, :, hc * 128:(hc + 1) * 128],
                        in_=self.slotA[:, hc * D:(hc + 1) * D], transpose=True)
            self.on("scalar", scl)
            st_t = self.stream_state(self.tsub)

            def syl(s, dst=dst, st_t=st_t):
                for tk_ in st_t:
                    s.wait_ge(tk_[0].h, tk_[1])
                self.dma(s, self.SY, out=dst, in_=self.slotB[:, :])
            self.on("sync", syl)
            sy_now = self.SY.n

            def scw(s, sy_now=sy_now):
                s.wait_ge(self.SY.h, sy_now)
            self.on("scalar", scw)

        self._stage_gates = ([(self.SY, self.SY.n)]
                             + self.stream_state(self.tsub))
        if lvl < 3:
            return
        for part, wdst in enumerate([self.wr_h, self.wi_h]):
            if part == 0:
                def gpl0(g):
                    self.dma(g, self.GP, out=self.g1sb.rearrange("p (b d) -> p b d", b=32),
                             in_=self.G1R.rearrange("a b c d -> (a c) b d"))
                    self.dma(g, self.GP, out=self.g2sb.rearrange("p (e g) -> p e g", e=16),
                             in_=self.G2R.rearrange("c e f g -> (c f) e g"))
                self.on("gpsimd", gpl0)
            if part == 1:
                sy_prev = self.SY.n

                st_p = self.stream_state(self.sg) + self.stream_state(self.sout)

                def gpl2(g, st_p=st_p):
                    for tk_ in st_p:
                        g.wait_ge(tk_[0].h, tk_[1])
                    self.dma(g, self.GP, out=self.g1sb.rearrange("p (b d) -> p b d", b=32),
                                in_=self.G1I.rearrange("a b c d -> (a c) b d"))
                    self.dma(g, self.GP, out=self.g2sb.rearrange("p (e g) -> p e g", e=16),
                                in_=self.G2I.rearrange("c e f g -> (c f) e g"))
                self.on("gpsimd", gpl2)
            gp_now = self.GP.n
            with nc.psum_tensor("ps_init%d" % part, [128, 1024], F32) as ps_init:
                ac_base = self.AC.n
                pe_base = self.PE.n

                def pe(t, gp_now=gp_now, ac_base=ac_base, ps_init=ps_init):
                    t.wait_ge(self.GP.h, gp_now)
                    for m in range(8):
                        if m >= 2:
                            t.wait_ge(self.AC.h, ac_base + m - 1)
                        i = t.matmul(ps_init[:, (m % 2) * 512:(m % 2) * 512 + 256],
                                     self.g1sb[:, m * 128:(m + 1) * 128], self.g2sb[:, :],
                                     start=True, stop=True)
                        self.PE.inc(i)
                self.on("tensor", pe)

                def ac(s, pe_base=pe_base, ps_init=ps_init):
                    for m in range(8):
                        s.wait_ge(self.PE.h, pe_base + m + 1)
                        i = s.copy(self.p1sb[:, m * 256:(m + 1) * 256],
                                   ps_init[:, (m % 2) * 512:(m % 2) * 512 + 256])
                        self.AC.inc(i)
                self.on("scalar", ac)
            ac_now = self.AC.n

            def syp(s, wdst=wdst, ac_now=ac_now):
                import os
                s.wait_ge(self.AC.h, ac_now)
                if os.environ.get("ATH_NOSYP"):
                    return
                wt4 = wdst.rearrange("(b e) (d g) -> b d e g", e=16, g=16)
                k = 0
                for m in range(8):
                    for bl in range(4):
                        b = m * 4 + bl
                        self.dma_s(s, (self.sg, self.sout)[k % 2],
                            out=wt4[b, :, :, :],
                            in_=self.p1sb[bl * 32:(bl + 1) * 32, m * 256:(m + 1) * 256]
                            .rearrange("p (e g) -> p e g", e=16))
                        k += 1
            self.on("sync", syp)

        if lvl < 4:
            return
        st_p2 = self.stream_state(self.sg) + self.stream_state(self.sout)

        def sym(s, st_p2=st_p2):
            for tk_ in st_p2:
                s.wait_ge(tk_[0].h, tk_[1])
            mtb = self.q_rm if self.IT >= 8 else self.slotB
            v = mtb.ap().rearrange("p (kc c) -> p kc c", kc=8)
            for (w, kgrp, half) in ((self.wr_h, 0, 0), (self.wi_h, 1, 0),
                                    (self.wi_h, 0, 1), (self.wr_h, 1, 1)):
                for r in range(4):
                    self.dma_s(s, self.tsub,
                        out=v[:, kgrp * 4:(kgrp + 1) * 4,
                              half * 512 + r * 128: half * 512 + (r + 1) * 128],
                        in_=w[r * 128:(r + 1) * 128, :], transpose=True)
        self.on("scalar", sym)
        st_mt = self.stream_state(self.tsub)

        def dvn(v, st_mt=st_mt):
            for tk_ in st_mt:
                v.wait_ge(tk_[0].h, tk_[1])
            mtb = self.q_rm if self.IT >= 8 else self.slotB
            for kc in range(4, 8):
                i = v.tensor_scalar_mul(mtb[:, kc * D:kc * D + 512],
                                        mtb[:, kc * D:kc * D + 512], -1.0)
                self.DV.inc(i)
        self.on("vector", dvn)
        dv_now = self.DV.n

        def sym2(s, dv_now=dv_now):
            s.wait_ge(self.DV.h, dv_now)
            self.dma(s, self.SY, out=self.mt_h,
                     in_=(self.q_rm if self.IT >= 8 else self.slotB)[:, :])
        self.on("sync", sym2)
        self._mt_store_gate = (self.SY, self.SY.n)
        if self.IT < 8:
            self._stage_gates = ([(self.SY, self.SY.n)]
                                 + self.stream_state(self.tsub)
                                 + self.stream_state(self.sg)
                                 + self.stream_state(self.sout))

        if lvl < 5:
            return
        def dvd(v):
            v.wait_ge(self.GP.h, 16 * 4)
            self.emit_derive_dv(v)
        self.on("vector", dvd)
        dv_now = self.DV.n

        def scd(s, dv_now=dv_now):
            s.wait_ge(self.DV.h, dv_now)
            self.emit_derive_sc(s)
        self.on("scalar", scd)

    def emit_derive_dv(self, v):
        i = v.tensor_copy(self.W1n_b[:, :], self.W1n_f[:, :]); self.DV.inc(i)
        i = v.tensor_copy(self.W2n_b[:, :], self.W2n_f[:, :]); self.DV.inc(i)
        i = v.tensor_copy(self.b1r_b[:, :], self.b1r_f[:, :]); self.DV.inc(i)
        i = v.tensor_copy(self.b2r_b[:, :], self.b2r_f[:, :]); self.DV.inc(i)

    def emit_derive_sc(self, s, init=False, part="all"):
        dma = (lambda *a, **k: self.dma(s, self.SC, *a, **k)) if init else \
              (lambda *a, **k: self.dma_s(s, self.tder, *a, **k))
        if part in ("all", "w1"):
            w1toks = []
            for mc in range(2):
                w1toks.append(dma(
                    out=self.W1T_b.ap().rearrange("p (hc m) -> p hc m", hc=8)[:, :, mc * 128:(mc + 1) * 128],
                    in_=self.W1n_b[:, mc * D:(mc + 1) * D], transpose=True))
            if not init:
                self._w1toks = w1toks
        if part in ("all", "w2"):
            w2toks = []
            for jc in range(8):
                w2toks.append(dma(
                    out=self.W2T_b.ap().rearrange("p (m2c m) -> p m2c m", m2c=2)[:, :, jc * 128:(jc + 1) * 128],
                    in_=self.W2n_b[:, jc * 256:(jc + 1) * 256], transpose=True))
            if not init:
                self._w2toks = w2toks

    def emit_A(self, t):
        nc = self.nc
        R, IT, NS, ITS = self.R, self.IT, self.NS, self.ITS

        pe_slotA = self.pe_qdone
        pe_slotB = self.pe_ffndone

        toks = []

        gates = list(getattr(self, "_init_gates", []) or [])
        self._init_gates = None

        def syA(s, pe_slotA=pe_slotA, pe_slotB=pe_slotB, gates=gates):
            s.wait_ge(self.PE.h, pe_slotA)
            if pe_slotB:
                s.wait_ge(self.PE.h, pe_slotB)
            for sem, v in gates:
                s.wait_ge(sem.h, v)
            toks.append(self.dma_s(s, self.sslot, out=self.slotA[:, :], in_=self.tk_h))
            toks.append(self.dma_s(s, self.sslot, out=self.slotB[:, :], in_=self.tv_h))
        self.on("sync", syA)
        slot_toks = list(toks)

        with nc.psum_tensor("zps0_%d" % t, [128, D], F32) as zps0, \
             nc.psum_tensor("zps1_%d" % t, [128, D], F32) as zps1, \
             nc.psum_tensor("yps0_%d" % t, [128, D], F32) as yps0, \
             nc.psum_tensor("yps1_%d" % t, [128, D], F32) as yps1:
            zps = [zps0, zps1]
            yps = [yps0, yps1]
            a_ac0 = self.AC.n
            evacs = {}
            for it in range(IT):
                self.emit_ln_x(t, it, self.lnxr, it % 2)

                dv_lnc = self.DV.n
                ttok = []

                def sct(s, it=it, dv_lnc=dv_lnc):
                    s.wait_ge(self.DV.h, dv_lnc)
                    ttok.append(self.dma_s(s, self.tlnx,
                             out=self.lnx_fm.ap().rearrange("p (c m) -> p c m", m=128)[:, it * 8:(it + 1) * 8, :],
                             in_=self.lnxr[:, (it % 2) * D:(it % 2 + 1) * D], transpose=True))
                self.on("scalar", sct)
                self.lngate[it % 2] = ttok[0]
                lnx_tok = ttok[0]

                prev_evac = evacs.get(it - 2)

                def pezy(tt, it=it, lnx_tok=lnx_tok, prev_evac=prev_evac,
                         slot_toks=slot_toks, a_ac0=a_ac0):
                    self.wtok(tt, lnx_tok)
                    for tk_ in slot_toks:
                        self.wtok(tt, tk_)
                    if prev_evac is not None:
                        tt.wait_ge(self.AC.h, prev_evac[0])
                        tt.wait_ge(self.DV.h, prev_evac[1])
                    else:
                        tt.wait_ge(self.AC.h, a_ac0)
                    for dst, w in ((zps[it % 2], self.slotA), (yps[it % 2], self.slotB)):
                        for h in range(2):
                            for dc in range(8):
                                i = tt.matmul(dst[:, h * 512:(h + 1) * 512],
                                              self.lnx_fm[:, (it * 8 + dc) * 128:(it * 8 + dc + 1) * 128],
                                              w[:, dc * D + h * 512: dc * D + (h + 1) * 512],
                                              start=(dc == 0), stop=(dc == 7))
                            self.PE.inc(i)
                self.on("tensor", pezy)
                pe_n = self.PE.n

                def acz(s, it=it, pe_n=pe_n):
                    s.wait_ge(self.PE.h, pe_n - 2)
                    i = s.copy(self.z_rm[:, it * D:(it + 1) * D], zps[it % 2][:, :])
                    self.AC.inc(i)
                self.on("scalar", acz)
                ac_z = self.AC.n

                def dvy(v, it=it, pe_n=pe_n, ac_z=ac_z):
                    v.wait_ge(self.PE.h, pe_n)
                    v.wait_ge(self.AC.h, ac_z)
                    i = v.tensor_tensor(self.y_rm[:, it * D:(it + 1) * D],
                                        self.z_rm[:, it * D:(it + 1) * D],
                                        yps[it % 2][:, :], ALU.subtract)
                    self.DV.inc(i)
                self.on("vector", dvy)
                evacs[it] = (self.AC.n, self.DV.n)
        self.pe_Adone = self.PE.n

    def emit_q(self, t):
        nc = self.nc
        R, IT = self.R, self.IT
        pe_zy = getattr(self, "pe_Adone", self.PE.n)

        qtk = []

        mtg = getattr(self, "_mt_gate_once", None)
        self._mt_gate_once = None

        def syq(s, pe_zy=pe_zy, mtg=mtg):
            if mtg is not None:
                s.wait_ge(mtg[0].h, mtg[1])
            s.wait_ge(self.PE.h, pe_zy)
            qtk.append(self.dma_s(s, self.sslot, out=self.slotA[:, :], in_=self.tq_h))
        self.on("sync", syq)
        slot_tok = qtk[0]
        lnx_state = self.stream_state(self.tlnx)
        a_ac = self.AC.n
        a_dv = self.DV.n

        with nc.psum_tensor("qps0_%d" % t, [128, D], F32) as qps0, \
             nc.psum_tensor("qps1_%d" % t, [128, D], F32) as qps1:
            qps = [qps0, qps1]
            qevac = {}
            for it in range(IT):
                prev = qevac.get(it - 2)

                def peq(tt, it=it, prev=prev, slot_tok=slot_tok, a_ac=a_ac, a_dv=a_dv,
                        lnx_state=lnx_state):
                    self.wtok(tt, slot_tok)
                    for tk_ in lnx_state:
                        tt.wait_ge(tk_[0].h, tk_[1])
                    if prev is not None:
                        tt.wait_ge(self.AC.h, prev)
                    else:
                        tt.wait_ge(self.AC.h, a_ac)
                        tt.wait_ge(self.DV.h, a_dv)
                    for h in range(2):
                        for dc in range(8):
                            i = tt.matmul(qps[it % 2][:, h * 512:(h + 1) * 512],
                                          self.lnx_fm[:, (it * 8 + dc) * 128:(it * 8 + dc + 1) * 128],
                                          self.slotA[:, dc * D + h * 512: dc * D + (h + 1) * 512],
                                          start=(dc == 0), stop=(dc == 7))
                        self.PE.inc(i)
                self.on("tensor", peq)
                pe_n = self.PE.n

                def acq(s, it=it, pe_n=pe_n):
                    s.wait_ge(self.PE.h, pe_n)
                    i = s.copy(self.q_rm[:, it * D:(it + 1) * D], qps[it % 2][:, :])
                    self.AC.inc(i)
                self.on("scalar", acq)
                qevac[it] = self.AC.n
        self.pe_qdone = self.PE.n

    def emit_D(self, t):
        nc = self.nc
        pe_b = max(getattr(self, "pe_Adone", 0), self.pe_ffndone)
        if pe_b == 0:
            pe_b = self.PE.n

        mtk = []

        def syd(s, pe_b=pe_b):
            s.wait_ge(self.PE.h, pe_b)
            mtk.append(self.dma_s(s, self.sslot, out=self.slotB[:, :], in_=self.mt_h))
        self.on("sync", syd)
        self.mt_tok = mtk[0]
        for s_i in range(self.NS):
            self.recon_subtile(t, s_i)
        self.pe_ffndone = self.PE.n

    def emit_ln_x(self, t, it, dstbuf, slot):
        R = self.R
        gate = self.xgate[self.xuse % 2]
        xslot = self.xuse % 2
        self.xuse += 1

        xt = []

        def syx(s, t=t, it=it, gate=gate, xslot=xslot):
            if gate:
                s.wait_ge(self.DV.h, gate)
            xt.append(self.dma_s(s, self.sx, out=self.x01[:, xslot * D:(xslot + 1) * D],
                      in_=self.x[t * R + it * 128: t * R + (it + 1) * 128, :]))
        self.on("sync", syx)
        x_tok = xt[0]
        lngate = self.lngate[slot]

        def dvln(v, x_tok=x_tok, xslot=xslot):
            self.wtok(v, x_tok)
            sl = self.x01[:, xslot * D:(xslot + 1) * D]
            v.bn_stats(self.st6[:, 0:6], sl[:, 0:512])
            i = v.bn_stats(self.st6[:, 6:12], sl[:, 512:1024]); self.DV.inc(i)
            v.wait_ge(self.DV.h, self.DV.n)
            i = v.bn_aggr(self.mv[:, :], self.st6[:, :]); self.DV.inc(i)
        self.on("vector", dvln)
        dv_n = self.DV.n

        def acln(s, dv_n=dv_n):
            s.wait_ge(self.DV.h, dv_n)
            i = s.activation(self.rstd[:, :], self.mv[:, 1:2], AF.Sqrt,
                             bias=self.epsc[:, 0:1], scale=1.0)
            self.AC.inc(i)
        self.on("scalar", acln)
        ac_n = self.AC.n

        def dvln2(v, ac_n=ac_n, xslot=xslot, slot=slot, lngate=lngate):
            v.wait_ge(self.AC.h, ac_n)
            self.wtok(v, lngate)
            i = v.reciprocal(self.rstd[:, :], self.rstd[:, :]); self.DV.inc(i)
            v.wait_ge(self.DV.h, self.DV.n)
            sl = self.x01[:, xslot * D:(xslot + 1) * D]
            i = v.tensor_scalar(dstbuf[:, slot * D:(slot + 1) * D],
                                sl, self.mv[:, 0:1], self.rstd[:, 0:1],
                                ALU.subtract, ALU.mult)
            self.DV.inc(i)
        self.on("vector", dvln2)
        self.xgate[xslot] = self.DV.n

    def chain_subtile(self, t, s_i):
        nc = self.nc
        ITS = self.ITS
        base_it = s_i * ITS
        ac_pre = self.AC.n
        dv_pre = self.DV.n
        pe_pre = self.PE.n

        ztoks = []

        def scz(s, base_it=base_it, ac_pre=ac_pre, pe_pre=pe_pre):
            s.wait_ge(self.AC.h, ac_pre)
            s.wait_ge(self.PE.h, pe_pre)
            for il in range(ITS):
                it = base_it + il
                ztoks.append(self.dma_s(s, self.tsub,
                    out=self.zfm.ap().rearrange("p (c m) -> p c m", m=128)[:, il * 8:(il + 1) * 8, :],
                    in_=self.z_rm[:, it * D:(it + 1) * D], transpose=True))
        self.on("scalar", scz)
        st_der = self.stream_state(self.tder)

        with nc.psum_tensor("aps0_%d_%d" % (t, s_i), [128, 512], F32) as aps0, \
             nc.psum_tensor("aps1_%d_%d" % (t, s_i), [128, 512], F32) as aps1, \
             nc.psum_tensor("rps0_%d_%d" % (t, s_i), [128, D], F32) as rps0, \
             nc.psum_tensor("rps1_%d_%d" % (t, s_i), [128, D], F32) as rps1, \
             nc.psum_tensor("dps0_%d_%d" % (t, s_i), [128, 512], F32) as dps0, \
             nc.psum_tensor("dps1_%d_%d" % (t, s_i), [128, 512], F32) as dps1:
            aps = [aps0, aps1]
            rps = [rps0, rps1]
            dps = [dps0, dps1]
            h_done = {}
            acw = {}
            htoks = []
            for il in range(ITS):
                prev_ac = acw.get(il - 2, 0)

                def pea(tt, il=il, st_der=st_der, prev_ac=prev_ac,
                        ac_pre=ac_pre, dv_pre=dv_pre):
                    for tk_ in [ztoks[il]] + st_der:
                        tt.wait_ge(tk_[0].h, tk_[1])
                    if prev_ac:
                        tt.wait_ge(self.AC.h, prev_ac)
                    else:
                        tt.wait_ge(self.AC.h, ac_pre)
                        tt.wait_ge(self.DV.h, dv_pre)
                    for hc in range(8):
                        tt.matmul(aps[il % 2][:, 0:256],
                                  self.zfm[:, (il * 8 + hc) * 128:(il * 8 + hc + 1) * 128],
                                  self.W1T_b[:, hc * 256:(hc + 1) * 256],
                                  start=(hc == 0), stop=False)
                    i = tt.matmul(aps[il % 2][:, 0:256], self.ones_r[:, :],
                                  self.b1r_b[:, :], start=False, stop=True)
                    self.PE.inc(i)
                self.on("tensor", pea)
                pe_a = self.PE.n
                dv_gate = h_done.get(il - 1, (0, 0))[1]

                def aca(s, il=il, pe_a=pe_a, dv_gate=dv_gate):
                    s.wait_ge(self.PE.h, pe_a)
                    if dv_gate:
                        s.wait_ge(self.DV.h, dv_gate)
                    ap = aps[il % 2][:, 0:256]
                    sl = slice((il % 2) * 256, (il % 2 + 1) * 256)
                    i = s.activation(self.hrm[:, il * 256:(il + 1) * 256], ap, AF.Gelu)
                    self.AC.inc(i)
                    i = s.copy(self.xa01[:, sl], ap); self.AC.inc(i)
                    i = s.activation(self.sq01[:, sl], ap, AF.Square); self.AC.inc(i)
                self.on("scalar", aca)
                acw[il] = self.AC.n
                ac_hil = self.AC.n

                def schil(s, il=il, ac_hil=ac_hil):
                    s.wait_ge(self.AC.h, ac_hil)
                    htoks.append(self.dma_s(s, self.tsub,
                        out=self.hfm.ap().rearrange("p (c m) -> p c m", m=128)[:, il * 2:(il + 1) * 2, :],
                        in_=self.hrm[:, il * 256:(il + 1) * 256], transpose=True))
                self.on("scalar", schil)

            def acE(s):
                s.wait_ge(self.AC.h, self.AC.n)
                for il in range(ITS):
                    sl = slice((il % 2) * 256, (il % 2 + 1) * 256)
                    i = s.activation(self.ex01[:, sl], self.sq01[:, sl], AF.Exp, scale=-0.5)
                    self.AC.inc(i)
            self.on("scalar", acE)
            ac_a = self.AC.n
            for il in range(ITS):
                def dvg(v, il=il, ac_a=ac_a):
                    v.wait_ge(self.AC.h, ac_a)
                    v.wait_ge(self.DV.h, self.DV.n)
                    sl = slice((il % 2) * 256, (il % 2 + 1) * 256)
                    i = v.tensor_single_scalar(self.ms01[:, :], self.sq01[:, sl], 1e-24, ALU.is_lt)
                    self.DV.inc(i)
                    i = v.tensor_copy(self.xs01[:, :], self.xa01[:, sl]); self.DV.inc(i)
                    v.wait_ge(self.DV.h, self.DV.n)
                    i = v.copy_predicated(self.xs01[:, :], self.ms01[:, :], self.epsT[:, :])
                    self.DV.inc(i)
                    v.wait_ge(self.DV.h, self.DV.n)
                    i = v.reciprocal(self.xs01[:, :], self.xs01[:, :]); self.DV.inc(i)
                    v.wait_ge(self.DV.h, self.DV.n)
                    i = v.tensor_tensor(self.ph01[:, :], self.hrm[:, il * 256:(il + 1) * 256],
                                        self.xs01[:, :], ALU.mult)
                    self.DV.inc(i)
                    v.wait_ge(self.DV.h, self.DV.n)
                    i = v.copy_predicated(self.ph01[:, :], self.ms01[:, :], self.halfT[:, :])
                    self.DV.inc(i)
                    i = v.tensor_tensor(self.xs01[:, :], self.xa01[:, sl], self.ex01[:, sl], ALU.mult)
                    self.DV.inc(i)
                    v.wait_ge(self.DV.h, self.DV.n)
                    i = v.scalar_tensor_tensor(self.gpb[:, il * 256:(il + 1) * 256],
                                               self.xs01[:, :], GELU_C, self.ph01[:, :],
                                               ALU.mult, ALU.add)
                    self.DV.inc(i)
                self.on("vector", dvg)
                h_done[il] = (self.AC.n, self.DV.n)



            edv = {}
            etoks = []
            for il in range(ITS):
                it = base_it + il
                prev_dv = edv.get(il - 2, 0)

                def per(tt, il=il, prev_dv=prev_dv):
                    self.wtok(tt, htoks[il])
                    if prev_dv:
                        tt.wait_ge(self.DV.h, prev_dv)
                    for h in range(2):
                        for m2c in range(2):
                            tt.matmul(rps[il % 2][:, h * 512:(h + 1) * 512],
                                      self.hfm[:, (il * 2 + m2c) * 128:(il * 2 + m2c + 1) * 128],
                                      self.W2T_b[:, m2c * D + h * 512:m2c * D + (h + 1) * 512],
                                      start=(m2c == 0), stop=False)
                        i = tt.matmul(rps[il % 2][:, h * 512:(h + 1) * 512],
                                      self.ones_r[:, :], self.b2r_b[:, h * 512:(h + 1) * 512],
                                      start=False, stop=True)
                        self.PE.inc(i)
                self.on("tensor", per)
                pe_r = self.PE.n

                def dve(v, il=il, it=it, pe_r=pe_r):
                    v.wait_ge(self.PE.h, pe_r)
                    i = v.tensor_tensor(self.erm[:, il * D:(il + 1) * D], rps[il % 2][:, :],
                                        self.y_rm[:, it * D:(it + 1) * D], ALU.add)
                    self.DV.inc(i)
                self.on("vector", dve)
                edv[il] = self.DV.n
                dv_eil = self.DV.n

                def sceil(s, il=il, dv_eil=dv_eil):
                    s.wait_ge(self.DV.h, dv_eil)
                    etoks.append(self.dma_s(s, self.tsub,
                        out=self.efm.ap().rearrange("p (c m) -> p c m", m=128)[:, il * 8:(il + 1) * 8, :],
                        in_=self.erm[:, il * D:(il + 1) * D], transpose=True))
                self.on("scalar", sceil)

            dadv = {}
            for il in range(ITS):
                prev_dv = dadv.get(il - 2, 0)

                def ped(tt, il=il, prev_dv=prev_dv):
                    self.wtok(tt, etoks[il])
                    if prev_dv:
                        tt.wait_ge(self.DV.h, prev_dv)
                    for jc in range(8):
                        i = tt.matmul(dps[il % 2][:, 0:256],
                                      self.efm[:, (il * 8 + jc) * 128:(il * 8 + jc + 1) * 128],
                                      self.W2n_b[:, jc * 256:(jc + 1) * 256],
                                      start=(jc == 0), stop=(jc == 7))
                    self.PE.inc(i)
                self.on("tensor", ped)
                pe_d = self.PE.n

                def dvda(v, il=il, pe_d=pe_d):
                    v.wait_ge(self.PE.h, pe_d)
                    i = v.tensor_tensor(self.dab[:, il * 256:(il + 1) * 256],
                                        dps[il % 2][:, 0:256],
                                        self.gpb[:, il * 256:(il + 1) * 256], ALU.mult)
                    self.DV.inc(i)
                self.on("vector", dvda)
                dadv[il] = self.DV.n
            loops_dv = self.DV.n
            loops_ac = self.AC.n

        with nc.psum_tensor("gw2ps_%d_%d" % (t, s_i), [128, 8 * 256], F32) as gw2ps, \
             nc.psum_tensor("gw1ps_%d_%d" % (t, s_i), [128, 2 * D], F32) as gw1ps:

            def peg(tt, base_it=base_it, loops_dv=loops_dv, loops_ac=loops_ac,
                    gw2ps=gw2ps, gw1ps=gw1ps):
                tt.wait_ge(self.DV.h, loops_dv)
                tt.wait_ge(self.AC.h, loops_ac)
                for jc in range(8):
                    for il in range(ITS):
                        i = tt.matmul(gw2ps[:, jc * 256:(jc + 1) * 256],
                                      self.erm[:, il * D + jc * 128: il * D + (jc + 1) * 128],
                                      self.hrm[:, il * 256:(il + 1) * 256],
                                      start=(il == 0), stop=(il == ITS - 1))
                    self.PE.inc(i)
                for m1c in range(2):
                    for h in range(2):
                        for il in range(ITS):
                            it = base_it + il
                            i = tt.matmul(gw1ps[:, m1c * D + h * 512: m1c * D + (h + 1) * 512],
                                          self.dab[:, il * 256 + m1c * 128: il * 256 + (m1c + 1) * 128],
                                          self.z_rm[:, it * D + h * 512: it * D + (h + 1) * 512],
                                          start=(il == 0), stop=(il == ITS - 1))
                        self.PE.inc(i)
            self.on("tensor", peg)
            pe_g = self.PE.n

            def dvgw(v, s_i=s_i, pe_g=pe_g, gw2ps=gw2ps, gw1ps=gw1ps):
                v.wait_ge(self.PE.h, pe_g)
                if s_i == 0:
                    i = v.tensor_copy(self.gW2a[:, :], gw2ps[:, :]); self.DV.inc(i)
                    i = v.tensor_copy(self.gW1a[:, :], gw1ps[:, :]); self.DV.inc(i)
                else:
                    i = v.tensor_tensor(self.gW2a[:, :], self.gW2a[:, :], gw2ps[:, :], ALU.add)
                    self.DV.inc(i)
                    i = v.tensor_tensor(self.gW1a[:, :], self.gW1a[:, :], gw1ps[:, :], ALU.add)
                    self.DV.inc(i)
            self.on("vector", dvgw)
            gw_dv = self.DV.n

        with nc.psum_tensor("b2ps_%d_%d" % (t, s_i), [1, D], F32) as b2ps, \
             nc.psum_tensor("b1ps_%d_%d" % (t, s_i), [1, M1], F32) as b1ps:

            def peb(tt, gw_dv=gw_dv, b2ps=b2ps, b1ps=b1ps):
                tt.wait_ge(self.DV.h, gw_dv)
                for h in range(2):
                    for il in range(ITS):
                        i = tt.matmul(b2ps[:, h * 512:(h + 1) * 512], self.ones_c[:, :],
                                      self.erm[:, il * D + h * 512: il * D + (h + 1) * 512],
                                      start=(il == 0), stop=(il == ITS - 1))
                    self.PE.inc(i)
                for il in range(ITS):
                    i = tt.matmul(b1ps[:, :], self.ones_c[:, :],
                                  self.dab[:, il * 256:(il + 1) * 256],
                                  start=(il == 0), stop=(il == ITS - 1))
                self.PE.inc(i)
            self.on("tensor", peb)
            pe_bb = self.PE.n

            def dvb(v, s_i=s_i, pe_bb=pe_bb, b2ps=b2ps, b1ps=b1ps):
                v.wait_ge(self.PE.h, pe_bb)
                if s_i == 0:
                    i = v.tensor_copy(self.b2ar[:, :], b2ps[:, :]); self.DV.inc(i)
                    i = v.tensor_copy(self.b1ar[:, :], b1ps[:, :]); self.DV.inc(i)
                else:
                    i = v.tensor_tensor(self.b2ar[:, :], self.b2ar[:, :], b2ps[:, :], ALU.add)
                    self.DV.inc(i)
                    i = v.tensor_tensor(self.b1ar[:, :], self.b1ar[:, :], b1ps[:, :], ALU.add)
                    self.DV.inc(i)
            self.on("vector", dvb)

    def ar_start(self, t):
        dv_g = self.DV.n
        g2d = self.g_in[0:128 * 4096].rearrange("(p c) -> p c", c=2048)

        def sya(s, dv_g=dv_g):
            s.wait_ge(self.DV.h, dv_g)
            self.dma_s(s, self.sg, out=g2d[0:128, :], in_=self.gW1a[:, :])
            self.dma_s(s, self.sg, out=g2d[128:256, :], in_=self.gW2a[:, :])
            self.dma_s(s, self.sg, out=self.g_in[128 * 4096:128 * 4096 + 1024][None, :],
                       in_=self.b2ar[:, :])
            self.dma_s(s, self.sg, out=self.g_in[128 * 4096 + 1024:128 * 4096 + 1280][None, :],
                       in_=self.b1ar[:, :])
        self.on("sync", sya)
        st_g = self.stream_state(self.sg)

        def gpc(g, st_g=st_g):
            for tk_ in st_g:
                g.wait_ge(tk_[0].h, tk_[1])
            i = g.collective_compute("AllReduce", ALU.add,
                                     replica_groups=[list(range(NCORES))],
                                     ins=[self.g_in], outs=[self.g_out])
            self.CC.inc(i)
        self.on("gpsimd", gpc)

    def ar_finish(self, t):
        g2o = self.g_out[0:128 * 4096].rearrange("(p c) -> p c", c=2048)
        cc_n = self.CC.n
        rtoks = []

        def syb(g, cc_n=cc_n):
            g.wait_ge(self.CC.h, cc_n)
            for tk_ in self.stream_state(self.sg):
                g.wait_ge(tk_[0].h, tk_[1])
            rtoks.append(self.dma_s(g, self.sgr, out=self.gW1a[:, :], in_=g2o[0:128, :]))
            rtoks.append(self.dma_s(g, self.sgr, out=self.gW2a[:, :], in_=g2o[128:256, :]))
            rtoks.append(self.dma_s(g, self.sgr, out=self.b2ar[:, :],
                         in_=self.g_out[128 * 4096:128 * 4096 + 1024][None, :]))
            rtoks.append(self.dma_s(g, self.sgr, out=self.b1ar[:, :],
                         in_=self.g_out[128 * 4096 + 1024:128 * 4096 + 1280][None, :]))
        self.on("gpsimd", syb)
        c = self.upd_c

        st_ln = self.stream_state(self.tlnx)

        def dvu1(v, c=c, st_ln=st_ln, rtoks=rtoks):
            for tk_ in rtoks:
                v.wait_ge(tk_[0].h, tk_[1])
            for tk_ in st_ln:
                v.wait_ge(tk_[0].h, tk_[1])
            i = v.scalar_tensor_tensor(self.W1n_f[:, :], self.gW1a[:, :], -c,
                                       self.W1n_f[:, :], ALU.mult, ALU.add)
            self.DV.inc(i)
            i = v.scalar_tensor_tensor(self.b1r_f[:, :], self.b1ar[:, :], -c,
                                       self.b1r_f[:, :], ALU.mult, ALU.add)
            self.DV.inc(i)
            i = v.scalar_tensor_tensor(self.b2r_f[:, :], self.b2ar[:, :], -c,
                                       self.b2r_f[:, :], ALU.mult, ALU.add)
            self.DV.inc(i)
            v.wait_ge(self.DV.h, self.DV.n)
            i = v.tensor_copy(self.W1n_b[:, :], self.W1n_f[:, :]); self.DV.inc(i)
            i = v.tensor_copy(self.b1r_b[:, :], self.b1r_f[:, :]); self.DV.inc(i)
            i = v.tensor_copy(self.b2r_b[:, :], self.b2r_f[:, :]); self.DV.inc(i)
        self.on("vector", dvu1)
        dv_u1 = self.DV.n

        def scu1(s, dv_u1=dv_u1):
            s.wait_ge(self.DV.h, dv_u1)
            self.emit_derive_sc(s, part="w1")
        self.on("scalar", scu1)

        def dvu2(v, c=c):
            i = v.scalar_tensor_tensor(self.W2n_f[:, :], self.gW2a[:, :], -c,
                                       self.W2n_f[:, :], ALU.mult, ALU.add)
            self.DV.inc(i)
            v.wait_ge(self.DV.h, self.DV.n)
            i = v.tensor_copy(self.W2n_b[:, :], self.W2n_f[:, :]); self.DV.inc(i)
        self.on("vector", dvu2)
        dv_u2 = self.DV.n

        def scu2(s, dv_u2=dv_u2):
            s.wait_ge(self.DV.h, dv_u2)
            self.emit_derive_sc(s, part="w2")
        self.on("scalar", scu2)
        self.lngate = [self._w1toks[0], self._w1toks[1]]

    def recon_subtile(self, t, s_i):
        nc = self.nc
        R, ITS = self.R, self.ITS
        base_it = s_i * ITS
        st_der = list(getattr(self, "_w1toks", []))
        st_der2 = list(getattr(self, "_w2toks", []))
        dv_pre = self.DV.n
        st_out_pre = self.stream_state(self.sout)

        qtoks = []

        def scq(s, base_it=base_it):
            for il in range(ITS):
                it = base_it + il
                qtoks.append(self.dma_s(s, self.tsub,
                    out=self.zfm.ap().rearrange("p (c m) -> p c m", m=128)[:, il * 8:(il + 1) * 8, :],
                    in_=self.q_rm[:, it * D:(it + 1) * D], transpose=True))
        self.on("scalar", scq)

        h2toks = []
        with nc.psum_tensor("a2ps0_%d_%d" % (t, s_i), [128, 512], F32) as a2ps0, \
             nc.psum_tensor("a2ps1_%d_%d" % (t, s_i), [128, 512], F32) as a2ps1:
            a2ps = [a2ps0, a2ps1]
            h2ac = {}
            for il in range(ITS):
                prev = h2ac.get(il - 2, 0)

                def pea2(tt, il=il, st_der=st_der, prev=prev, dv_pre=dv_pre,
                         a2ps=a2ps):
                    for tk_ in [qtoks[il]] + st_der:
                        tt.wait_ge(tk_[0].h, tk_[1])
                    if prev:
                        tt.wait_ge(self.AC.h, prev)
                    else:
                        tt.wait_ge(self.DV.h, dv_pre)
                    for hc in range(8):
                        tt.matmul(a2ps[il % 2][:, 0:256],
                                  self.zfm[:, (il * 8 + hc) * 128:(il * 8 + hc + 1) * 128],
                                  self.W1T_b[:, hc * 256:(hc + 1) * 256],
                                  start=(hc == 0), stop=False)
                    i = tt.matmul(a2ps[il % 2][:, 0:256], self.ones_r[:, :],
                                  self.b1r_b[:, :], start=False, stop=True)
                    self.PE.inc(i)
                self.on("tensor", pea2)
                pe_a2 = self.PE.n

                def ach2(s, il=il, pe_a2=pe_a2, a2ps=a2ps):
                    s.wait_ge(self.PE.h, pe_a2)
                    i = s.activation(self.hrm[:, il * 256:(il + 1) * 256],
                                     a2ps[il % 2][:, 0:256], AF.Gelu)
                    self.AC.inc(i)
                self.on("scalar", ach2)
                h2ac[il] = self.AC.n
                ac_h2il = self.AC.n

                def sch2il(s, il=il, ac_h2il=ac_h2il):
                    s.wait_ge(self.AC.h, ac_h2il)
                    h2toks.append(self.dma_s(s, self.tsub,
                        out=self.hfm.ap().rearrange("p (c m) -> p c m", m=128)[:, il * 2:(il + 1) * 2, :],
                        in_=self.hrm[:, il * 256:(il + 1) * 256], transpose=True))
                self.on("scalar", sch2il)

        ac_h2all = self.AC.n
        with nc.psum_tensor("ops0_%d_%d" % (t, s_i), [128, D], F32) as ops0, \
             nc.psum_tensor("ops1_%d_%d" % (t, s_i), [128, D], F32) as ops1, \
             nc.psum_tensor("fps0_%d_%d" % (t, s_i), [128, D], F32) as fps0, \
             nc.psum_tensor("fps1_%d_%d" % (t, s_i), [128, D], F32) as fps1:
            ops = [ops0, ops1]
            fps = [fps0, fps1]
            xr_toks = []
            xslots = []
            for il in range(ITS):
                it = base_it + il
                gate = self.xgate[self.xuse % 2]
                xslot = self.xuse % 2
                self.xuse += 1
                xslots.append(xslot)
                xrt = []

                def syxr(s, t=t, it=it, gate=gate, xslot=xslot):
                    if gate:
                        s.wait_ge(self.DV.h, gate)
                    xrt.append(self.dma_s(s, self.sx, out=self.x01[:, xslot * D:(xslot + 1) * D],
                               in_=self.x[t * R + it * 128: t * R + (it + 1) * 128, :]))
                self.on("sync", syxr)
                xr_toks.append(xrt[0])
            ln2dv = {}
            l2toks = []
            for il in range(ITS):
                it = base_it + il
                xr_tok = xr_toks[il]
                xslot = xslots[il]
                prev = ln2dv.get(il - 2, 0)

                def peo(tt, il=il, prev=prev, ops=ops, st_der2=st_der2,
                        ac_h2all=ac_h2all):
                    for tk_ in [h2toks[il]] + st_der2:
                        tt.wait_ge(tk_[0].h, tk_[1])
                    if prev:
                        tt.wait_ge(self.DV.h, prev)
                    else:
                        tt.wait_ge(self.AC.h, ac_h2all)
                    for h in range(2):
                        for m2c in range(2):
                            tt.matmul(ops[il % 2][:, h * 512:(h + 1) * 512],
                                      self.hfm[:, (il * 2 + m2c) * 128:(il * 2 + m2c + 1) * 128],
                                      self.W2T_b[:, m2c * D + h * 512:m2c * D + (h + 1) * 512],
                                      start=(m2c == 0), stop=False)
                        i = tt.matmul(ops[il % 2][:, h * 512:(h + 1) * 512],
                                      self.ones_r[:, :], self.b2r_b[:, h * 512:(h + 1) * 512],
                                      start=False, stop=True)
                        self.PE.inc(i)
                self.on("tensor", peo)
                pe_o = self.PE.n

                def dvx2(v, il=il, it=it, pe_o=pe_o, xr_tok=xr_tok, xslot=xslot,
                         st_out_pre=st_out_pre, ops=ops):
                    v.wait_ge(self.PE.h, pe_o)
                    for tk_ in st_out_pre:
                        v.wait_ge(tk_[0].h, tk_[1])
                    i = v.tensor_tensor(self.x2s[:, il * D:(il + 1) * D], ops[il % 2][:, :],
                                        self.q_rm[:, it * D:(it + 1) * D], ALU.add)
                    self.DV.inc(i)
                    self.wtok(v, xr_tok)
                    v.wait_ge(self.DV.h, self.DV.n)
                    i = v.tensor_tensor(self.x2s[:, il * D:(il + 1) * D],
                                        self.x2s[:, il * D:(il + 1) * D],
                                        self.x01[:, xslot * D:(xslot + 1) * D], ALU.add)
                    self.DV.inc(i)
                    sl = self.x2s[:, il * D:(il + 1) * D]
                    v.wait_ge(self.DV.h, self.DV.n)
                    v.bn_stats(self.st6[:, 0:6], sl[:, 0:512])
                    i = v.bn_stats(self.st6[:, 6:12], sl[:, 512:1024]); self.DV.inc(i)
                    v.wait_ge(self.DV.h, self.DV.n)
                    i = v.bn_aggr(self.mv[:, :], self.st6[:, :]); self.DV.inc(i)
                self.on("vector", dvx2)
                self.xgate[xslot] = self.DV.n
                dv_st = self.DV.n

                def acl2(s, dv_st=dv_st):
                    s.wait_ge(self.DV.h, dv_st)
                    i = s.activation(self.rstd[:, :], self.mv[:, 1:2], AF.Sqrt,
                                     bias=self.epsc[:, 0:1], scale=1.0)
                    self.AC.inc(i)
                self.on("scalar", acl2)
                ac_l2 = self.AC.n

                def dvl2(v, il=il, ac_l2=ac_l2):
                    v.wait_ge(self.AC.h, ac_l2)
                    i = v.reciprocal(self.rstd[:, :], self.rstd[:, :]); self.DV.inc(i)
                    v.wait_ge(self.DV.h, self.DV.n)
                    i = v.tensor_scalar(self.ln2r[:, il * D:(il + 1) * D],
                                        self.x2s[:, il * D:(il + 1) * D],
                                        self.mv[:, 0:1], self.rstd[:, 0:1],
                                        ALU.subtract, ALU.mult)
                    self.DV.inc(i)
                self.on("vector", dvl2)
                ln2dv[il] = self.DV.n
                dv_l2il = self.DV.n

                def scl2il(s, il=il, dv_l2il=dv_l2il):
                    s.wait_ge(self.DV.h, dv_l2il)
                    l2toks.append(self.dma_s(s, self.tsub,
                        out=self.ln2f.ap().rearrange("p (c m) -> p c m", m=128)[:, il * 8:(il + 1) * 8, :],
                        in_=self.ln2r[:, il * D:(il + 1) * D], transpose=True))
                self.on("scalar", scl2il)

            outdv = {}
            for il in range(ITS):
                it = base_it + il
                prev = outdv.get(il - 2, 0)

                def pef(tt, il=il, prev=prev, mt_tok=self.mt_tok, fps=fps,
                        ac_h2all=ac_h2all):
                    self.wtok(tt, l2toks[il])
                    self.wtok(tt, mt_tok)
                    if prev:
                        tt.wait_ge(self.DV.h, prev)
                    else:
                        tt.wait_ge(self.AC.h, ac_h2all)
                    for h in range(2):
                        for kc in range(8):
                            tt.matmul(fps[il % 2][:, h * 512:(h + 1) * 512],
                                      self.ln2f[:, (il * 8 + kc) * 128:(il * 8 + kc + 1) * 128],
                                      self.slotB[:, kc * D + h * 512: kc * D + (h + 1) * 512],
                                      start=(kc == 0), stop=False)
                        i = tt.matmul(fps[il % 2][:, h * 512:(h + 1) * 512],
                                      self.ones_r[:, :], self.fbias_b[:, h * 512:(h + 1) * 512],
                                      start=False, stop=True)
                        self.PE.inc(i)
                self.on("tensor", pef)
                pe_f = self.PE.n

                def dvo(v, il=il, pe_f=pe_f, fps=fps):
                    v.wait_ge(self.PE.h, pe_f)
                    i = v.tensor_tensor(self.x2s[:, il * D:(il + 1) * D],
                                        fps[il % 2][:, :],
                                        self.x2s[:, il * D:(il + 1) * D], ALU.add)
                    self.DV.inc(i)
                self.on("vector", dvo)
                dv_o = self.DV.n
                outdv[il] = dv_o

                def syo(s, t=t, it=it, il=il, dv_o=dv_o):
                    s.wait_ge(self.DV.h, dv_o)
                    self.dma_s(s, self.sout, out=self.y[t * R + it * 128: t * R + (it + 1) * 128, :],
                               in_=self.x2s[:, il * D:(il + 1) * D])
                self.on("sync", syo)


def build_program(R):
    return Builder(R).build()


def kernel(L0_attn_in_seq, L0_attn_t_k, L0_attn_t_v, L0_attn_t_q, L0_attn_W1,
           L0_attn_b1, L0_attn_W2, L0_attn_b2, L0_ffn_G1_R, L0_ffn_G2_R,
           L0_ffn_G1_I, L0_ffn_G2_I, L0_ffn_bias_real, L0_ffn_bias_imag):
    x = np.asarray(L0_attn_in_seq)
    Tn, N, Dn = x.shape
    R = N // NCORES
    nc = build_program(R)
    shared = {
        "t_k": np.ascontiguousarray(L0_attn_t_k),
        "t_v": np.ascontiguousarray(L0_attn_t_v),
        "t_q": np.ascontiguousarray(L0_attn_t_q),
        "W1": np.ascontiguousarray(L0_attn_W1),
        "b1": np.ascontiguousarray(L0_attn_b1),
        "W2": np.ascontiguousarray(L0_attn_W2),
        "b2": np.ascontiguousarray(L0_attn_b2),
        "G1R": np.ascontiguousarray(L0_ffn_G1_R),
        "G2R": np.ascontiguousarray(L0_ffn_G2_R),
        "G1I": np.ascontiguousarray(L0_ffn_G1_I),
        "G2I": np.ascontiguousarray(L0_ffn_G2_I),
        "br": np.ascontiguousarray(L0_ffn_bias_real),
        "bi": np.ascontiguousarray(L0_ffn_bias_imag),
    }
    in_maps = []
    for c in range(NCORES):
        m = dict(shared)
        m["x"] = np.ascontiguousarray(x[:, c * R:(c + 1) * R, :]).reshape(Tn * R, Dn)
        in_maps.append(m)
    res = run_bass_kernel_spmd(nc, in_maps, core_ids=list(range(NCORES)))
    out = np.empty((Tn, N, Dn), dtype=np.float32)
    for c in range(NCORES):
        out[:, c * R:(c + 1) * R, :] = res.results[c]["y"].reshape(Tn, R, Dn)
    return out


# version 16
# speedup vs baseline: 1.0416x; 1.0234x over previous
import numpy as np
import concourse.bass as bass
import concourse.mybir as mybir
from concourse.bass_utils import run_bass_kernel_spmd

F32 = mybir.dt.float32
BF16 = mybir.dt.bfloat16
U8 = mybir.dt.uint8
AF = mybir.ActivationFunctionType
ALU = mybir.AluOpType

D = 1024
M1 = 256
T = 4
LR = 1e-3
NCORES = 8
GELU_C = float(1.0 / np.sqrt(2.0 * np.pi))


class Sem:
    def __init__(self, nc, name):
        self.h = nc.alloc_semaphore(name=name)
        self.n = 0

    def inc(self, instr, amt=1):
        instr.then_inc(self.h, amt)
        self.n += amt
        return self.n


class Stream:
    def __init__(self, nc, name):
        self.sems = [Sem(nc, name + "_0"), Sem(nc, name + "_1")]
        self.i = 0


class Builder:
    def __init__(self, R):
        self.R = R
        self.IT = R // 128
        self.NS = 4 if self.IT >= 8 else (2 if self.IT >= 2 else 1)
        self.ITS = self.IT // self.NS
        self.upd_c = float(LR * 2.0 / ((NCORES * R) * D))
        nc = bass.Bass("TRN2", target_bir_lowering=False)
        self.nc = nc

        self.x = nc.dram_tensor("x", [T * R, D], F32, kind="ExternalInput").ap()
        self.y = nc.dram_tensor("y", [T * R, D], F32, kind="ExternalOutput").ap()
        self.tk = nc.dram_tensor("t_k", [D, D], F32, kind="ExternalInput").ap()
        self.tv = nc.dram_tensor("t_v", [D, D], F32, kind="ExternalInput").ap()
        self.tq = nc.dram_tensor("t_q", [D, D], F32, kind="ExternalInput").ap()
        self.W1 = nc.dram_tensor("W1", [M1, D], F32, kind="ExternalInput").ap()
        self.b1 = nc.dram_tensor("b1", [M1], F32, kind="ExternalInput").ap()
        self.W2 = nc.dram_tensor("W2", [D, M1], F32, kind="ExternalInput").ap()
        self.b2 = nc.dram_tensor("b2", [D], F32, kind="ExternalInput").ap()
        self.G1R = nc.dram_tensor("G1R", [1, 32, 4, 32], F32, kind="ExternalInput").ap()
        self.G2R = nc.dram_tensor("G2R", [4, 16, 1, 16], F32, kind="ExternalInput").ap()
        self.G1I = nc.dram_tensor("G1I", [1, 32, 4, 32], F32, kind="ExternalInput").ap()
        self.G2I = nc.dram_tensor("G2I", [4, 16, 1, 16], F32, kind="ExternalInput").ap()
        self.br = nc.dram_tensor("br", [512], F32, kind="ExternalInput").ap()
        self.bi = nc.dram_tensor("bi", [512], F32, kind="ExternalInput").ap()
        self.tk_h = nc.dram_tensor("tk_h", [128, 8 * D], BF16).ap()
        self.tv_h = nc.dram_tensor("tv_h", [128, 8 * D], BF16).ap()
        self.tq_h = nc.dram_tensor("tq_h", [128, 8 * D], BF16).ap()
        self.mt_h = nc.dram_tensor("mt_h", [128, 8 * D], BF16).ap()
        self.wr_h = nc.dram_tensor("wr_h", [512, 512], BF16).ap()
        self.wi_h = nc.dram_tensor("wi_h", [512, 512], BF16).ap()
        self.g_in = nc.dram_tensor("g_in", [128 * 4096 + 1024 + 256], F32).ap()
        self.g_out = nc.dram_tensor("g_out", [128 * 4096 + 1024 + 256], F32).ap()

        self.PE = Sem(nc, "sPE")
        self.DV = Sem(nc, "sDV")
        self.AC = Sem(nc, "sAC")
        self.SY = Sem(nc, "sSY")
        self.SC = Sem(nc, "sSC")
        self.GP = Sem(nc, "sGP")
        self.CC = Sem(nc, "sCC")
        self.sx = Stream(nc, "sx")
        self.sout = Stream(nc, "sout")
        self.sslot = Stream(nc, "sslot")
        self.sg = Stream(nc, "sg")
        self.tlnx = Stream(nc, "tlnx")
        self.tsub = Stream(nc, "tsub")
        self.tder = Stream(nc, "tder")
        self.sgr = Stream(nc, "sgr")
        self.xgate = [0, 0]
        self.lngate = [None, None]
        self.xuse = 0
        self.pe_qdone = 0
        self.pe_ffndone = 0

    def on(self, eng, fn):
        getattr(self.block, eng)(fn)

    def dma_s(self, eng, stream, out, in_, **kw):
        sem = stream.sems[stream.i % 2]
        stream.i += 1
        eng.wait_ge(sem.h, sem.n)
        i = eng.dma_start(out=out, in_=in_, **kw)
        sem.inc(i, 16)
        return (sem, sem.n)

    def wtok(self, eng, tok):
        if tok is not None:
            eng.wait_ge(tok[0].h, tok[1])

    def wstream(self, eng, stream):
        for sem in stream.sems:
            if sem.n:
                eng.wait_ge(sem.h, sem.n)

    def stream_state(self, stream):
        return [(sem, sem.n) for sem in stream.sems if sem.n]

    def dma(self, eng, sem, out, in_, **kw):
        eng.wait_ge(sem.h, sem.n)
        i = eng.dma_start(out=out, in_=in_, **kw)
        sem.inc(i, 16)
        return i

    def barrier(self):
        sems = [self.PE, self.DV, self.AC, self.SY, self.SC, self.GP, self.CC]
        for stx in (self.sx, self.sout, self.sslot, self.sg, self.sgr,
                    self.tlnx, self.tsub, self.tder):
            sems += stx.sems
        vals = [(s.h, s.n) for s in sems if s.n > 0]

        def mk(e):
            for h, n in vals:
                e.wait_ge(h, n)
        for eng in ("tensor", "vector", "scalar", "sync", "gpsimd"):
            self.on(eng, mk)

    def build(self):
        nc = self.nc
        import contextlib
        stack = contextlib.ExitStack()
        sb = lambda name, shape, dt: stack.enter_context(nc.sbuf_tensor(name, shape, dt))
        IT, ITS = self.IT, self.ITS
        self.W1n_f = sb("W1n_f", [128, 2 * D], F32)
        self.W2n_f = sb("W2n_f", [128, 8 * M1], F32)
        self.W2n_b = sb("W2n_b", [128, 8 * M1], BF16)
        self.W1T_b = sb("W1T_b", [128, 8 * M1], BF16)
        self.W2T_b = sb("W2T_b", [128, 2 * D], BF16)
        self.b1r_f = sb("b1r_f", [1, M1], F32)
        self.b2r_f = sb("b2r_f", [1, D], F32)
        self.b1r_b = sb("b1r_b", [1, M1], BF16)
        self.b2r_b = sb("b2r_b", [1, D], BF16)
        self.fbias_b = sb("fbias_b", [1, D], BF16)
        self.ones_r = sb("ones_r", [1, 128], BF16)
        self.ones_c = sb("ones_c", [128, 1], BF16)
        self.epsc = sb("epsc", [128, 1], F32)
        self.epsT = sb("epsT", [128, M1], F32)
        self.halfT = sb("halfT", [128, M1], F32)
        self.slotA = sb("slotA", [128, 8 * D], BF16)
        self.slotB = sb("slotB", [128, 8 * D], BF16)
        self.lnx_fm = sb("lnx_fm", [128, IT * D], BF16)
        self.z_rm = sb("z_rm", [128, IT * D], BF16)
        self.y_rm = sb("y_rm", [128, IT * D], BF16)
        self.q_rm = sb("q_rm", [128, IT * D], BF16)
        self.gW1a = sb("gW1a", [128, 2 * D], F32)
        self.gW2a = sb("gW2a", [128, 8 * M1], F32)
        self.b1ar = sb("b1ar", [1, M1], F32)
        self.b2ar = sb("b2ar", [1, D], F32)
        self.st6 = sb("st6", [128, 12], F32)
        self.mv = sb("mv", [128, 2], F32)
        self.rstd = sb("rstd", [128, 1], F32)
        self.x01 = sb("x01", [128, 2 * D], F32)
        self.lnxr = sb("lnxr", [128, 2 * D], BF16)
        self.W1n_b = self.lnxr.ap()
        self.zfm = sb("zfm", [128, ITS * D], BF16)
        self.erm = sb("erm", [128, ITS * D], BF16)
        self.efm = sb("efm", [128, ITS * D], BF16)
        self.hrm = sb("hrm", [128, ITS * M1], BF16)
        self.hfm = sb("hfm", [128, ITS * M1], BF16)
        self.gpb = sb("gpb", [128, ITS * M1], BF16)
        self.dab = sb("dab", [128, ITS * M1], BF16)
        self.xa01 = sb("xa01", [128, 2 * M1], BF16)
        self.sq01 = sb("sq01", [128, 2 * M1], BF16)
        self.ex01 = sb("ex01", [128, 2 * M1], BF16)
        self.ms01 = sb("ms01", [128, M1], U8)
        self.xs01 = sb("xs01", [128, M1], F32)
        self.ph01 = sb("ph01", [128, M1], F32)
        self.x2s = sb("x2s", [128, ITS * D], F32)
        self.ln2r = sb("ln2r", [128, ITS * D], BF16)
        self.ln2f = sb("ln2f", [128, ITS * D], BF16)
        self.p1sb = self.erm[:, 0:2048]
        self.g1sb = self.efm[0:4, 0:1024]
        self.g2sb = self.efm[0:4, 1024:1280]

        import os
        nphase = int(os.environ.get("ATH_PHASES", "99"))
        with nc.Block() as block:
            self.block = block
            self.init_phase()
            if nphase >= 1:
                self._init_gates = list(getattr(self, "_stage_gates", []))
                nT = min(T, nphase)
                self.emit_A(0)
                self._mt_gate_once = getattr(self, "_mt_store_gate", None)
                self.emit_q(0)
                p1g = (self.stream_state(self.sg) + self.stream_state(self.sout)
                       + [(self.GP, self.GP.n)])

                def pre_chain_gate(e, p1g=p1g):
                    for sem, v in p1g:
                        e.wait_ge(sem.h, v)
                self.on("vector", pre_chain_gate)
                self.on("scalar", pre_chain_gate)
                for t in range(nT):
                    for s_i in range(self.NS):
                        self.chain_subtile(t, s_i)
                    self.ar_start(t)
                    if t + 1 < nT:
                        self.emit_A(t + 1)
                    self.ar_finish(t)
                    self.emit_D(t)
                    if t + 1 < nT:
                        self.emit_q(t + 1)
            flush = [(self.SY, self.SY.n)]
            for stx in (self.sx, self.sout, self.sslot, self.sg, self.sgr,
                        self.tlnx, self.tsub, self.tder):
                flush += self.stream_state(stx)

            def fin(s, flush=flush):
                for sem, v in flush:
                    if v:
                        s.wait_ge(sem.h, v)
            self.on("sync", fin)
        stack.close()
        return nc

    def init_phase(self):
        import os
        lvl = int(os.environ.get("ATH_INIT", "99"))
        nc = self.nc

        def gp(g):
            self.dma(g, self.GP, out=self.W1n_f.ap().rearrange("p (c f) -> p c f", c=2),
                     in_=self.W1.rearrange("(c p) f -> p c f", p=128))
            self.dma(g, self.GP, out=self.W2n_f.ap().rearrange("p (c f) -> p c f", c=8),
                        in_=self.W2.rearrange("(c p) f -> p c f", p=128))
            self.dma(g, self.GP, out=self.b1r_f[:, :], in_=self.b1[None, :])
            self.dma(g, self.GP, out=self.b2r_f[:, :], in_=self.b2[None, :])
            self.dma(g, self.GP, out=self.fbias_b[:, 0:512], in_=self.br[None, :])
            self.dma(g, self.GP, out=self.fbias_b[:, 512:1024], in_=self.bi[None, :])
        self.on("gpsimd", gp)

        def dv(v):
            v.memset(self.ones_r[:, :], 1.0)
            v.memset(self.ones_c[:, :], 1.0)
            v.memset(self.epsc[:, :], 1e-5)
            v.memset(self.epsT[:, :], 1e-12)
            i = v.memset(self.halfT[:, :], 0.5)
            self.DV.inc(i)
        self.on("vector", dv)

        if lvl < 2:
            return
        for wi, (src, dst) in enumerate([(self.tk, self.tk_h), (self.tv, self.tv_h),
                                         (self.tq, self.tq_h)]):
            st_before = self.stream_state(self.tsub)

            def gpl(g, src=src, wi=wi, st_before=st_before):
                if wi > 0:
                    for tk_ in st_before:
                        g.wait_ge(tk_[0].h, tk_[1])
                self.dma(g, self.GP, out=self.slotA.ap().rearrange("p (c f) -> p c f", c=8),
                            in_=src.rearrange("(c p) f -> p c f", p=128))
            self.on("gpsimd", gpl)
            gp_now = self.GP.n

            def scl(s, gp_now=gp_now):
                s.wait_ge(self.GP.h, gp_now)
                for hc in range(8):
                    self.dma_s(s, self.tsub,
                        out=self.slotB.ap().rearrange("p (dc m) -> p dc m", dc=8)[:, :, hc * 128:(hc + 1) * 128],
                        in_=self.slotA[:, hc * D:(hc + 1) * D], transpose=True)
            self.on("scalar", scl)
            st_t = self.stream_state(self.tsub)

            def syl(s, dst=dst, st_t=st_t):
                for tk_ in st_t:
                    s.wait_ge(tk_[0].h, tk_[1])
                self.dma(s, self.SY, out=dst, in_=self.slotB[:, :])
            self.on("sync", syl)
            sy_now = self.SY.n

            def scw(s, sy_now=sy_now):
                s.wait_ge(self.SY.h, sy_now)
            self.on("scalar", scw)

        self._stage_gates = ([(self.SY, self.SY.n)]
                             + self.stream_state(self.tsub))
        if lvl < 3:
            return
        for part, wdst in enumerate([self.wr_h, self.wi_h]):
            if part == 0:
                def gpl0(g):
                    self.dma(g, self.GP, out=self.g1sb.rearrange("p (b d) -> p b d", b=32),
                             in_=self.G1R.rearrange("a b c d -> (a c) b d"))
                    self.dma(g, self.GP, out=self.g2sb.rearrange("p (e g) -> p e g", e=16),
                             in_=self.G2R.rearrange("c e f g -> (c f) e g"))
                self.on("gpsimd", gpl0)
            if part == 1:
                sy_prev = self.SY.n

                st_p = self.stream_state(self.sg) + self.stream_state(self.sout)

                def gpl2(g, st_p=st_p):
                    for tk_ in st_p:
                        g.wait_ge(tk_[0].h, tk_[1])
                    self.dma(g, self.GP, out=self.g1sb.rearrange("p (b d) -> p b d", b=32),
                                in_=self.G1I.rearrange("a b c d -> (a c) b d"))
                    self.dma(g, self.GP, out=self.g2sb.rearrange("p (e g) -> p e g", e=16),
                                in_=self.G2I.rearrange("c e f g -> (c f) e g"))
                self.on("gpsimd", gpl2)
            gp_now = self.GP.n
            with nc.psum_tensor("ps_init%d" % part, [128, 1024], F32) as ps_init:
                ac_base = self.AC.n
                pe_base = self.PE.n

                def pe(t, gp_now=gp_now, ac_base=ac_base, ps_init=ps_init):
                    t.wait_ge(self.GP.h, gp_now)
                    for m in range(8):
                        if m >= 2:
                            t.wait_ge(self.AC.h, ac_base + m - 1)
                        i = t.matmul(ps_init[:, (m % 2) * 512:(m % 2) * 512 + 256],
                                     self.g1sb[:, m * 128:(m + 1) * 128], self.g2sb[:, :],
                                     start=True, stop=True)
                        self.PE.inc(i)
                self.on("tensor", pe)

                def ac(s, pe_base=pe_base, ps_init=ps_init):
                    for m in range(8):
                        s.wait_ge(self.PE.h, pe_base + m + 1)
                        i = s.copy(self.p1sb[:, m * 256:(m + 1) * 256],
                                   ps_init[:, (m % 2) * 512:(m % 2) * 512 + 256])
                        self.AC.inc(i)
                self.on("scalar", ac)
            ac_now = self.AC.n

            def syp(s, wdst=wdst, ac_now=ac_now):
                import os
                s.wait_ge(self.AC.h, ac_now)
                if os.environ.get("ATH_NOSYP"):
                    return
                wt4 = wdst.rearrange("(b e) (d g) -> b d e g", e=16, g=16)
                k = 0
                for m in range(8):
                    for bl in range(4):
                        b = m * 4 + bl
                        self.dma_s(s, (self.sg, self.sout)[k % 2],
                            out=wt4[b, :, :, :],
                            in_=self.p1sb[bl * 32:(bl + 1) * 32, m * 256:(m + 1) * 256]
                            .rearrange("p (e g) -> p e g", e=16))
                        k += 1
            self.on("sync", syp)

        if lvl < 4:
            return
        st_p2 = self.stream_state(self.sg) + self.stream_state(self.sout)

        def sym(s, st_p2=st_p2):
            for tk_ in st_p2:
                s.wait_ge(tk_[0].h, tk_[1])
            mtb = self.q_rm if self.IT >= 8 else self.slotB
            v = mtb.ap().rearrange("p (kc c) -> p kc c", kc=8)
            for (w, kgrp, half) in ((self.wr_h, 0, 0), (self.wi_h, 1, 0),
                                    (self.wi_h, 0, 1), (self.wr_h, 1, 1)):
                for r in range(4):
                    self.dma_s(s, self.tsub,
                        out=v[:, kgrp * 4:(kgrp + 1) * 4,
                              half * 512 + r * 128: half * 512 + (r + 1) * 128],
                        in_=w[r * 128:(r + 1) * 128, :], transpose=True)
        self.on("scalar", sym)
        st_mt = self.stream_state(self.tsub)

        def dvn(v, st_mt=st_mt):
            for tk_ in st_mt:
                v.wait_ge(tk_[0].h, tk_[1])
            mtb = self.q_rm if self.IT >= 8 else self.slotB
            for kc in range(4, 8):
                i = v.tensor_scalar_mul(mtb[:, kc * D:kc * D + 512],
                                        mtb[:, kc * D:kc * D + 512], -1.0)
                self.DV.inc(i)
        self.on("vector", dvn)
        dv_now = self.DV.n

        def sym2(s, dv_now=dv_now):
            s.wait_ge(self.DV.h, dv_now)
            self.dma(s, self.SY, out=self.mt_h,
                     in_=(self.q_rm if self.IT >= 8 else self.slotB)[:, :])
        self.on("sync", sym2)
        self._mt_store_gate = (self.SY, self.SY.n)
        if self.IT < 8:
            self._stage_gates = ([(self.SY, self.SY.n)]
                                 + self.stream_state(self.tsub)
                                 + self.stream_state(self.sg)
                                 + self.stream_state(self.sout))

        if lvl < 5:
            return
        def dvd(v):
            v.wait_ge(self.GP.h, 16 * 4)
            self.emit_derive_dv(v)
        self.on("vector", dvd)
        dv_now = self.DV.n

        def scd(s, dv_now=dv_now):
            s.wait_ge(self.DV.h, dv_now)
            self.emit_derive_sc(s)
        self.on("scalar", scd)

    def emit_derive_dv(self, v):
        i = v.tensor_copy(self.W1n_b[:, :], self.W1n_f[:, :]); self.DV.inc(i)
        i = v.tensor_copy(self.W2n_b[:, :], self.W2n_f[:, :]); self.DV.inc(i)
        i = v.tensor_copy(self.b1r_b[:, :], self.b1r_f[:, :]); self.DV.inc(i)
        i = v.tensor_copy(self.b2r_b[:, :], self.b2r_f[:, :]); self.DV.inc(i)

    def emit_derive_sc(self, s, init=False, part="all"):
        dma = (lambda *a, **k: self.dma(s, self.SC, *a, **k)) if init else \
              (lambda *a, **k: self.dma_s(s, self.tder, *a, **k))
        if part in ("all", "w1"):
            w1toks = []
            for mc in range(2):
                w1toks.append(dma(
                    out=self.W1T_b.ap().rearrange("p (hc m) -> p hc m", hc=8)[:, :, mc * 128:(mc + 1) * 128],
                    in_=self.W1n_b[:, mc * D:(mc + 1) * D], transpose=True))
            if not init:
                self._w1toks = w1toks
        if part in ("all", "w2"):
            w2toks = []
            for jc in range(8):
                w2toks.append(dma(
                    out=self.W2T_b.ap().rearrange("p (m2c m) -> p m2c m", m2c=2)[:, :, jc * 128:(jc + 1) * 128],
                    in_=self.W2n_b[:, jc * 256:(jc + 1) * 256], transpose=True))
            if not init:
                self._w2toks = w2toks

    def emit_A(self, t):
        nc = self.nc
        R, IT, NS, ITS = self.R, self.IT, self.NS, self.ITS

        pe_slotA = self.pe_qdone
        pe_slotB = self.pe_ffndone

        toks = []

        gates = list(getattr(self, "_init_gates", []) or [])
        self._init_gates = None

        def syA(s, pe_slotA=pe_slotA, pe_slotB=pe_slotB, gates=gates):
            s.wait_ge(self.PE.h, pe_slotA)
            if pe_slotB:
                s.wait_ge(self.PE.h, pe_slotB)
            for sem, v in gates:
                s.wait_ge(sem.h, v)
            toks.append(self.dma_s(s, self.sslot, out=self.slotA[:, :], in_=self.tk_h))
            toks.append(self.dma_s(s, self.sslot, out=self.slotB[:, :], in_=self.tv_h))
        self.on("sync", syA)
        slot_toks = list(toks)

        with nc.psum_tensor("zps0_%d" % t, [128, D], F32) as zps0, \
             nc.psum_tensor("zps1_%d" % t, [128, D], F32) as zps1, \
             nc.psum_tensor("yps0_%d" % t, [128, D], F32) as yps0, \
             nc.psum_tensor("yps1_%d" % t, [128, D], F32) as yps1:
            zps = [zps0, zps1]
            yps = [yps0, yps1]
            a_ac0 = self.AC.n
            evacs = {}
            for it in range(IT):
                self.emit_ln_x(t, it, self.lnxr, it % 2)

                dv_lnc = self.DV.n
                ttok = []

                def sct(s, it=it, dv_lnc=dv_lnc):
                    s.wait_ge(self.DV.h, dv_lnc)
                    ttok.append(self.dma_s(s, self.tlnx,
                             out=self.lnx_fm.ap().rearrange("p (c m) -> p c m", m=128)[:, it * 8:(it + 1) * 8, :],
                             in_=self.lnxr[:, (it % 2) * D:(it % 2 + 1) * D], transpose=True))
                self.on("scalar", sct)
                self.lngate[it % 2] = ttok[0]
                lnx_tok = ttok[0]

                prev_evac = evacs.get(it - 2)

                def pezy(tt, it=it, lnx_tok=lnx_tok, prev_evac=prev_evac,
                         slot_toks=slot_toks, a_ac0=a_ac0):
                    self.wtok(tt, lnx_tok)
                    for tk_ in slot_toks:
                        self.wtok(tt, tk_)
                    if prev_evac is not None:
                        tt.wait_ge(self.AC.h, prev_evac[0])
                        tt.wait_ge(self.DV.h, prev_evac[1])
                    else:
                        tt.wait_ge(self.AC.h, a_ac0)
                    for dst, w in ((zps[it % 2], self.slotA), (yps[it % 2], self.slotB)):
                        for h in range(2):
                            for dc in range(8):
                                i = tt.matmul(dst[:, h * 512:(h + 1) * 512],
                                              self.lnx_fm[:, (it * 8 + dc) * 128:(it * 8 + dc + 1) * 128],
                                              w[:, dc * D + h * 512: dc * D + (h + 1) * 512],
                                              start=(dc == 0), stop=(dc == 7))
                            self.PE.inc(i)
                self.on("tensor", pezy)
                pe_n = self.PE.n

                def acz(s, it=it, pe_n=pe_n):
                    s.wait_ge(self.PE.h, pe_n - 2)
                    i = s.copy(self.z_rm[:, it * D:(it + 1) * D], zps[it % 2][:, :])
                    self.AC.inc(i)
                self.on("scalar", acz)
                ac_z = self.AC.n

                def dvy(v, it=it, pe_n=pe_n, ac_z=ac_z):
                    v.wait_ge(self.PE.h, pe_n)
                    v.wait_ge(self.AC.h, ac_z)
                    i = v.tensor_tensor(self.y_rm[:, it * D:(it + 1) * D],
                                        self.z_rm[:, it * D:(it + 1) * D],
                                        yps[it % 2][:, :], ALU.subtract)
                    self.DV.inc(i)
                self.on("vector", dvy)
                evacs[it] = (self.AC.n, self.DV.n)

    def emit_q(self, t):
        nc = self.nc
        R, IT = self.R, self.IT
        pe_zy = self.PE.n

        qtk = []

        mtg = getattr(self, "_mt_gate_once", None)
        self._mt_gate_once = None

        def syq(s, pe_zy=pe_zy, mtg=mtg):
            if mtg is not None:
                s.wait_ge(mtg[0].h, mtg[1])
            s.wait_ge(self.PE.h, pe_zy)
            qtk.append(self.dma_s(s, self.sslot, out=self.slotA[:, :], in_=self.tq_h))
        self.on("sync", syq)
        slot_tok = qtk[0]
        lnx_state = self.stream_state(self.tlnx)
        a_ac = self.AC.n
        a_dv = self.DV.n

        with nc.psum_tensor("qps0_%d" % t, [128, D], F32) as qps0, \
             nc.psum_tensor("qps1_%d" % t, [128, D], F32) as qps1:
            qps = [qps0, qps1]
            qevac = {}
            for it in range(IT):
                prev = qevac.get(it - 2)

                def peq(tt, it=it, prev=prev, slot_tok=slot_tok, a_ac=a_ac, a_dv=a_dv,
                        lnx_state=lnx_state):
                    self.wtok(tt, slot_tok)
                    for tk_ in lnx_state:
                        tt.wait_ge(tk_[0].h, tk_[1])
                    if prev is not None:
                        tt.wait_ge(self.AC.h, prev)
                    else:
                        tt.wait_ge(self.AC.h, a_ac)
                        tt.wait_ge(self.DV.h, a_dv)
                    for h in range(2):
                        for dc in range(8):
                            i = tt.matmul(qps[it % 2][:, h * 512:(h + 1) * 512],
                                          self.lnx_fm[:, (it * 8 + dc) * 128:(it * 8 + dc + 1) * 128],
                                          self.slotA[:, dc * D + h * 512: dc * D + (h + 1) * 512],
                                          start=(dc == 0), stop=(dc == 7))
                        self.PE.inc(i)
                self.on("tensor", peq)
                pe_n = self.PE.n

                def acq(s, it=it, pe_n=pe_n):
                    s.wait_ge(self.PE.h, pe_n)
                    i = s.copy(self.q_rm[:, it * D:(it + 1) * D], qps[it % 2][:, :])
                    self.AC.inc(i)
                self.on("scalar", acq)
                qevac[it] = self.AC.n
        self.pe_qdone = self.PE.n

    def emit_D(self, t):
        nc = self.nc
        pe_b = self.PE.n

        mtk = []

        def syd(s, pe_b=pe_b):
            s.wait_ge(self.PE.h, pe_b)
            mtk.append(self.dma_s(s, self.sslot, out=self.slotB[:, :], in_=self.mt_h))
        self.on("sync", syd)
        self.mt_tok = mtk[0]
        for s_i in range(self.NS):
            self.recon_subtile(t, s_i)
        self.pe_ffndone = self.PE.n

    def emit_ln_x(self, t, it, dstbuf, slot):
        R = self.R
        gate = self.xgate[self.xuse % 2]
        xslot = self.xuse % 2
        self.xuse += 1

        xt = []

        def syx(s, t=t, it=it, gate=gate, xslot=xslot):
            if gate:
                s.wait_ge(self.DV.h, gate)
            xt.append(self.dma_s(s, self.sx, out=self.x01[:, xslot * D:(xslot + 1) * D],
                      in_=self.x[t * R + it * 128: t * R + (it + 1) * 128, :]))
        self.on("sync", syx)
        x_tok = xt[0]
        lngate = self.lngate[slot]

        def dvln(v, x_tok=x_tok, xslot=xslot):
            self.wtok(v, x_tok)
            sl = self.x01[:, xslot * D:(xslot + 1) * D]
            v.bn_stats(self.st6[:, 0:6], sl[:, 0:512])
            i = v.bn_stats(self.st6[:, 6:12], sl[:, 512:1024]); self.DV.inc(i)
            v.wait_ge(self.DV.h, self.DV.n)
            i = v.bn_aggr(self.mv[:, :], self.st6[:, :]); self.DV.inc(i)
        self.on("vector", dvln)
        dv_n = self.DV.n

        def acln(s, dv_n=dv_n):
            s.wait_ge(self.DV.h, dv_n)
            i = s.activation(self.rstd[:, :], self.mv[:, 1:2], AF.Sqrt,
                             bias=self.epsc[:, 0:1], scale=1.0)
            self.AC.inc(i)
        self.on("scalar", acln)
        ac_n = self.AC.n

        def dvln2(v, ac_n=ac_n, xslot=xslot, slot=slot, lngate=lngate):
            v.wait_ge(self.AC.h, ac_n)
            self.wtok(v, lngate)
            i = v.reciprocal(self.rstd[:, :], self.rstd[:, :]); self.DV.inc(i)
            v.wait_ge(self.DV.h, self.DV.n)
            sl = self.x01[:, xslot * D:(xslot + 1) * D]
            i = v.tensor_scalar(dstbuf[:, slot * D:(slot + 1) * D],
                                sl, self.mv[:, 0:1], self.rstd[:, 0:1],
                                ALU.subtract, ALU.mult)
            self.DV.inc(i)
        self.on("vector", dvln2)
        self.xgate[xslot] = self.DV.n

    def chain_subtile(self, t, s_i):
        nc = self.nc
        ITS = self.ITS
        base_it = s_i * ITS
        ac_pre = self.AC.n
        dv_pre = self.DV.n
        pe_pre = self.PE.n

        ztoks = []

        def scz(s, base_it=base_it, ac_pre=ac_pre, pe_pre=pe_pre):
            s.wait_ge(self.AC.h, ac_pre)
            s.wait_ge(self.PE.h, pe_pre)
            for il in range(ITS):
                it = base_it + il
                ztoks.append(self.dma_s(s, self.tsub,
                    out=self.zfm.ap().rearrange("p (c m) -> p c m", m=128)[:, il * 8:(il + 1) * 8, :],
                    in_=self.z_rm[:, it * D:(it + 1) * D], transpose=True))
        self.on("scalar", scz)
        st_der = self.stream_state(self.tder)

        with nc.psum_tensor("aps0_%d_%d" % (t, s_i), [128, 512], F32) as aps0, \
             nc.psum_tensor("aps1_%d_%d" % (t, s_i), [128, 512], F32) as aps1, \
             nc.psum_tensor("rps0_%d_%d" % (t, s_i), [128, D], F32) as rps0, \
             nc.psum_tensor("rps1_%d_%d" % (t, s_i), [128, D], F32) as rps1:
            aps = [aps0, aps1]
            rps = [rps0, rps1]
            h_done = {}
            acw = {}
            htoks = []
            for il in range(ITS):
                prev_ac = acw.get(il - 2, 0)

                def pea(tt, il=il, st_der=st_der, prev_ac=prev_ac,
                        ac_pre=ac_pre, dv_pre=dv_pre):
                    for tk_ in [ztoks[il]] + st_der:
                        tt.wait_ge(tk_[0].h, tk_[1])
                    if prev_ac:
                        tt.wait_ge(self.AC.h, prev_ac)
                    else:
                        tt.wait_ge(self.AC.h, ac_pre)
                        tt.wait_ge(self.DV.h, dv_pre)
                    for hc in range(8):
                        tt.matmul(aps[il % 2][:, 0:256],
                                  self.zfm[:, (il * 8 + hc) * 128:(il * 8 + hc + 1) * 128],
                                  self.W1T_b[:, hc * 256:(hc + 1) * 256],
                                  start=(hc == 0), stop=False)
                    i = tt.matmul(aps[il % 2][:, 0:256], self.ones_r[:, :],
                                  self.b1r_b[:, :], start=False, stop=True)
                    self.PE.inc(i)
                self.on("tensor", pea)
                pe_a = self.PE.n
                dv_gate = h_done.get(il - 1, (0, 0))[1]

                def aca(s, il=il, pe_a=pe_a, dv_gate=dv_gate):
                    s.wait_ge(self.PE.h, pe_a)
                    if dv_gate:
                        s.wait_ge(self.DV.h, dv_gate)
                    ap = aps[il % 2][:, 0:256]
                    sl = slice((il % 2) * 256, (il % 2 + 1) * 256)
                    i = s.activation(self.hrm[:, il * 256:(il + 1) * 256], ap, AF.Gelu)
                    self.AC.inc(i)
                    i = s.copy(self.xa01[:, sl], ap); self.AC.inc(i)
                    i = s.activation(self.sq01[:, sl], ap, AF.Square); self.AC.inc(i)
                self.on("scalar", aca)
                acw[il] = self.AC.n
                ac_hil = self.AC.n

                def schil(s, il=il, ac_hil=ac_hil):
                    s.wait_ge(self.AC.h, ac_hil)
                    htoks.append(self.dma_s(s, self.tsub,
                        out=self.hfm.ap().rearrange("p (c m) -> p c m", m=128)[:, il * 2:(il + 1) * 2, :],
                        in_=self.hrm[:, il * 256:(il + 1) * 256], transpose=True))
                self.on("scalar", schil)

            def acE(s):
                s.wait_ge(self.AC.h, self.AC.n)
                for il in range(ITS):
                    sl = slice((il % 2) * 256, (il % 2 + 1) * 256)
                    i = s.activation(self.ex01[:, sl], self.sq01[:, sl], AF.Exp, scale=-0.5)
                    self.AC.inc(i)
            self.on("scalar", acE)
            ac_a = self.AC.n
            for il in range(ITS):
                def dvg(v, il=il, ac_a=ac_a):
                    v.wait_ge(self.AC.h, ac_a)
                    v.wait_ge(self.DV.h, self.DV.n)
                    sl = slice((il % 2) * 256, (il % 2 + 1) * 256)
                    i = v.tensor_single_scalar(self.ms01[:, :], self.sq01[:, sl], 1e-24, ALU.is_lt)
                    self.DV.inc(i)
                    i = v.tensor_copy(self.xs01[:, :], self.xa01[:, sl]); self.DV.inc(i)
                    v.wait_ge(self.DV.h, self.DV.n)
                    i = v.copy_predicated(self.xs01[:, :], self.ms01[:, :], self.epsT[:, :])
                    self.DV.inc(i)
                    v.wait_ge(self.DV.h, self.DV.n)
                    i = v.reciprocal(self.xs01[:, :], self.xs01[:, :]); self.DV.inc(i)
                    v.wait_ge(self.DV.h, self.DV.n)
                    i = v.tensor_tensor(self.ph01[:, :], self.hrm[:, il * 256:(il + 1) * 256],
                                        self.xs01[:, :], ALU.mult)
                    self.DV.inc(i)
                    v.wait_ge(self.DV.h, self.DV.n)
                    i = v.copy_predicated(self.ph01[:, :], self.ms01[:, :], self.halfT[:, :])
                    self.DV.inc(i)
                    i = v.tensor_tensor(self.xs01[:, :], self.xa01[:, sl], self.ex01[:, sl], ALU.mult)
                    self.DV.inc(i)
                    v.wait_ge(self.DV.h, self.DV.n)
                    i = v.scalar_tensor_tensor(self.gpb[:, il * 256:(il + 1) * 256],
                                               self.xs01[:, :], GELU_C, self.ph01[:, :],
                                               ALU.mult, ALU.add)
                    self.DV.inc(i)
                self.on("vector", dvg)
                h_done[il] = (self.AC.n, self.DV.n)



            edv = {}
            etoks = []
            for il in range(ITS):
                it = base_it + il
                prev_dv = edv.get(il - 2, 0)

                def per(tt, il=il, prev_dv=prev_dv):
                    self.wtok(tt, htoks[il])
                    if prev_dv:
                        tt.wait_ge(self.DV.h, prev_dv)
                    for h in range(2):
                        for m2c in range(2):
                            tt.matmul(rps[il % 2][:, h * 512:(h + 1) * 512],
                                      self.hfm[:, (il * 2 + m2c) * 128:(il * 2 + m2c + 1) * 128],
                                      self.W2T_b[:, m2c * D + h * 512:m2c * D + (h + 1) * 512],
                                      start=(m2c == 0), stop=False)
                        i = tt.matmul(rps[il % 2][:, h * 512:(h + 1) * 512],
                                      self.ones_r[:, :], self.b2r_b[:, h * 512:(h + 1) * 512],
                                      start=False, stop=True)
                        self.PE.inc(i)
                self.on("tensor", per)
                pe_r = self.PE.n

                def dve(v, il=il, it=it, pe_r=pe_r):
                    v.wait_ge(self.PE.h, pe_r)
                    i = v.tensor_tensor(self.erm[:, il * D:(il + 1) * D], rps[il % 2][:, :],
                                        self.y_rm[:, it * D:(it + 1) * D], ALU.add)
                    self.DV.inc(i)
                self.on("vector", dve)
                edv[il] = self.DV.n
                dv_eil = self.DV.n

                def sceil(s, il=il, dv_eil=dv_eil):
                    s.wait_ge(self.DV.h, dv_eil)
                    etoks.append(self.dma_s(s, self.tsub,
                        out=self.efm.ap().rearrange("p (c m) -> p c m", m=128)[:, il * 8:(il + 1) * 8, :],
                        in_=self.erm[:, il * D:(il + 1) * D], transpose=True))
                self.on("scalar", sceil)

            e_dv = self.DV.n
            gp_dv = self.DV.n
            ac_all = self.AC.n

        with nc.psum_tensor("gw2ps_%d_%d" % (t, s_i), [128, 8 * 256], F32) as gw2ps, \
             nc.psum_tensor("dps0_%d_%d" % (t, s_i), [128, 512], F32) as dps0, \
             nc.psum_tensor("dps1_%d_%d" % (t, s_i), [128, 512], F32) as dps1:
            dps = [dps0, dps1]

            def peg2(tt, e_dv=e_dv, ac_all=ac_all, gw2ps=gw2ps):
                tt.wait_ge(self.DV.h, e_dv)
                tt.wait_ge(self.AC.h, ac_all)
                for jc in range(8):
                    for il in range(ITS):
                        i = tt.matmul(gw2ps[:, jc * 256:(jc + 1) * 256],
                                      self.erm[:, il * D + jc * 128: il * D + (jc + 1) * 128],
                                      self.hrm[:, il * 256:(il + 1) * 256],
                                      start=(il == 0), stop=(il == ITS - 1))
                    self.PE.inc(i)
            self.on("tensor", peg2)
            pe_g2 = self.PE.n

            def dvg2(v, s_i=s_i, pe_g2=pe_g2, gw2ps=gw2ps):
                v.wait_ge(self.PE.h, pe_g2)
                if s_i == 0:
                    i = v.tensor_copy(self.gW2a[:, :], gw2ps[:, :]); self.DV.inc(i)
                else:
                    i = v.tensor_tensor(self.gW2a[:, :], self.gW2a[:, :], gw2ps[:, :], ALU.add)
                    self.DV.inc(i)
            self.on("vector", dvg2)

            dadv = {}
            for il in range(ITS):
                prev_dv = dadv.get(il - 2, 0)

                def ped(tt, il=il, prev_dv=prev_dv):
                    self.wtok(tt, etoks[il])
                    if prev_dv:
                        tt.wait_ge(self.DV.h, prev_dv)
                    for jc in range(8):
                        i = tt.matmul(dps[il % 2][:, 0:256],
                                      self.efm[:, (il * 8 + jc) * 128:(il * 8 + jc + 1) * 128],
                                      self.W2n_b[:, jc * 256:(jc + 1) * 256],
                                      start=(jc == 0), stop=(jc == 7))
                    self.PE.inc(i)
                self.on("tensor", ped)
                pe_d = self.PE.n

                def dvda(v, il=il, pe_d=pe_d):
                    v.wait_ge(self.PE.h, pe_d)
                    i = v.tensor_tensor(self.dab[:, il * 256:(il + 1) * 256],
                                        dps[il % 2][:, 0:256],
                                        self.gpb[:, il * 256:(il + 1) * 256], ALU.mult)
                    self.DV.inc(i)
                self.on("vector", dvda)
                dadv[il] = self.DV.n
            loops_dv = self.DV.n
            loops_ac = self.AC.n
            gw2_dv = self.DV.n

        with nc.psum_tensor("gw1ps_%d_%d" % (t, s_i), [128, 2 * D], F32) as gw1ps:

            def peg1(tt, base_it=base_it, loops_dv=loops_dv, gw1ps=gw1ps):
                tt.wait_ge(self.DV.h, loops_dv)
                for m1c in range(2):
                    for h in range(2):
                        for il in range(ITS):
                            it = base_it + il
                            i = tt.matmul(gw1ps[:, m1c * D + h * 512: m1c * D + (h + 1) * 512],
                                          self.dab[:, il * 256 + m1c * 128: il * 256 + (m1c + 1) * 128],
                                          self.z_rm[:, it * D + h * 512: it * D + (h + 1) * 512],
                                          start=(il == 0), stop=(il == ITS - 1))
                        self.PE.inc(i)
            self.on("tensor", peg1)
            pe_g1 = self.PE.n

            def dvg1(v, s_i=s_i, pe_g1=pe_g1, gw1ps=gw1ps):
                v.wait_ge(self.PE.h, pe_g1)
                if s_i == 0:
                    i = v.tensor_copy(self.gW1a[:, :], gw1ps[:, :]); self.DV.inc(i)
                else:
                    i = v.tensor_tensor(self.gW1a[:, :], self.gW1a[:, :], gw1ps[:, :], ALU.add)
                    self.DV.inc(i)
            self.on("vector", dvg1)
            gw_dv = self.DV.n

        with nc.psum_tensor("b2ps_%d_%d" % (t, s_i), [1, D], F32) as b2ps, \
             nc.psum_tensor("b1ps_%d_%d" % (t, s_i), [1, M1], F32) as b1ps:

            def peb(tt, gw_dv=gw_dv, b2ps=b2ps, b1ps=b1ps):
                tt.wait_ge(self.DV.h, gw_dv)
                for h in range(2):
                    for il in range(ITS):
                        i = tt.matmul(b2ps[:, h * 512:(h + 1) * 512], self.ones_c[:, :],
                                      self.erm[:, il * D + h * 512: il * D + (h + 1) * 512],
                                      start=(il == 0), stop=(il == ITS - 1))
                    self.PE.inc(i)
                for il in range(ITS):
                    i = tt.matmul(b1ps[:, :], self.ones_c[:, :],
                                  self.dab[:, il * 256:(il + 1) * 256],
                                  start=(il == 0), stop=(il == ITS - 1))
                self.PE.inc(i)
            self.on("tensor", peb)
            pe_bb = self.PE.n

            def dvb(v, s_i=s_i, pe_bb=pe_bb, b2ps=b2ps, b1ps=b1ps):
                v.wait_ge(self.PE.h, pe_bb)
                if s_i == 0:
                    i = v.tensor_copy(self.b2ar[:, :], b2ps[:, :]); self.DV.inc(i)
                    i = v.tensor_copy(self.b1ar[:, :], b1ps[:, :]); self.DV.inc(i)
                else:
                    i = v.tensor_tensor(self.b2ar[:, :], self.b2ar[:, :], b2ps[:, :], ALU.add)
                    self.DV.inc(i)
                    i = v.tensor_tensor(self.b1ar[:, :], self.b1ar[:, :], b1ps[:, :], ALU.add)
                    self.DV.inc(i)
            self.on("vector", dvb)

    def ar_start(self, t):
        dv_g = self.DV.n
        g2d = self.g_in[0:128 * 4096].rearrange("(p c) -> p c", c=2048)

        def sya(s, dv_g=dv_g):
            s.wait_ge(self.DV.h, dv_g)
            self.dma_s(s, self.sg, out=g2d[0:128, :], in_=self.gW1a[:, :])
            self.dma_s(s, self.sg, out=g2d[128:256, :], in_=self.gW2a[:, :])
            self.dma_s(s, self.sg, out=self.g_in[128 * 4096:128 * 4096 + 1024][None, :],
                       in_=self.b2ar[:, :])
            self.dma_s(s, self.sg, out=self.g_in[128 * 4096 + 1024:128 * 4096 + 1280][None, :],
                       in_=self.b1ar[:, :])
        self.on("sync", sya)
        st_g = self.stream_state(self.sg)

        def gpc(g, st_g=st_g):
            for tk_ in st_g:
                g.wait_ge(tk_[0].h, tk_[1])
            i = g.collective_compute("AllReduce", ALU.add,
                                     replica_groups=[list(range(NCORES))],
                                     ins=[self.g_in], outs=[self.g_out])
            self.CC.inc(i)
        self.on("gpsimd", gpc)

    def ar_finish(self, t):
        g2o = self.g_out[0:128 * 4096].rearrange("(p c) -> p c", c=2048)
        cc_n = self.CC.n
        rtoks = []

        def syb(g, cc_n=cc_n):
            g.wait_ge(self.CC.h, cc_n)
            for tk_ in self.stream_state(self.sg):
                g.wait_ge(tk_[0].h, tk_[1])
            rtoks.append(self.dma_s(g, self.sgr, out=self.gW1a[:, :], in_=g2o[0:128, :]))
            rtoks.append(self.dma_s(g, self.sgr, out=self.gW2a[:, :], in_=g2o[128:256, :]))
            rtoks.append(self.dma_s(g, self.sgr, out=self.b2ar[:, :],
                         in_=self.g_out[128 * 4096:128 * 4096 + 1024][None, :]))
            rtoks.append(self.dma_s(g, self.sgr, out=self.b1ar[:, :],
                         in_=self.g_out[128 * 4096 + 1024:128 * 4096 + 1280][None, :]))
        self.on("gpsimd", syb)
        c = self.upd_c

        st_ln = self.stream_state(self.tlnx)

        def dvu1(v, c=c, st_ln=st_ln, rtoks=rtoks):
            for tk_ in rtoks:
                v.wait_ge(tk_[0].h, tk_[1])
            for tk_ in st_ln:
                v.wait_ge(tk_[0].h, tk_[1])
            i = v.scalar_tensor_tensor(self.W1n_f[:, :], self.gW1a[:, :], -c,
                                       self.W1n_f[:, :], ALU.mult, ALU.add)
            self.DV.inc(i)
            i = v.scalar_tensor_tensor(self.b1r_f[:, :], self.b1ar[:, :], -c,
                                       self.b1r_f[:, :], ALU.mult, ALU.add)
            self.DV.inc(i)
            i = v.scalar_tensor_tensor(self.b2r_f[:, :], self.b2ar[:, :], -c,
                                       self.b2r_f[:, :], ALU.mult, ALU.add)
            self.DV.inc(i)
            v.wait_ge(self.DV.h, self.DV.n)
            i = v.tensor_copy(self.W1n_b[:, :], self.W1n_f[:, :]); self.DV.inc(i)
            i = v.tensor_copy(self.b1r_b[:, :], self.b1r_f[:, :]); self.DV.inc(i)
            i = v.tensor_copy(self.b2r_b[:, :], self.b2r_f[:, :]); self.DV.inc(i)
        self.on("vector", dvu1)
        dv_u1 = self.DV.n

        def scu1(s, dv_u1=dv_u1):
            s.wait_ge(self.DV.h, dv_u1)
            self.emit_derive_sc(s, part="w1")
        self.on("scalar", scu1)

        def dvu2(v, c=c):
            i = v.scalar_tensor_tensor(self.W2n_f[:, :], self.gW2a[:, :], -c,
                                       self.W2n_f[:, :], ALU.mult, ALU.add)
            self.DV.inc(i)
            v.wait_ge(self.DV.h, self.DV.n)
            i = v.tensor_copy(self.W2n_b[:, :], self.W2n_f[:, :]); self.DV.inc(i)
        self.on("vector", dvu2)
        dv_u2 = self.DV.n

        def scu2(s, dv_u2=dv_u2):
            s.wait_ge(self.DV.h, dv_u2)
            self.emit_derive_sc(s, part="w2")
        self.on("scalar", scu2)
        self.lngate = [self._w1toks[0], self._w1toks[1]]

    def recon_subtile(self, t, s_i):
        nc = self.nc
        R, ITS = self.R, self.ITS
        base_it = s_i * ITS
        st_der = list(getattr(self, "_w1toks", []))
        st_der2 = list(getattr(self, "_w2toks", []))
        dv_pre = self.DV.n
        st_out_pre = self.stream_state(self.sout)

        qtoks = []

        def scq(s, base_it=base_it):
            for il in range(ITS):
                it = base_it + il
                qtoks.append(self.dma_s(s, self.tsub,
                    out=self.zfm.ap().rearrange("p (c m) -> p c m", m=128)[:, il * 8:(il + 1) * 8, :],
                    in_=self.q_rm[:, it * D:(it + 1) * D], transpose=True))
        self.on("scalar", scq)

        h2toks = []
        with nc.psum_tensor("a2ps0_%d_%d" % (t, s_i), [128, 512], F32) as a2ps0, \
             nc.psum_tensor("a2ps1_%d_%d" % (t, s_i), [128, 512], F32) as a2ps1:
            a2ps = [a2ps0, a2ps1]
            h2ac = {}
            for il in range(ITS):
                prev = h2ac.get(il - 2, 0)

                def pea2(tt, il=il, st_der=st_der, prev=prev, dv_pre=dv_pre,
                         a2ps=a2ps):
                    for tk_ in [qtoks[il]] + st_der:
                        tt.wait_ge(tk_[0].h, tk_[1])
                    if prev:
                        tt.wait_ge(self.AC.h, prev)
                    else:
                        tt.wait_ge(self.DV.h, dv_pre)
                    for hc in range(8):
                        tt.matmul(a2ps[il % 2][:, 0:256],
                                  self.zfm[:, (il * 8 + hc) * 128:(il * 8 + hc + 1) * 128],
                                  self.W1T_b[:, hc * 256:(hc + 1) * 256],
                                  start=(hc == 0), stop=False)
                    i = tt.matmul(a2ps[il % 2][:, 0:256], self.ones_r[:, :],
                                  self.b1r_b[:, :], start=False, stop=True)
                    self.PE.inc(i)
                self.on("tensor", pea2)
                pe_a2 = self.PE.n

                def ach2(s, il=il, pe_a2=pe_a2, a2ps=a2ps):
                    s.wait_ge(self.PE.h, pe_a2)
                    i = s.activation(self.hrm[:, il * 256:(il + 1) * 256],
                                     a2ps[il % 2][:, 0:256], AF.Gelu)
                    self.AC.inc(i)
                self.on("scalar", ach2)
                h2ac[il] = self.AC.n
                ac_h2il = self.AC.n

                def sch2il(s, il=il, ac_h2il=ac_h2il):
                    s.wait_ge(self.AC.h, ac_h2il)
                    h2toks.append(self.dma_s(s, self.tsub,
                        out=self.hfm.ap().rearrange("p (c m) -> p c m", m=128)[:, il * 2:(il + 1) * 2, :],
                        in_=self.hrm[:, il * 256:(il + 1) * 256], transpose=True))
                self.on("scalar", sch2il)

        ac_h2all = self.AC.n
        with nc.psum_tensor("ops0_%d_%d" % (t, s_i), [128, D], F32) as ops0, \
             nc.psum_tensor("ops1_%d_%d" % (t, s_i), [128, D], F32) as ops1, \
             nc.psum_tensor("fps0_%d_%d" % (t, s_i), [128, D], F32) as fps0, \
             nc.psum_tensor("fps1_%d_%d" % (t, s_i), [128, D], F32) as fps1:
            ops = [ops0, ops1]
            fps = [fps0, fps1]
            xr_toks = []
            xslots = []
            for il in range(ITS):
                it = base_it + il
                gate = self.xgate[self.xuse % 2]
                xslot = self.xuse % 2
                self.xuse += 1
                xslots.append(xslot)
                xrt = []

                def syxr(s, t=t, it=it, gate=gate, xslot=xslot):
                    if gate:
                        s.wait_ge(self.DV.h, gate)
                    xrt.append(self.dma_s(s, self.sx, out=self.x01[:, xslot * D:(xslot + 1) * D],
                               in_=self.x[t * R + it * 128: t * R + (it + 1) * 128, :]))
                self.on("sync", syxr)
                xr_toks.append(xrt[0])
            ln2dv = {}
            l2toks = []
            for il in range(ITS):
                it = base_it + il
                xr_tok = xr_toks[il]
                xslot = xslots[il]
                prev = ln2dv.get(il - 2, 0)

                def peo(tt, il=il, prev=prev, ops=ops, st_der2=st_der2,
                        ac_h2all=ac_h2all):
                    for tk_ in [h2toks[il]] + st_der2:
                        tt.wait_ge(tk_[0].h, tk_[1])
                    if prev:
                        tt.wait_ge(self.DV.h, prev)
                    else:
                        tt.wait_ge(self.AC.h, ac_h2all)
                    for h in range(2):
                        for m2c in range(2):
                            tt.matmul(ops[il % 2][:, h * 512:(h + 1) * 512],
                                      self.hfm[:, (il * 2 + m2c) * 128:(il * 2 + m2c + 1) * 128],
                                      self.W2T_b[:, m2c * D + h * 512:m2c * D + (h + 1) * 512],
                                      start=(m2c == 0), stop=False)
                        i = tt.matmul(ops[il % 2][:, h * 512:(h + 1) * 512],
                                      self.ones_r[:, :], self.b2r_b[:, h * 512:(h + 1) * 512],
                                      start=False, stop=True)
                        self.PE.inc(i)
                self.on("tensor", peo)
                pe_o = self.PE.n

                def dvx2(v, il=il, it=it, pe_o=pe_o, xr_tok=xr_tok, xslot=xslot,
                         st_out_pre=st_out_pre, ops=ops):
                    v.wait_ge(self.PE.h, pe_o)
                    for tk_ in st_out_pre:
                        v.wait_ge(tk_[0].h, tk_[1])
                    i = v.tensor_tensor(self.x2s[:, il * D:(il + 1) * D], ops[il % 2][:, :],
                                        self.q_rm[:, it * D:(it + 1) * D], ALU.add)
                    self.DV.inc(i)
                    self.wtok(v, xr_tok)
                    v.wait_ge(self.DV.h, self.DV.n)
                    i = v.tensor_tensor(self.x2s[:, il * D:(il + 1) * D],
                                        self.x2s[:, il * D:(il + 1) * D],
                                        self.x01[:, xslot * D:(xslot + 1) * D], ALU.add)
                    self.DV.inc(i)
                    sl = self.x2s[:, il * D:(il + 1) * D]
                    v.wait_ge(self.DV.h, self.DV.n)
                    v.bn_stats(self.st6[:, 0:6], sl[:, 0:512])
                    i = v.bn_stats(self.st6[:, 6:12], sl[:, 512:1024]); self.DV.inc(i)
                    v.wait_ge(self.DV.h, self.DV.n)
                    i = v.bn_aggr(self.mv[:, :], self.st6[:, :]); self.DV.inc(i)
                self.on("vector", dvx2)
                self.xgate[xslot] = self.DV.n
                dv_st = self.DV.n

                def acl2(s, dv_st=dv_st):
                    s.wait_ge(self.DV.h, dv_st)
                    i = s.activation(self.rstd[:, :], self.mv[:, 1:2], AF.Sqrt,
                                     bias=self.epsc[:, 0:1], scale=1.0)
                    self.AC.inc(i)
                self.on("scalar", acl2)
                ac_l2 = self.AC.n

                def dvl2(v, il=il, ac_l2=ac_l2):
                    v.wait_ge(self.AC.h, ac_l2)
                    i = v.reciprocal(self.rstd[:, :], self.rstd[:, :]); self.DV.inc(i)
                    v.wait_ge(self.DV.h, self.DV.n)
                    i = v.tensor_scalar(self.ln2r[:, il * D:(il + 1) * D],
                                        self.x2s[:, il * D:(il + 1) * D],
                                        self.mv[:, 0:1], self.rstd[:, 0:1],
                                        ALU.subtract, ALU.mult)
                    self.DV.inc(i)
                self.on("vector", dvl2)
                ln2dv[il] = self.DV.n
                dv_l2il = self.DV.n

                def scl2il(s, il=il, dv_l2il=dv_l2il):
                    s.wait_ge(self.DV.h, dv_l2il)
                    l2toks.append(self.dma_s(s, self.tsub,
                        out=self.ln2f.ap().rearrange("p (c m) -> p c m", m=128)[:, il * 8:(il + 1) * 8, :],
                        in_=self.ln2r[:, il * D:(il + 1) * D], transpose=True))
                self.on("scalar", scl2il)

            outdv = {}
            for il in range(ITS):
                it = base_it + il
                prev = outdv.get(il - 2, 0)

                def pef(tt, il=il, prev=prev, mt_tok=self.mt_tok, fps=fps,
                        ac_h2all=ac_h2all):
                    self.wtok(tt, l2toks[il])
                    self.wtok(tt, mt_tok)
                    if prev:
                        tt.wait_ge(self.DV.h, prev)
                    else:
                        tt.wait_ge(self.AC.h, ac_h2all)
                    for h in range(2):
                        for kc in range(8):
                            tt.matmul(fps[il % 2][:, h * 512:(h + 1) * 512],
                                      self.ln2f[:, (il * 8 + kc) * 128:(il * 8 + kc + 1) * 128],
                                      self.slotB[:, kc * D + h * 512: kc * D + (h + 1) * 512],
                                      start=(kc == 0), stop=False)
                        i = tt.matmul(fps[il % 2][:, h * 512:(h + 1) * 512],
                                      self.ones_r[:, :], self.fbias_b[:, h * 512:(h + 1) * 512],
                                      start=False, stop=True)
                        self.PE.inc(i)
                self.on("tensor", pef)
                pe_f = self.PE.n

                def dvo(v, il=il, pe_f=pe_f, fps=fps):
                    v.wait_ge(self.PE.h, pe_f)
                    i = v.tensor_tensor(self.x2s[:, il * D:(il + 1) * D],
                                        fps[il % 2][:, :],
                                        self.x2s[:, il * D:(il + 1) * D], ALU.add)
                    self.DV.inc(i)
                self.on("vector", dvo)
                dv_o = self.DV.n
                outdv[il] = dv_o

                def syo(s, t=t, it=it, il=il, dv_o=dv_o):
                    s.wait_ge(self.DV.h, dv_o)
                    self.dma_s(s, self.sout, out=self.y[t * R + it * 128: t * R + (it + 1) * 128, :],
                               in_=self.x2s[:, il * D:(il + 1) * D])
                self.on("sync", syo)


def build_program(R):
    return Builder(R).build()


def kernel(L0_attn_in_seq, L0_attn_t_k, L0_attn_t_v, L0_attn_t_q, L0_attn_W1,
           L0_attn_b1, L0_attn_W2, L0_attn_b2, L0_ffn_G1_R, L0_ffn_G2_R,
           L0_ffn_G1_I, L0_ffn_G2_I, L0_ffn_bias_real, L0_ffn_bias_imag):
    x = np.asarray(L0_attn_in_seq)
    Tn, N, Dn = x.shape
    R = N // NCORES
    nc = build_program(R)
    shared = {
        "t_k": np.ascontiguousarray(L0_attn_t_k),
        "t_v": np.ascontiguousarray(L0_attn_t_v),
        "t_q": np.ascontiguousarray(L0_attn_t_q),
        "W1": np.ascontiguousarray(L0_attn_W1),
        "b1": np.ascontiguousarray(L0_attn_b1),
        "W2": np.ascontiguousarray(L0_attn_W2),
        "b2": np.ascontiguousarray(L0_attn_b2),
        "G1R": np.ascontiguousarray(L0_ffn_G1_R),
        "G2R": np.ascontiguousarray(L0_ffn_G2_R),
        "G1I": np.ascontiguousarray(L0_ffn_G1_I),
        "G2I": np.ascontiguousarray(L0_ffn_G2_I),
        "br": np.ascontiguousarray(L0_ffn_bias_real),
        "bi": np.ascontiguousarray(L0_ffn_bias_imag),
    }
    in_maps = []
    for c in range(NCORES):
        m = dict(shared)
        m["x"] = np.ascontiguousarray(x[:, c * R:(c + 1) * R, :]).reshape(Tn * R, Dn)
        in_maps.append(m)
    res = run_bass_kernel_spmd(nc, in_maps, core_ids=list(range(NCORES)))
    out = np.empty((Tn, N, Dn), dtype=np.float32)
    for c in range(NCORES):
        out[:, c * R:(c + 1) * R, :] = res.results[c]["y"].reshape(Tn, R, Dn)
    return out


# version 17
# speedup vs baseline: 1.0759x; 1.0329x over previous
import numpy as np
import concourse.bass as bass
import concourse.mybir as mybir
from concourse.bass_utils import run_bass_kernel_spmd

F32 = mybir.dt.float32
BF16 = mybir.dt.bfloat16
U8 = mybir.dt.uint8
AF = mybir.ActivationFunctionType
ALU = mybir.AluOpType

D = 1024
M1 = 256
T = 4
LR = 1e-3
NCORES = 8
GELU_C = float(1.0 / np.sqrt(2.0 * np.pi))


class Sem:
    def __init__(self, nc, name):
        self.h = nc.alloc_semaphore(name=name)
        self.n = 0

    def inc(self, instr, amt=1):
        instr.then_inc(self.h, amt)
        self.n += amt
        return self.n


class Stream:
    def __init__(self, nc, name):
        self.sems = [Sem(nc, name + "_0"), Sem(nc, name + "_1")]
        self.i = 0


class Builder:
    def __init__(self, R):
        self.R = R
        self.IT = R // 128
        self.NS = 4 if self.IT >= 8 else (2 if self.IT >= 2 else 1)
        self.ITS = self.IT // self.NS
        self.upd_c = float(LR * 2.0 / ((NCORES * R) * D))
        nc = bass.Bass("TRN2", target_bir_lowering=False)
        self.nc = nc

        self.x = nc.dram_tensor("x", [T * R, D], F32, kind="ExternalInput").ap()
        self.y = nc.dram_tensor("y", [T * R, D], F32, kind="ExternalOutput").ap()
        self.tk = nc.dram_tensor("t_k", [D, D], F32, kind="ExternalInput").ap()
        self.tv = nc.dram_tensor("t_v", [D, D], F32, kind="ExternalInput").ap()
        self.tq = nc.dram_tensor("t_q", [D, D], F32, kind="ExternalInput").ap()
        self.W1 = nc.dram_tensor("W1", [M1, D], F32, kind="ExternalInput").ap()
        self.b1 = nc.dram_tensor("b1", [M1], F32, kind="ExternalInput").ap()
        self.W2 = nc.dram_tensor("W2", [D, M1], F32, kind="ExternalInput").ap()
        self.b2 = nc.dram_tensor("b2", [D], F32, kind="ExternalInput").ap()
        self.G1R = nc.dram_tensor("G1R", [1, 32, 4, 32], F32, kind="ExternalInput").ap()
        self.G2R = nc.dram_tensor("G2R", [4, 16, 1, 16], F32, kind="ExternalInput").ap()
        self.G1I = nc.dram_tensor("G1I", [1, 32, 4, 32], F32, kind="ExternalInput").ap()
        self.G2I = nc.dram_tensor("G2I", [4, 16, 1, 16], F32, kind="ExternalInput").ap()
        self.br = nc.dram_tensor("br", [512], F32, kind="ExternalInput").ap()
        self.bi = nc.dram_tensor("bi", [512], F32, kind="ExternalInput").ap()
        self.tk_h = nc.dram_tensor("tk_h", [128, 8 * D], BF16).ap()
        self.tv_h = nc.dram_tensor("tv_h", [128, 8 * D], BF16).ap()
        self.tq_h = nc.dram_tensor("tq_h", [128, 8 * D], BF16).ap()
        self.mt_h = nc.dram_tensor("mt_h", [128, 8 * D], BF16).ap()
        self.wr_h = nc.dram_tensor("wr_h", [512, 512], BF16).ap()
        self.wi_h = nc.dram_tensor("wi_h", [512, 512], BF16).ap()
        self.g_in = nc.dram_tensor("g_in", [128 * 4096 + 1024 + 256], F32).ap()
        self.g_out = nc.dram_tensor("g_out", [128 * 4096 + 1024 + 256], F32).ap()

        self.PE = Sem(nc, "sPE")
        self.DV = Sem(nc, "sDV")
        self.AC = Sem(nc, "sAC")
        self.SY = Sem(nc, "sSY")
        self.SC = Sem(nc, "sSC")
        self.GP = Sem(nc, "sGP")
        self.CC = Sem(nc, "sCC")
        self.sx = Stream(nc, "sx")
        self.sout = Stream(nc, "sout")
        self.sslot = Stream(nc, "sslot")
        self.sg = Stream(nc, "sg")
        self.tlnx = Stream(nc, "tlnx")
        self.tsub = Stream(nc, "tsub")
        self.tder = Stream(nc, "tder")
        self.sgr = Stream(nc, "sgr")
        self.xgate = [0, 0]
        self.lngate = [None, None]
        self.xuse = 0
        self.pe_qdone = 0
        self.pe_ffndone = 0

    def on(self, eng, fn):
        getattr(self.block, eng)(fn)

    def dma_s(self, eng, stream, out, in_, **kw):
        sem = stream.sems[stream.i % 2]
        stream.i += 1
        eng.wait_ge(sem.h, sem.n)
        i = eng.dma_start(out=out, in_=in_, **kw)
        sem.inc(i, 16)
        return (sem, sem.n)

    def wtok(self, eng, tok):
        if tok is not None:
            eng.wait_ge(tok[0].h, tok[1])

    def wstream(self, eng, stream):
        for sem in stream.sems:
            if sem.n:
                eng.wait_ge(sem.h, sem.n)

    def stream_state(self, stream):
        return [(sem, sem.n) for sem in stream.sems if sem.n]

    def dma(self, eng, sem, out, in_, **kw):
        eng.wait_ge(sem.h, sem.n)
        i = eng.dma_start(out=out, in_=in_, **kw)
        sem.inc(i, 16)
        return i

    def barrier(self):
        sems = [self.PE, self.DV, self.AC, self.SY, self.SC, self.GP, self.CC]
        for stx in (self.sx, self.sout, self.sslot, self.sg, self.sgr,
                    self.tlnx, self.tsub, self.tder):
            sems += stx.sems
        vals = [(s.h, s.n) for s in sems if s.n > 0]

        def mk(e):
            for h, n in vals:
                e.wait_ge(h, n)
        for eng in ("tensor", "vector", "scalar", "sync", "gpsimd"):
            self.on(eng, mk)

    def build(self):
        nc = self.nc
        import contextlib
        stack = contextlib.ExitStack()
        sb = lambda name, shape, dt: stack.enter_context(nc.sbuf_tensor(name, shape, dt))
        IT, ITS = self.IT, self.ITS
        self.W1n_f = sb("W1n_f", [128, 2 * D], F32)
        self.W2n_f = sb("W2n_f", [128, 8 * M1], F32)
        self.W2n_b = sb("W2n_b", [128, 8 * M1], BF16)
        self.W1T_b = sb("W1T_b", [128, 8 * M1], BF16)
        self.W2T_b = sb("W2T_b", [128, 2 * D], BF16)
        self.b1r_f = sb("b1r_f", [1, M1], F32)
        self.b2r_f = sb("b2r_f", [1, D], F32)
        self.b1r_b = sb("b1r_b", [1, M1], BF16)
        self.b2r_b = sb("b2r_b", [1, D], BF16)
        self.fbias_b = sb("fbias_b", [1, D], BF16)
        self.ones_r = sb("ones_r", [1, 128], BF16)
        self.ones_c = sb("ones_c", [128, 1], BF16)
        self.epsc = sb("epsc", [128, 1], F32)
        self.epsT = sb("epsT", [128, M1], F32)
        self.halfT = sb("halfT", [128, M1], F32)
        self.slotA = sb("slotA", [128, 8 * D], BF16)
        self.slotB = sb("slotB", [128, 8 * D], BF16)
        self.lnx_fm = sb("lnx_fm", [128, IT * D], BF16)
        self.z_rm = sb("z_rm", [128, IT * D], BF16)
        self.y_rm = sb("y_rm", [128, IT * D], BF16)
        self.q_rm = sb("q_rm", [128, IT * D], BF16)
        self.gW1a = sb("gW1a", [128, 2 * D], F32)
        self.gW2a = sb("gW2a", [128, 8 * M1], F32)
        self.b1ar = sb("b1ar", [1, M1], F32)
        self.b2ar = sb("b2ar", [1, D], F32)
        self.st6 = sb("st6", [128, 12], F32)
        self.mv = sb("mv", [128, 2], F32)
        self.rstd = sb("rstd", [128, 1], F32)
        self.x01 = sb("x01", [128, 2 * D], F32)
        self.lnxr = sb("lnxr", [128, 2 * D], BF16)
        self.W1n_b = self.lnxr.ap()
        self.zfm = sb("zfm", [128, ITS * D], BF16)
        self.erm = sb("erm", [128, ITS * D], BF16)
        self.efm = sb("efm", [128, ITS * D], BF16)
        self.hrm = sb("hrm", [128, ITS * M1], BF16)
        self.hfm = sb("hfm", [128, ITS * M1], BF16)
        self.gpb = sb("gpb", [128, ITS * M1], BF16)
        self.dab = sb("dab", [128, ITS * M1], BF16)
        self.xa01 = sb("xa01", [128, 2 * M1], BF16)
        self.sq01 = sb("sq01", [128, 2 * M1], BF16)
        self.ex01 = sb("ex01", [128, 2 * M1], BF16)
        self.ms01 = sb("ms01", [128, M1], U8)
        self.xs01 = sb("xs01", [128, M1], F32)
        self.ph01 = sb("ph01", [128, M1], F32)
        self.x2s = sb("x2s", [128, ITS * D], F32)
        self.ln2r = sb("ln2r", [128, ITS * D], BF16)
        self.ln2f = sb("ln2f", [128, ITS * D], BF16)
        self.p1sb = self.erm[:, 0:2048]
        self.g1sb = self.efm[0:4, 0:1024]
        self.g2sb = self.efm[0:4, 1024:1280]

        import os
        nphase = int(os.environ.get("ATH_PHASES", "99"))
        with nc.Block() as block:
            self.block = block
            self.init_phase()
            if nphase >= 1:
                self._init_gates = list(getattr(self, "_stage_gates", []))
                nT = min(T, nphase)
                self.emit_A(0)
                self._mt_gate_once = getattr(self, "_mt_store_gate", None)
                self.emit_q(0)
                p1g = (self.stream_state(self.sg) + self.stream_state(self.sout)
                       + [(self.GP, self.GP.n)])

                def pre_chain_gate(e, p1g=p1g):
                    for sem, v in p1g:
                        e.wait_ge(sem.h, v)
                self.on("vector", pre_chain_gate)
                self.on("scalar", pre_chain_gate)
                for t in range(nT):
                    for s_i in range(self.NS):
                        self.chain_subtile(t, s_i)
                    self.ar_start(t)
                    if t + 1 < nT:
                        self.emit_A(t + 1)
                    self.ar_finish(t)
                    self.emit_D(t)
                    if t + 1 < nT:
                        self.emit_q(t + 1)
            flush = [(self.SY, self.SY.n)]
            for stx in (self.sx, self.sout, self.sslot, self.sg, self.sgr,
                        self.tlnx, self.tsub, self.tder):
                flush += self.stream_state(stx)

            def fin(s, flush=flush):
                for sem, v in flush:
                    if v:
                        s.wait_ge(sem.h, v)
            self.on("sync", fin)
        stack.close()
        return nc

    def init_phase(self):
        import os
        lvl = int(os.environ.get("ATH_INIT", "99"))
        nc = self.nc

        def gp(g):
            self.dma(g, self.GP, out=self.W1n_f.ap().rearrange("p (c f) -> p c f", c=2),
                     in_=self.W1.rearrange("(c p) f -> p c f", p=128))
            self.dma(g, self.GP, out=self.W2n_f.ap().rearrange("p (c f) -> p c f", c=8),
                        in_=self.W2.rearrange("(c p) f -> p c f", p=128))
            self.dma(g, self.GP, out=self.b1r_f[:, :], in_=self.b1[None, :])
            self.dma(g, self.GP, out=self.b2r_f[:, :], in_=self.b2[None, :])
            self.dma(g, self.GP, out=self.fbias_b[:, 0:512], in_=self.br[None, :])
            self.dma(g, self.GP, out=self.fbias_b[:, 512:1024], in_=self.bi[None, :])
        self.on("gpsimd", gp)

        def dv(v):
            v.memset(self.ones_r[:, :], 1.0)
            v.memset(self.ones_c[:, :], 1.0)
            v.memset(self.epsc[:, :], 1e-5)
            v.memset(self.epsT[:, :], 1e-12)
            i = v.memset(self.halfT[:, :], 0.5)
            self.DV.inc(i)
        self.on("vector", dv)

        if lvl < 2:
            return
        for wi, (src, dst) in enumerate([(self.tk, self.tk_h), (self.tv, self.tv_h),
                                         (self.tq, self.tq_h)]):
            st_before = self.stream_state(self.tsub)

            def gpl(g, src=src, wi=wi, st_before=st_before):
                if wi > 0:
                    for tk_ in st_before:
                        g.wait_ge(tk_[0].h, tk_[1])
                self.dma(g, self.GP, out=self.slotA.ap().rearrange("p (c f) -> p c f", c=8),
                            in_=src.rearrange("(c p) f -> p c f", p=128))
            self.on("gpsimd", gpl)
            gp_now = self.GP.n

            def scl(s, gp_now=gp_now):
                s.wait_ge(self.GP.h, gp_now)
                for hc in range(8):
                    self.dma_s(s, self.tsub,
                        out=self.slotB.ap().rearrange("p (dc m) -> p dc m", dc=8)[:, :, hc * 128:(hc + 1) * 128],
                        in_=self.slotA[:, hc * D:(hc + 1) * D], transpose=True)
            self.on("scalar", scl)
            st_t = self.stream_state(self.tsub)

            def syl(s, dst=dst, st_t=st_t):
                for tk_ in st_t:
                    s.wait_ge(tk_[0].h, tk_[1])
                self.dma(s, self.SY, out=dst, in_=self.slotB[:, :])
            self.on("sync", syl)
            sy_now = self.SY.n

            def scw(s, sy_now=sy_now):
                s.wait_ge(self.SY.h, sy_now)
            self.on("scalar", scw)

        self._stage_gates = ([(self.SY, self.SY.n)]
                             + self.stream_state(self.tsub))
        if lvl < 3:
            return
        for part, wdst in enumerate([self.wr_h, self.wi_h]):
            if part == 0:
                def gpl0(g):
                    self.dma(g, self.GP, out=self.g1sb.rearrange("p (b d) -> p b d", b=32),
                             in_=self.G1R.rearrange("a b c d -> (a c) b d"))
                    self.dma(g, self.GP, out=self.g2sb.rearrange("p (e g) -> p e g", e=16),
                             in_=self.G2R.rearrange("c e f g -> (c f) e g"))
                self.on("gpsimd", gpl0)
            if part == 1:
                sy_prev = self.SY.n

                st_p = self.stream_state(self.sg) + self.stream_state(self.sout)

                def gpl2(g, st_p=st_p):
                    for tk_ in st_p:
                        g.wait_ge(tk_[0].h, tk_[1])
                    self.dma(g, self.GP, out=self.g1sb.rearrange("p (b d) -> p b d", b=32),
                                in_=self.G1I.rearrange("a b c d -> (a c) b d"))
                    self.dma(g, self.GP, out=self.g2sb.rearrange("p (e g) -> p e g", e=16),
                                in_=self.G2I.rearrange("c e f g -> (c f) e g"))
                self.on("gpsimd", gpl2)
            gp_now = self.GP.n
            with nc.psum_tensor("ps_init%d" % part, [128, 1024], F32) as ps_init:
                ac_base = self.AC.n
                pe_base = self.PE.n

                def pe(t, gp_now=gp_now, ac_base=ac_base, ps_init=ps_init):
                    t.wait_ge(self.GP.h, gp_now)
                    for m in range(8):
                        if m >= 2:
                            t.wait_ge(self.AC.h, ac_base + m - 1)
                        i = t.matmul(ps_init[:, (m % 2) * 512:(m % 2) * 512 + 256],
                                     self.g1sb[:, m * 128:(m + 1) * 128], self.g2sb[:, :],
                                     start=True, stop=True)
                        self.PE.inc(i)
                self.on("tensor", pe)

                def ac(s, pe_base=pe_base, ps_init=ps_init):
                    for m in range(8):
                        s.wait_ge(self.PE.h, pe_base + m + 1)
                        i = s.copy(self.p1sb[:, m * 256:(m + 1) * 256],
                                   ps_init[:, (m % 2) * 512:(m % 2) * 512 + 256])
                        self.AC.inc(i)
                self.on("scalar", ac)
            ac_now = self.AC.n

            def syp(s, wdst=wdst, ac_now=ac_now):
                import os
                s.wait_ge(self.AC.h, ac_now)
                if os.environ.get("ATH_NOSYP"):
                    return
                wt4 = wdst.rearrange("(b e) (d g) -> b d e g", e=16, g=16)
                k = 0
                for m in range(8):
                    for bl in range(4):
                        b = m * 4 + bl
                        self.dma_s(s, (self.sg, self.sout)[k % 2],
                            out=wt4[b, :, :, :],
                            in_=self.p1sb[bl * 32:(bl + 1) * 32, m * 256:(m + 1) * 256]
                            .rearrange("p (e g) -> p e g", e=16))
                        k += 1
            self.on("sync", syp)

        if lvl < 4:
            return
        st_p2 = self.stream_state(self.sg) + self.stream_state(self.sout)

        def sym(s, st_p2=st_p2):
            for tk_ in st_p2:
                s.wait_ge(tk_[0].h, tk_[1])
            mtb = self.q_rm if self.IT >= 8 else self.slotB
            v = mtb.ap().rearrange("p (kc c) -> p kc c", kc=8)
            for (w, kgrp, half) in ((self.wr_h, 0, 0), (self.wi_h, 1, 0),
                                    (self.wi_h, 0, 1), (self.wr_h, 1, 1)):
                for r in range(4):
                    self.dma_s(s, self.tsub,
                        out=v[:, kgrp * 4:(kgrp + 1) * 4,
                              half * 512 + r * 128: half * 512 + (r + 1) * 128],
                        in_=w[r * 128:(r + 1) * 128, :], transpose=True)
        self.on("scalar", sym)
        st_mt = self.stream_state(self.tsub)

        def dvn(v, st_mt=st_mt):
            for tk_ in st_mt:
                v.wait_ge(tk_[0].h, tk_[1])
            mtb = self.q_rm if self.IT >= 8 else self.slotB
            for kc in range(4, 8):
                i = v.tensor_scalar_mul(mtb[:, kc * D:kc * D + 512],
                                        mtb[:, kc * D:kc * D + 512], -1.0)
                self.DV.inc(i)
        self.on("vector", dvn)
        dv_now = self.DV.n

        def sym2(s, dv_now=dv_now):
            s.wait_ge(self.DV.h, dv_now)
            self.dma(s, self.SY, out=self.mt_h,
                     in_=(self.q_rm if self.IT >= 8 else self.slotB)[:, :])
        self.on("sync", sym2)
        self._mt_store_gate = (self.SY, self.SY.n)
        if self.IT < 8:
            self._stage_gates = ([(self.SY, self.SY.n)]
                                 + self.stream_state(self.tsub)
                                 + self.stream_state(self.sg)
                                 + self.stream_state(self.sout))

        if lvl < 5:
            return
        def dvd(v):
            v.wait_ge(self.GP.h, 16 * 4)
            self.emit_derive_dv(v)
        self.on("vector", dvd)
        dv_now = self.DV.n

        def scd(s, dv_now=dv_now):
            s.wait_ge(self.DV.h, dv_now)
            self.emit_derive_sc(s)
        self.on("scalar", scd)

    def emit_derive_dv(self, v):
        i = v.tensor_copy(self.W1n_b[:, :], self.W1n_f[:, :]); self.DV.inc(i)
        i = v.tensor_copy(self.W2n_b[:, :], self.W2n_f[:, :]); self.DV.inc(i)
        i = v.tensor_copy(self.b1r_b[:, :], self.b1r_f[:, :]); self.DV.inc(i)
        i = v.tensor_copy(self.b2r_b[:, :], self.b2r_f[:, :]); self.DV.inc(i)

    def emit_derive_sc(self, s, init=False, part="all"):
        dma = (lambda *a, **k: self.dma(s, self.SC, *a, **k)) if init else \
              (lambda *a, **k: self.dma_s(s, self.tder, *a, **k))
        if part in ("all", "w1"):
            w1toks = []
            for mc in range(2):
                w1toks.append(dma(
                    out=self.W1T_b.ap().rearrange("p (hc m) -> p hc m", hc=8)[:, :, mc * 128:(mc + 1) * 128],
                    in_=self.W1n_b[:, mc * D:(mc + 1) * D], transpose=True))
            if not init:
                self._w1toks = w1toks
        if part in ("all", "w2"):
            w2toks = []
            for jc in range(8):
                w2toks.append(dma(
                    out=self.W2T_b.ap().rearrange("p (m2c m) -> p m2c m", m2c=2)[:, :, jc * 128:(jc + 1) * 128],
                    in_=self.W2n_b[:, jc * 256:(jc + 1) * 256], transpose=True))
            if not init:
                self._w2toks = w2toks

    def emit_A(self, t):
        nc = self.nc
        R, IT, NS, ITS = self.R, self.IT, self.NS, self.ITS

        pe_slotA = self.pe_qdone
        pe_slotB = self.pe_ffndone

        toks = []

        gates = list(getattr(self, "_init_gates", []) or [])
        self._init_gates = None

        def syA(s, pe_slotA=pe_slotA, pe_slotB=pe_slotB, gates=gates):
            s.wait_ge(self.PE.h, pe_slotA)
            if pe_slotB:
                s.wait_ge(self.PE.h, pe_slotB)
            for sem, v in gates:
                s.wait_ge(sem.h, v)
            toks.append(self.dma_s(s, self.sslot, out=self.slotA[:, :], in_=self.tk_h))
            toks.append(self.dma_s(s, self.sslot, out=self.slotB[:, :], in_=self.tv_h))
        self.on("sync", syA)
        slot_toks = list(toks)

        with nc.psum_tensor("zps0_%d" % t, [128, D], F32) as zps0, \
             nc.psum_tensor("zps1_%d" % t, [128, D], F32) as zps1, \
             nc.psum_tensor("yps0_%d" % t, [128, D], F32) as yps0, \
             nc.psum_tensor("yps1_%d" % t, [128, D], F32) as yps1:
            zps = [zps0, zps1]
            yps = [yps0, yps1]
            a_ac0 = self.AC.n
            evacs = {}
            for it in range(IT):
                self.emit_ln_x(t, it, self.lnxr, it % 2)

                dv_lnc = self.DV.n
                ttok = []

                def sct(s, it=it, dv_lnc=dv_lnc):
                    s.wait_ge(self.DV.h, dv_lnc)
                    ttok.append(self.dma_s(s, self.tlnx,
                             out=self.lnx_fm.ap().rearrange("p (c m) -> p c m", m=128)[:, it * 8:(it + 1) * 8, :],
                             in_=self.lnxr[:, (it % 2) * D:(it % 2 + 1) * D], transpose=True))
                self.on("scalar", sct)
                self.lngate[it % 2] = ttok[0]
                lnx_tok = ttok[0]

                prev_evac = evacs.get(it - 2)

                def pezy(tt, it=it, lnx_tok=lnx_tok, prev_evac=prev_evac,
                         slot_toks=slot_toks, a_ac0=a_ac0):
                    self.wtok(tt, lnx_tok)
                    for tk_ in slot_toks:
                        self.wtok(tt, tk_)
                    if prev_evac is not None:
                        tt.wait_ge(self.AC.h, prev_evac[0])
                        tt.wait_ge(self.DV.h, prev_evac[1])
                    else:
                        tt.wait_ge(self.AC.h, a_ac0)
                    for dst, w in ((zps[it % 2], self.slotA), (yps[it % 2], self.slotB)):
                        for h in range(2):
                            for dc in range(8):
                                i = tt.matmul(dst[:, h * 512:(h + 1) * 512],
                                              self.lnx_fm[:, (it * 8 + dc) * 128:(it * 8 + dc + 1) * 128],
                                              w[:, dc * D + h * 512: dc * D + (h + 1) * 512],
                                              start=(dc == 0), stop=(dc == 7))
                            self.PE.inc(i)
                self.on("tensor", pezy)
                pe_n = self.PE.n

                def acz(s, it=it, pe_n=pe_n):
                    s.wait_ge(self.PE.h, pe_n - 2)
                    i = s.copy(self.z_rm[:, it * D:(it + 1) * D], zps[it % 2][:, :])
                    self.AC.inc(i)
                self.on("scalar", acz)
                ac_z = self.AC.n

                def dvy(v, it=it, pe_n=pe_n, ac_z=ac_z):
                    v.wait_ge(self.PE.h, pe_n)
                    v.wait_ge(self.AC.h, ac_z)
                    i = v.tensor_tensor(self.y_rm[:, it * D:(it + 1) * D],
                                        self.z_rm[:, it * D:(it + 1) * D],
                                        yps[it % 2][:, :], ALU.subtract)
                    self.DV.inc(i)
                self.on("vector", dvy)
                evacs[it] = (self.AC.n, self.DV.n)

    def emit_q(self, t):
        nc = self.nc
        R, IT = self.R, self.IT
        pe_zy = self.PE.n

        qtk = []

        mtg = getattr(self, "_mt_gate_once", None)
        self._mt_gate_once = None

        def syq(s, pe_zy=pe_zy, mtg=mtg):
            if mtg is not None:
                s.wait_ge(mtg[0].h, mtg[1])
            s.wait_ge(self.PE.h, pe_zy)
            qtk.append(self.dma_s(s, self.sslot, out=self.slotA[:, :], in_=self.tq_h))
        self.on("sync", syq)
        slot_tok = qtk[0]
        lnx_state = self.stream_state(self.tlnx)
        a_ac = self.AC.n
        a_dv = self.DV.n

        with nc.psum_tensor("qps0_%d" % t, [128, D], F32) as qps0, \
             nc.psum_tensor("qps1_%d" % t, [128, D], F32) as qps1:
            qps = [qps0, qps1]
            qevac = {}
            for it in range(IT):
                prev = qevac.get(it - 2)

                def peq(tt, it=it, prev=prev, slot_tok=slot_tok, a_ac=a_ac, a_dv=a_dv,
                        lnx_state=lnx_state):
                    self.wtok(tt, slot_tok)
                    for tk_ in lnx_state:
                        tt.wait_ge(tk_[0].h, tk_[1])
                    if prev is not None:
                        tt.wait_ge(self.AC.h, prev)
                    else:
                        tt.wait_ge(self.AC.h, a_ac)
                        tt.wait_ge(self.DV.h, a_dv)
                    for h in range(2):
                        for dc in range(8):
                            i = tt.matmul(qps[it % 2][:, h * 512:(h + 1) * 512],
                                          self.lnx_fm[:, (it * 8 + dc) * 128:(it * 8 + dc + 1) * 128],
                                          self.slotA[:, dc * D + h * 512: dc * D + (h + 1) * 512],
                                          start=(dc == 0), stop=(dc == 7))
                        self.PE.inc(i)
                self.on("tensor", peq)
                pe_n = self.PE.n

                def acq(s, it=it, pe_n=pe_n):
                    s.wait_ge(self.PE.h, pe_n)
                    i = s.copy(self.q_rm[:, it * D:(it + 1) * D], qps[it % 2][:, :])
                    self.AC.inc(i)
                self.on("scalar", acq)
                qevac[it] = self.AC.n
        self.pe_qdone = self.PE.n

    def emit_D(self, t):
        nc = self.nc
        pe_b = self.PE.n

        mtk = []

        def syd(s, pe_b=pe_b):
            s.wait_ge(self.PE.h, pe_b)
            mtk.append(self.dma_s(s, self.sslot, out=self.slotB[:, :], in_=self.mt_h))
        self.on("sync", syd)
        self.mt_tok = mtk[0]
        for s_i in range(self.NS):
            self.recon_subtile(t, s_i)
        self.pe_ffndone = self.PE.n

    def emit_ln_x(self, t, it, dstbuf, slot):
        R = self.R
        gate = self.xgate[self.xuse % 2]
        xslot = self.xuse % 2
        self.xuse += 1

        xt = []

        def syx(s, t=t, it=it, gate=gate, xslot=xslot):
            if gate:
                s.wait_ge(self.DV.h, gate)
            xt.append(self.dma_s(s, self.sx, out=self.x01[:, xslot * D:(xslot + 1) * D],
                      in_=self.x[t * R + it * 128: t * R + (it + 1) * 128, :]))
        self.on("sync", syx)
        x_tok = xt[0]
        lngate = self.lngate[slot]

        def dvln(v, x_tok=x_tok, xslot=xslot):
            self.wtok(v, x_tok)
            sl = self.x01[:, xslot * D:(xslot + 1) * D]
            v.bn_stats(self.st6[:, 0:6], sl[:, 0:512])
            i = v.bn_stats(self.st6[:, 6:12], sl[:, 512:1024]); self.DV.inc(i)
            v.wait_ge(self.DV.h, self.DV.n)
            i = v.bn_aggr(self.mv[:, :], self.st6[:, :]); self.DV.inc(i)
        self.on("vector", dvln)
        dv_n = self.DV.n

        def acln(s, dv_n=dv_n):
            s.wait_ge(self.DV.h, dv_n)
            i = s.activation(self.rstd[:, :], self.mv[:, 1:2], AF.Sqrt,
                             bias=self.epsc[:, 0:1], scale=1.0)
            self.AC.inc(i)
        self.on("scalar", acln)
        ac_n = self.AC.n

        def dvln2(v, ac_n=ac_n, xslot=xslot, slot=slot, lngate=lngate):
            v.wait_ge(self.AC.h, ac_n)
            self.wtok(v, lngate)
            i = v.reciprocal(self.rstd[:, :], self.rstd[:, :]); self.DV.inc(i)
            v.wait_ge(self.DV.h, self.DV.n)
            sl = self.x01[:, xslot * D:(xslot + 1) * D]
            i = v.tensor_scalar(dstbuf[:, slot * D:(slot + 1) * D],
                                sl, self.mv[:, 0:1], self.rstd[:, 0:1],
                                ALU.subtract, ALU.mult)
            self.DV.inc(i)
        self.on("vector", dvln2)
        self.xgate[xslot] = self.DV.n

    def chain_subtile(self, t, s_i):
        nc = self.nc
        ITS = self.ITS
        base_it = s_i * ITS
        ac_pre = self.AC.n
        dv_pre = self.DV.n
        pe_pre = self.PE.n

        ztoks = []

        def scz(s, base_it=base_it, ac_pre=ac_pre, pe_pre=pe_pre):
            s.wait_ge(self.AC.h, ac_pre)
            s.wait_ge(self.PE.h, pe_pre)
            for il in range(ITS):
                it = base_it + il
                ztoks.append(self.dma_s(s, self.tsub,
                    out=self.zfm.ap().rearrange("p (c m) -> p c m", m=128)[:, il * 8:(il + 1) * 8, :],
                    in_=self.z_rm[:, it * D:(it + 1) * D], transpose=True))
        self.on("scalar", scz)
        st_der = self.stream_state(self.tder)

        with nc.psum_tensor("aps0_%d_%d" % (t, s_i), [128, 512], F32) as aps0, \
             nc.psum_tensor("aps1_%d_%d" % (t, s_i), [128, 512], F32) as aps1, \
             nc.psum_tensor("rps0_%d_%d" % (t, s_i), [128, D], F32) as rps0, \
             nc.psum_tensor("rps1_%d_%d" % (t, s_i), [128, D], F32) as rps1:
            aps = [aps0, aps1]
            rps = [rps0, rps1]
            h_done = {}
            acw = {}
            htoks = []
            for il in range(ITS):
                prev_ac = acw.get(il - 2, 0)

                def pea(tt, il=il, st_der=st_der, prev_ac=prev_ac,
                        ac_pre=ac_pre, dv_pre=dv_pre):
                    for tk_ in [ztoks[il]] + st_der:
                        tt.wait_ge(tk_[0].h, tk_[1])
                    if prev_ac:
                        tt.wait_ge(self.AC.h, prev_ac)
                    else:
                        tt.wait_ge(self.AC.h, ac_pre)
                        tt.wait_ge(self.DV.h, dv_pre)
                    for hc in range(8):
                        tt.matmul(aps[il % 2][:, 0:256],
                                  self.zfm[:, (il * 8 + hc) * 128:(il * 8 + hc + 1) * 128],
                                  self.W1T_b[:, hc * 256:(hc + 1) * 256],
                                  start=(hc == 0), stop=False)
                    i = tt.matmul(aps[il % 2][:, 0:256], self.ones_r[:, :],
                                  self.b1r_b[:, :], start=False, stop=True)
                    self.PE.inc(i)
                self.on("tensor", pea)
                pe_a = self.PE.n
                dv_gate = h_done.get(il - 1, (0, 0))[1]

                def aca(s, il=il, pe_a=pe_a, dv_gate=dv_gate):
                    s.wait_ge(self.PE.h, pe_a)
                    if dv_gate:
                        s.wait_ge(self.DV.h, dv_gate)
                    ap = aps[il % 2][:, 0:256]
                    sl = slice((il % 2) * 256, (il % 2 + 1) * 256)
                    i = s.activation(self.hrm[:, il * 256:(il + 1) * 256], ap, AF.Gelu)
                    self.AC.inc(i)
                    i = s.copy(self.xa01[:, sl], ap); self.AC.inc(i)
                    i = s.activation(self.sq01[:, sl], ap, AF.Square); self.AC.inc(i)
                self.on("scalar", aca)
                acw[il] = self.AC.n
                ac_hil = self.AC.n

                def schil(s, il=il, ac_hil=ac_hil):
                    s.wait_ge(self.AC.h, ac_hil)
                    htoks.append(self.dma_s(s, self.tsub,
                        out=self.hfm.ap().rearrange("p (c m) -> p c m", m=128)[:, il * 2:(il + 1) * 2, :],
                        in_=self.hrm[:, il * 256:(il + 1) * 256], transpose=True))
                self.on("scalar", schil)

            def acE(s):
                s.wait_ge(self.AC.h, self.AC.n)
                for il in range(ITS):
                    sl = slice((il % 2) * 256, (il % 2 + 1) * 256)
                    i = s.activation(self.ex01[:, sl], self.sq01[:, sl], AF.Exp, scale=-0.5)
                    self.AC.inc(i)
            self.on("scalar", acE)
            ac_a = self.AC.n
            for il in range(ITS):
                def dvg(v, il=il, ac_a=ac_a):
                    v.wait_ge(self.AC.h, ac_a)
                    v.wait_ge(self.DV.h, self.DV.n)
                    sl = slice((il % 2) * 256, (il % 2 + 1) * 256)
                    i = v.tensor_single_scalar(self.ms01[:, :], self.sq01[:, sl], 1e-24, ALU.is_lt)
                    self.DV.inc(i)
                    i = v.tensor_copy(self.xs01[:, :], self.xa01[:, sl]); self.DV.inc(i)
                    v.wait_ge(self.DV.h, self.DV.n)
                    i = v.copy_predicated(self.xs01[:, :], self.ms01[:, :], self.epsT[:, :])
                    self.DV.inc(i)
                    v.wait_ge(self.DV.h, self.DV.n)
                    i = v.reciprocal(self.xs01[:, :], self.xs01[:, :]); self.DV.inc(i)
                    v.wait_ge(self.DV.h, self.DV.n)
                    i = v.tensor_tensor(self.ph01[:, :], self.hrm[:, il * 256:(il + 1) * 256],
                                        self.xs01[:, :], ALU.mult)
                    self.DV.inc(i)
                    v.wait_ge(self.DV.h, self.DV.n)
                    i = v.copy_predicated(self.ph01[:, :], self.ms01[:, :], self.halfT[:, :])
                    self.DV.inc(i)
                    i = v.tensor_tensor(self.xs01[:, :], self.xa01[:, sl], self.ex01[:, sl], ALU.mult)
                    self.DV.inc(i)
                    v.wait_ge(self.DV.h, self.DV.n)
                    i = v.scalar_tensor_tensor(self.gpb[:, il * 256:(il + 1) * 256],
                                               self.xs01[:, :], GELU_C, self.ph01[:, :],
                                               ALU.mult, ALU.add)
                    self.DV.inc(i)
                self.on("vector", dvg)
                h_done[il] = (self.AC.n, self.DV.n)



            edv = {}
            etoks = []
            for il in range(ITS):
                it = base_it + il
                prev_dv = edv.get(il - 2, 0)

                def per(tt, il=il, prev_dv=prev_dv):
                    self.wtok(tt, htoks[il])
                    if prev_dv:
                        tt.wait_ge(self.DV.h, prev_dv)
                    for h in range(2):
                        for m2c in range(2):
                            tt.matmul(rps[il % 2][:, h * 512:(h + 1) * 512],
                                      self.hfm[:, (il * 2 + m2c) * 128:(il * 2 + m2c + 1) * 128],
                                      self.W2T_b[:, m2c * D + h * 512:m2c * D + (h + 1) * 512],
                                      start=(m2c == 0), stop=False)
                        i = tt.matmul(rps[il % 2][:, h * 512:(h + 1) * 512],
                                      self.ones_r[:, :], self.b2r_b[:, h * 512:(h + 1) * 512],
                                      start=False, stop=True)
                        self.PE.inc(i)
                self.on("tensor", per)
                pe_r = self.PE.n

                def dve(v, il=il, it=it, pe_r=pe_r):
                    v.wait_ge(self.PE.h, pe_r)
                    i = v.tensor_tensor(self.erm[:, il * D:(il + 1) * D], rps[il % 2][:, :],
                                        self.y_rm[:, it * D:(it + 1) * D], ALU.add)
                    self.DV.inc(i)
                self.on("vector", dve)
                edv[il] = self.DV.n
                dv_eil = self.DV.n

                def sceil(s, il=il, dv_eil=dv_eil):
                    s.wait_ge(self.DV.h, dv_eil)
                    etoks.append(self.dma_s(s, self.tsub,
                        out=self.efm.ap().rearrange("p (c m) -> p c m", m=128)[:, il * 8:(il + 1) * 8, :],
                        in_=self.erm[:, il * D:(il + 1) * D], transpose=True))
                self.on("scalar", sceil)

            e_dv = self.DV.n
            gp_dv = self.DV.n
            ac_all = self.AC.n

        with nc.psum_tensor("gw2ps_%d_%d" % (t, s_i), [128, 8 * 256], F32) as gw2ps, \
             nc.psum_tensor("dps0_%d_%d" % (t, s_i), [128, 512], F32) as dps0, \
             nc.psum_tensor("dps1_%d_%d" % (t, s_i), [128, 512], F32) as dps1:
            dps = [dps0, dps1]

            def peg2(tt, e_dv=e_dv, ac_all=ac_all, gw2ps=gw2ps):
                tt.wait_ge(self.DV.h, e_dv)
                tt.wait_ge(self.AC.h, ac_all)
                for jc in range(8):
                    for il in range(ITS):
                        i = tt.matmul(gw2ps[:, jc * 256:(jc + 1) * 256],
                                      self.erm[:, il * D + jc * 128: il * D + (jc + 1) * 128],
                                      self.hrm[:, il * 256:(il + 1) * 256],
                                      start=(il == 0), stop=(il == ITS - 1))
                    self.PE.inc(i)
            self.on("tensor", peg2)
            pe_g2 = self.PE.n

            def dvg2(v, s_i=s_i, pe_g2=pe_g2, gw2ps=gw2ps):
                v.wait_ge(self.PE.h, pe_g2)
                if s_i == 0:
                    i = v.tensor_copy(self.gW2a[:, :], gw2ps[:, :]); self.DV.inc(i)
                else:
                    i = v.tensor_tensor(self.gW2a[:, :], self.gW2a[:, :], gw2ps[:, :], ALU.add)
                    self.DV.inc(i)
            self.on("vector", dvg2)

            dadv = {}
            for il in range(ITS):
                prev_dv = dadv.get(il - 2, 0)

                def ped(tt, il=il, prev_dv=prev_dv):
                    self.wtok(tt, etoks[il])
                    if prev_dv:
                        tt.wait_ge(self.DV.h, prev_dv)
                    for jc in range(8):
                        i = tt.matmul(dps[il % 2][:, 0:256],
                                      self.efm[:, (il * 8 + jc) * 128:(il * 8 + jc + 1) * 128],
                                      self.W2n_b[:, jc * 256:(jc + 1) * 256],
                                      start=(jc == 0), stop=(jc == 7))
                    self.PE.inc(i)
                self.on("tensor", ped)
                pe_d = self.PE.n

                def dvda(v, il=il, pe_d=pe_d):
                    v.wait_ge(self.PE.h, pe_d)
                    i = v.tensor_tensor(self.dab[:, il * 256:(il + 1) * 256],
                                        dps[il % 2][:, 0:256],
                                        self.gpb[:, il * 256:(il + 1) * 256], ALU.mult)
                    self.DV.inc(i)
                self.on("vector", dvda)
                dadv[il] = self.DV.n
            loops_dv = self.DV.n
            loops_ac = self.AC.n
            gw2_dv = self.DV.n

        with nc.psum_tensor("gw1ps_%d_%d" % (t, s_i), [128, 2 * D], F32) as gw1ps, \
             nc.psum_tensor("b2ps_%d_%d" % (t, s_i), [1, D], F32) as b2ps, \
             nc.psum_tensor("b1ps_%d_%d" % (t, s_i), [1, M1], F32) as b1ps:

            def peg1(tt, base_it=base_it, loops_dv=loops_dv, gw1ps=gw1ps,
                     b2ps=b2ps, b1ps=b1ps):
                tt.wait_ge(self.DV.h, loops_dv)
                for m1c in range(2):
                    for h in range(2):
                        for il in range(ITS):
                            it = base_it + il
                            i = tt.matmul(gw1ps[:, m1c * D + h * 512: m1c * D + (h + 1) * 512],
                                          self.dab[:, il * 256 + m1c * 128: il * 256 + (m1c + 1) * 128],
                                          self.z_rm[:, it * D + h * 512: it * D + (h + 1) * 512],
                                          start=(il == 0), stop=(il == ITS - 1))
                        self.PE.inc(i)
                for h in range(2):
                    for il in range(ITS):
                        i = tt.matmul(b2ps[:, h * 512:(h + 1) * 512], self.ones_c[:, :],
                                      self.erm[:, il * D + h * 512: il * D + (h + 1) * 512],
                                      start=(il == 0), stop=(il == ITS - 1))
                    self.PE.inc(i)
                for il in range(ITS):
                    i = tt.matmul(b1ps[:, :], self.ones_c[:, :],
                                  self.dab[:, il * 256:(il + 1) * 256],
                                  start=(il == 0), stop=(il == ITS - 1))
                self.PE.inc(i)
            self.on("tensor", peg1)
            pe_g1 = self.PE.n

            def dvg1(v, s_i=s_i, pe_g1=pe_g1, gw1ps=gw1ps, b2ps=b2ps, b1ps=b1ps):
                v.wait_ge(self.PE.h, pe_g1)
                if s_i == 0:
                    i = v.tensor_copy(self.gW1a[:, :], gw1ps[:, :]); self.DV.inc(i)
                    i = v.tensor_copy(self.b2ar[:, :], b2ps[:, :]); self.DV.inc(i)
                    i = v.tensor_copy(self.b1ar[:, :], b1ps[:, :]); self.DV.inc(i)
                else:
                    i = v.tensor_tensor(self.gW1a[:, :], self.gW1a[:, :], gw1ps[:, :], ALU.add)
                    self.DV.inc(i)
                    i = v.tensor_tensor(self.b2ar[:, :], self.b2ar[:, :], b2ps[:, :], ALU.add)
                    self.DV.inc(i)
                    i = v.tensor_tensor(self.b1ar[:, :], self.b1ar[:, :], b1ps[:, :], ALU.add)
                    self.DV.inc(i)
            self.on("vector", dvg1)

    def ar_start(self, t):
        dv_g = self.DV.n
        g2d = self.g_in[0:128 * 4096].rearrange("(p c) -> p c", c=2048)

        def sya(s, dv_g=dv_g):
            s.wait_ge(self.DV.h, dv_g)
            self.dma_s(s, self.sg, out=g2d[0:128, :], in_=self.gW1a[:, :])
            self.dma_s(s, self.sg, out=g2d[128:256, :], in_=self.gW2a[:, :])
            self.dma_s(s, self.sg, out=self.g_in[128 * 4096:128 * 4096 + 1024][None, :],
                       in_=self.b2ar[:, :])
            self.dma_s(s, self.sg, out=self.g_in[128 * 4096 + 1024:128 * 4096 + 1280][None, :],
                       in_=self.b1ar[:, :])
        self.on("sync", sya)
        st_g = self.stream_state(self.sg)

        def gpc(g, st_g=st_g):
            for tk_ in st_g:
                g.wait_ge(tk_[0].h, tk_[1])
            i = g.collective_compute("AllReduce", ALU.add,
                                     replica_groups=[list(range(NCORES))],
                                     ins=[self.g_in], outs=[self.g_out])
            self.CC.inc(i)
        self.on("gpsimd", gpc)

    def ar_finish(self, t):
        g2o = self.g_out[0:128 * 4096].rearrange("(p c) -> p c", c=2048)
        cc_n = self.CC.n
        rtoks = []

        def syb(g, cc_n=cc_n):
            g.wait_ge(self.CC.h, cc_n)
            for tk_ in self.stream_state(self.sg):
                g.wait_ge(tk_[0].h, tk_[1])
            rtoks.append(self.dma_s(g, self.sgr, out=self.gW1a[:, :], in_=g2o[0:128, :]))
            rtoks.append(self.dma_s(g, self.sgr, out=self.gW2a[:, :], in_=g2o[128:256, :]))
            rtoks.append(self.dma_s(g, self.sgr, out=self.b2ar[:, :],
                         in_=self.g_out[128 * 4096:128 * 4096 + 1024][None, :]))
            rtoks.append(self.dma_s(g, self.sgr, out=self.b1ar[:, :],
                         in_=self.g_out[128 * 4096 + 1024:128 * 4096 + 1280][None, :]))
        self.on("gpsimd", syb)
        c = self.upd_c

        st_ln = self.stream_state(self.tlnx)

        def dvu1(v, c=c, st_ln=st_ln, rtoks=rtoks):
            for tk_ in rtoks:
                v.wait_ge(tk_[0].h, tk_[1])
            for tk_ in st_ln:
                v.wait_ge(tk_[0].h, tk_[1])
            i = v.scalar_tensor_tensor(self.W1n_f[:, :], self.gW1a[:, :], -c,
                                       self.W1n_f[:, :], ALU.mult, ALU.add)
            self.DV.inc(i)
            i = v.scalar_tensor_tensor(self.b1r_f[:, :], self.b1ar[:, :], -c,
                                       self.b1r_f[:, :], ALU.mult, ALU.add)
            self.DV.inc(i)
            i = v.scalar_tensor_tensor(self.b2r_f[:, :], self.b2ar[:, :], -c,
                                       self.b2r_f[:, :], ALU.mult, ALU.add)
            self.DV.inc(i)
            v.wait_ge(self.DV.h, self.DV.n)
            i = v.tensor_copy(self.W1n_b[:, :], self.W1n_f[:, :]); self.DV.inc(i)
            i = v.tensor_copy(self.b1r_b[:, :], self.b1r_f[:, :]); self.DV.inc(i)
            i = v.tensor_copy(self.b2r_b[:, :], self.b2r_f[:, :]); self.DV.inc(i)
        self.on("vector", dvu1)
        dv_u1 = self.DV.n

        def scu1(s, dv_u1=dv_u1):
            s.wait_ge(self.DV.h, dv_u1)
            self.emit_derive_sc(s, part="w1")
        self.on("scalar", scu1)

        def dvu2(v, c=c):
            i = v.scalar_tensor_tensor(self.W2n_f[:, :], self.gW2a[:, :], -c,
                                       self.W2n_f[:, :], ALU.mult, ALU.add)
            self.DV.inc(i)
            v.wait_ge(self.DV.h, self.DV.n)
            i = v.tensor_copy(self.W2n_b[:, :], self.W2n_f[:, :]); self.DV.inc(i)
        self.on("vector", dvu2)
        dv_u2 = self.DV.n

        def scu2(s, dv_u2=dv_u2):
            s.wait_ge(self.DV.h, dv_u2)
            self.emit_derive_sc(s, part="w2")
        self.on("scalar", scu2)
        self.lngate = [self._w1toks[0], self._w1toks[1]]

    def recon_subtile(self, t, s_i):
        nc = self.nc
        R, ITS = self.R, self.ITS
        base_it = s_i * ITS
        st_der = list(getattr(self, "_w1toks", []))
        st_der2 = list(getattr(self, "_w2toks", []))
        dv_pre = self.DV.n
        st_out_pre = self.stream_state(self.sout)

        qtoks = []

        def scq(s, base_it=base_it):
            for il in range(ITS):
                it = base_it + il
                qtoks.append(self.dma_s(s, self.tsub,
                    out=self.zfm.ap().rearrange("p (c m) -> p c m", m=128)[:, il * 8:(il + 1) * 8, :],
                    in_=self.q_rm[:, it * D:(it + 1) * D], transpose=True))
        self.on("scalar", scq)

        h2toks = []
        with nc.psum_tensor("a2ps0_%d_%d" % (t, s_i), [128, 512], F32) as a2ps0, \
             nc.psum_tensor("a2ps1_%d_%d" % (t, s_i), [128, 512], F32) as a2ps1:
            a2ps = [a2ps0, a2ps1]
            h2ac = {}
            for il in range(ITS):
                prev = h2ac.get(il - 2, 0)

                def pea2(tt, il=il, st_der=st_der, prev=prev, dv_pre=dv_pre,
                         a2ps=a2ps):
                    for tk_ in [qtoks[il]] + st_der:
                        tt.wait_ge(tk_[0].h, tk_[1])
                    if prev:
                        tt.wait_ge(self.AC.h, prev)
                    else:
                        tt.wait_ge(self.DV.h, dv_pre)
                    for hc in range(8):
                        tt.matmul(a2ps[il % 2][:, 0:256],
                                  self.zfm[:, (il * 8 + hc) * 128:(il * 8 + hc + 1) * 128],
                                  self.W1T_b[:, hc * 256:(hc + 1) * 256],
                                  start=(hc == 0), stop=False)
                    i = tt.matmul(a2ps[il % 2][:, 0:256], self.ones_r[:, :],
                                  self.b1r_b[:, :], start=False, stop=True)
                    self.PE.inc(i)
                self.on("tensor", pea2)
                pe_a2 = self.PE.n

                def ach2(s, il=il, pe_a2=pe_a2, a2ps=a2ps):
                    s.wait_ge(self.PE.h, pe_a2)
                    i = s.activation(self.hrm[:, il * 256:(il + 1) * 256],
                                     a2ps[il % 2][:, 0:256], AF.Gelu)
                    self.AC.inc(i)
                self.on("scalar", ach2)
                h2ac[il] = self.AC.n
                ac_h2il = self.AC.n

                def sch2il(s, il=il, ac_h2il=ac_h2il):
                    s.wait_ge(self.AC.h, ac_h2il)
                    h2toks.append(self.dma_s(s, self.tsub,
                        out=self.hfm.ap().rearrange("p (c m) -> p c m", m=128)[:, il * 2:(il + 1) * 2, :],
                        in_=self.hrm[:, il * 256:(il + 1) * 256], transpose=True))
                self.on("scalar", sch2il)

        ac_h2all = self.AC.n
        with nc.psum_tensor("ops0_%d_%d" % (t, s_i), [128, D], F32) as ops0, \
             nc.psum_tensor("ops1_%d_%d" % (t, s_i), [128, D], F32) as ops1, \
             nc.psum_tensor("fps0_%d_%d" % (t, s_i), [128, D], F32) as fps0, \
             nc.psum_tensor("fps1_%d_%d" % (t, s_i), [128, D], F32) as fps1:
            ops = [ops0, ops1]
            fps = [fps0, fps1]
            xr_toks = []
            xslots = []
            for il in range(ITS):
                it = base_it + il
                gate = self.xgate[self.xuse % 2]
                xslot = self.xuse % 2
                self.xuse += 1
                xslots.append(xslot)
                xrt = []

                def syxr(s, t=t, it=it, gate=gate, xslot=xslot):
                    if gate:
                        s.wait_ge(self.DV.h, gate)
                    xrt.append(self.dma_s(s, self.sx, out=self.x01[:, xslot * D:(xslot + 1) * D],
                               in_=self.x[t * R + it * 128: t * R + (it + 1) * 128, :]))
                self.on("sync", syxr)
                xr_toks.append(xrt[0])
            ln2dv = {}
            l2toks = []
            for il in range(ITS):
                it = base_it + il
                xr_tok = xr_toks[il]
                xslot = xslots[il]
                prev = ln2dv.get(il - 2, 0)

                def peo(tt, il=il, prev=prev, ops=ops, st_der2=st_der2,
                        ac_h2all=ac_h2all):
                    for tk_ in [h2toks[il]] + st_der2:
                        tt.wait_ge(tk_[0].h, tk_[1])
                    if prev:
                        tt.wait_ge(self.DV.h, prev)
                    else:
                        tt.wait_ge(self.AC.h, ac_h2all)
                    for h in range(2):
                        for m2c in range(2):
                            tt.matmul(ops[il % 2][:, h * 512:(h + 1) * 512],
                                      self.hfm[:, (il * 2 + m2c) * 128:(il * 2 + m2c + 1) * 128],
                                      self.W2T_b[:, m2c * D + h * 512:m2c * D + (h + 1) * 512],
                                      start=(m2c == 0), stop=False)
                        i = tt.matmul(ops[il % 2][:, h * 512:(h + 1) * 512],
                                      self.ones_r[:, :], self.b2r_b[:, h * 512:(h + 1) * 512],
                                      start=False, stop=True)
                        self.PE.inc(i)
                self.on("tensor", peo)
                pe_o = self.PE.n

                def dvx2(v, il=il, it=it, pe_o=pe_o, xr_tok=xr_tok, xslot=xslot,
                         st_out_pre=st_out_pre, ops=ops):
                    v.wait_ge(self.PE.h, pe_o)
                    for tk_ in st_out_pre:
                        v.wait_ge(tk_[0].h, tk_[1])
                    i = v.tensor_tensor(self.x2s[:, il * D:(il + 1) * D], ops[il % 2][:, :],
                                        self.q_rm[:, it * D:(it + 1) * D], ALU.add)
                    self.DV.inc(i)
                    self.wtok(v, xr_tok)
                    v.wait_ge(self.DV.h, self.DV.n)
                    i = v.tensor_tensor(self.x2s[:, il * D:(il + 1) * D],
                                        self.x2s[:, il * D:(il + 1) * D],
                                        self.x01[:, xslot * D:(xslot + 1) * D], ALU.add)
                    self.DV.inc(i)
                    sl = self.x2s[:, il * D:(il + 1) * D]
                    v.wait_ge(self.DV.h, self.DV.n)
                    v.bn_stats(self.st6[:, 0:6], sl[:, 0:512])
                    i = v.bn_stats(self.st6[:, 6:12], sl[:, 512:1024]); self.DV.inc(i)
                    v.wait_ge(self.DV.h, self.DV.n)
                    i = v.bn_aggr(self.mv[:, :], self.st6[:, :]); self.DV.inc(i)
                self.on("vector", dvx2)
                self.xgate[xslot] = self.DV.n
                dv_st = self.DV.n

                def acl2(s, dv_st=dv_st):
                    s.wait_ge(self.DV.h, dv_st)
                    i = s.activation(self.rstd[:, :], self.mv[:, 1:2], AF.Sqrt,
                                     bias=self.epsc[:, 0:1], scale=1.0)
                    self.AC.inc(i)
                self.on("scalar", acl2)
                ac_l2 = self.AC.n

                def dvl2(v, il=il, ac_l2=ac_l2):
                    v.wait_ge(self.AC.h, ac_l2)
                    i = v.reciprocal(self.rstd[:, :], self.rstd[:, :]); self.DV.inc(i)
                    v.wait_ge(self.DV.h, self.DV.n)
                    i = v.tensor_scalar(self.ln2r[:, il * D:(il + 1) * D],
                                        self.x2s[:, il * D:(il + 1) * D],
                                        self.mv[:, 0:1], self.rstd[:, 0:1],
                                        ALU.subtract, ALU.mult)
                    self.DV.inc(i)
                self.on("vector", dvl2)
                ln2dv[il] = self.DV.n
                dv_l2il = self.DV.n

                def scl2il(s, il=il, dv_l2il=dv_l2il):
                    s.wait_ge(self.DV.h, dv_l2il)
                    l2toks.append(self.dma_s(s, self.tsub,
                        out=self.ln2f.ap().rearrange("p (c m) -> p c m", m=128)[:, il * 8:(il + 1) * 8, :],
                        in_=self.ln2r[:, il * D:(il + 1) * D], transpose=True))
                self.on("scalar", scl2il)

            outdv = {}
            for il in range(ITS):
                it = base_it + il
                prev = outdv.get(il - 2, 0)

                def pef(tt, il=il, prev=prev, mt_tok=self.mt_tok, fps=fps,
                        ac_h2all=ac_h2all):
                    self.wtok(tt, l2toks[il])
                    self.wtok(tt, mt_tok)
                    if prev:
                        tt.wait_ge(self.DV.h, prev)
                    else:
                        tt.wait_ge(self.AC.h, ac_h2all)
                    for h in range(2):
                        for kc in range(8):
                            tt.matmul(fps[il % 2][:, h * 512:(h + 1) * 512],
                                      self.ln2f[:, (il * 8 + kc) * 128:(il * 8 + kc + 1) * 128],
                                      self.slotB[:, kc * D + h * 512: kc * D + (h + 1) * 512],
                                      start=(kc == 0), stop=False)
                        i = tt.matmul(fps[il % 2][:, h * 512:(h + 1) * 512],
                                      self.ones_r[:, :], self.fbias_b[:, h * 512:(h + 1) * 512],
                                      start=False, stop=True)
                        self.PE.inc(i)
                self.on("tensor", pef)
                pe_f = self.PE.n

                def dvo(v, il=il, pe_f=pe_f, fps=fps):
                    v.wait_ge(self.PE.h, pe_f)
                    i = v.tensor_tensor(self.x2s[:, il * D:(il + 1) * D],
                                        fps[il % 2][:, :],
                                        self.x2s[:, il * D:(il + 1) * D], ALU.add)
                    self.DV.inc(i)
                self.on("vector", dvo)
                dv_o = self.DV.n
                outdv[il] = dv_o

                def syo(s, t=t, it=it, il=il, dv_o=dv_o):
                    s.wait_ge(self.DV.h, dv_o)
                    self.dma_s(s, self.sout, out=self.y[t * R + it * 128: t * R + (it + 1) * 128, :],
                               in_=self.x2s[:, il * D:(il + 1) * D])
                self.on("sync", syo)


def build_program(R):
    return Builder(R).build()


def kernel(L0_attn_in_seq, L0_attn_t_k, L0_attn_t_v, L0_attn_t_q, L0_attn_W1,
           L0_attn_b1, L0_attn_W2, L0_attn_b2, L0_ffn_G1_R, L0_ffn_G2_R,
           L0_ffn_G1_I, L0_ffn_G2_I, L0_ffn_bias_real, L0_ffn_bias_imag):
    x = np.asarray(L0_attn_in_seq)
    Tn, N, Dn = x.shape
    R = N // NCORES
    nc = build_program(R)
    shared = {
        "t_k": np.ascontiguousarray(L0_attn_t_k),
        "t_v": np.ascontiguousarray(L0_attn_t_v),
        "t_q": np.ascontiguousarray(L0_attn_t_q),
        "W1": np.ascontiguousarray(L0_attn_W1),
        "b1": np.ascontiguousarray(L0_attn_b1),
        "W2": np.ascontiguousarray(L0_attn_W2),
        "b2": np.ascontiguousarray(L0_attn_b2),
        "G1R": np.ascontiguousarray(L0_ffn_G1_R),
        "G2R": np.ascontiguousarray(L0_ffn_G2_R),
        "G1I": np.ascontiguousarray(L0_ffn_G1_I),
        "G2I": np.ascontiguousarray(L0_ffn_G2_I),
        "br": np.ascontiguousarray(L0_ffn_bias_real),
        "bi": np.ascontiguousarray(L0_ffn_bias_imag),
    }
    in_maps = []
    for c in range(NCORES):
        m = dict(shared)
        m["x"] = np.ascontiguousarray(x[:, c * R:(c + 1) * R, :]).reshape(Tn * R, Dn)
        in_maps.append(m)
    res = run_bass_kernel_spmd(nc, in_maps, core_ids=list(range(NCORES)))
    out = np.empty((Tn, N, Dn), dtype=np.float32)
    for c in range(NCORES):
        out[:, c * R:(c + 1) * R, :] = res.results[c]["y"].reshape(Tn, R, Dn)
    return out


# version 18 (final)
# speedup vs baseline: 1.1606x; 1.0787x over previous
import numpy as np
import concourse.bass as bass
import concourse.mybir as mybir
from concourse.bass_utils import run_bass_kernel_spmd

F32 = mybir.dt.float32
BF16 = mybir.dt.bfloat16
U8 = mybir.dt.uint8
AF = mybir.ActivationFunctionType
ALU = mybir.AluOpType

D = 1024
M1 = 256
T = 4
LR = 1e-3
NCORES = 8
GELU_C = float(1.0 / np.sqrt(2.0 * np.pi))


class Sem:
    def __init__(self, nc, name):
        self.h = nc.alloc_semaphore(name=name)
        self.n = 0

    def inc(self, instr, amt=1):
        instr.then_inc(self.h, amt)
        self.n += amt
        return self.n


class Stream:
    def __init__(self, nc, name):
        self.sems = [Sem(nc, name + "_0"), Sem(nc, name + "_1")]
        self.i = 0


class Builder:
    def __init__(self, R):
        self.R = R
        self.IT = R // 128
        self.NS = 4 if self.IT >= 8 else (2 if self.IT >= 2 else 1)
        self.ITS = self.IT // self.NS
        self.upd_c = float(LR * 2.0 / ((NCORES * R) * D))
        nc = bass.Bass("TRN2", target_bir_lowering=False)
        self.nc = nc

        self.x = nc.dram_tensor("x", [T * R, D], F32, kind="ExternalInput").ap()
        self.y = nc.dram_tensor("y", [T * R, D], F32, kind="ExternalOutput").ap()
        self.tk = nc.dram_tensor("t_k", [D, D], F32, kind="ExternalInput").ap()
        self.tv = nc.dram_tensor("t_v", [D, D], F32, kind="ExternalInput").ap()
        self.tq = nc.dram_tensor("t_q", [D, D], F32, kind="ExternalInput").ap()
        self.W1 = nc.dram_tensor("W1", [M1, D], F32, kind="ExternalInput").ap()
        self.b1 = nc.dram_tensor("b1", [M1], F32, kind="ExternalInput").ap()
        self.W2 = nc.dram_tensor("W2", [D, M1], F32, kind="ExternalInput").ap()
        self.b2 = nc.dram_tensor("b2", [D], F32, kind="ExternalInput").ap()
        self.G1R = nc.dram_tensor("G1R", [1, 32, 4, 32], F32, kind="ExternalInput").ap()
        self.G2R = nc.dram_tensor("G2R", [4, 16, 1, 16], F32, kind="ExternalInput").ap()
        self.G1I = nc.dram_tensor("G1I", [1, 32, 4, 32], F32, kind="ExternalInput").ap()
        self.G2I = nc.dram_tensor("G2I", [4, 16, 1, 16], F32, kind="ExternalInput").ap()
        self.br = nc.dram_tensor("br", [512], F32, kind="ExternalInput").ap()
        self.bi = nc.dram_tensor("bi", [512], F32, kind="ExternalInput").ap()
        self.tk_h = nc.dram_tensor("tk_h", [128, 8 * D], BF16).ap()
        self.tv_h = nc.dram_tensor("tv_h", [128, 8 * D], BF16).ap()
        self.tq_h = nc.dram_tensor("tq_h", [128, 8 * D], BF16).ap()
        self.mt_h = nc.dram_tensor("mt_h", [128, 8 * D], BF16).ap()
        self.wr_h = nc.dram_tensor("wr_h", [512, 512], BF16).ap()
        self.wi_h = nc.dram_tensor("wi_h", [512, 512], BF16).ap()
        self.g_in = nc.dram_tensor("g_in", [128 * 4096 + 1024 + 256], F32).ap()
        self.g_out = nc.dram_tensor("g_out", [128 * 4096 + 1024 + 256], F32).ap()

        self.PE = Sem(nc, "sPE")
        self.DV = Sem(nc, "sDV")
        self.AC = Sem(nc, "sAC")
        self.SY = Sem(nc, "sSY")
        self.SC = Sem(nc, "sSC")
        self.GP = Sem(nc, "sGP")
        self.CC = Sem(nc, "sCC")
        self.sx = Stream(nc, "sx")
        self.sout = Stream(nc, "sout")
        self.sslot = Stream(nc, "sslot")
        self.sg = Stream(nc, "sg")
        self.tlnx = Stream(nc, "tlnx")
        self.tsub = Stream(nc, "tsub")
        self.tder = Stream(nc, "tder")
        self.sgr = Stream(nc, "sgr")
        self.xgate = [0, 0]
        self.lngate = [None, None]
        self.xuse = 0
        self.pe_qdone = 0
        self.pe_ffndone = 0

    def on(self, eng, fn):
        getattr(self.block, eng)(fn)

    def dma_s(self, eng, stream, out, in_, **kw):
        sem = stream.sems[stream.i % 2]
        stream.i += 1
        eng.wait_ge(sem.h, sem.n)
        i = eng.dma_start(out=out, in_=in_, **kw)
        sem.inc(i, 16)
        return (sem, sem.n)

    def wtok(self, eng, tok):
        if tok is not None:
            eng.wait_ge(tok[0].h, tok[1])

    def wstream(self, eng, stream):
        for sem in stream.sems:
            if sem.n:
                eng.wait_ge(sem.h, sem.n)

    def stream_state(self, stream):
        return [(sem, sem.n) for sem in stream.sems if sem.n]

    def dma(self, eng, sem, out, in_, **kw):
        eng.wait_ge(sem.h, sem.n)
        i = eng.dma_start(out=out, in_=in_, **kw)
        sem.inc(i, 16)
        return i

    def barrier(self):
        sems = [self.PE, self.DV, self.AC, self.SY, self.SC, self.GP, self.CC]
        for stx in (self.sx, self.sout, self.sslot, self.sg, self.sgr,
                    self.tlnx, self.tsub, self.tder):
            sems += stx.sems
        vals = [(s.h, s.n) for s in sems if s.n > 0]

        def mk(e):
            for h, n in vals:
                e.wait_ge(h, n)
        for eng in ("tensor", "vector", "scalar", "sync", "gpsimd"):
            self.on(eng, mk)

    def build(self):
        nc = self.nc
        import contextlib
        stack = contextlib.ExitStack()
        sb = lambda name, shape, dt: stack.enter_context(nc.sbuf_tensor(name, shape, dt))
        IT, ITS = self.IT, self.ITS
        self.W1n_f = sb("W1n_f", [128, 2 * D], F32)
        self.W2n_f = sb("W2n_f", [128, 8 * M1], F32)
        self.W2n_b = sb("W2n_b", [128, 8 * M1], BF16)
        self.W1T_b = sb("W1T_b", [128, 8 * M1], BF16)
        self.W2T_b = sb("W2T_b", [128, 2 * D], BF16)
        self.b1r_f = sb("b1r_f", [1, M1], F32)
        self.b2r_f = sb("b2r_f", [1, D], F32)
        self.b1r_b = sb("b1r_b", [1, M1], BF16)
        self.b2r_b = sb("b2r_b", [1, D], BF16)
        self.fbias_b = sb("fbias_b", [1, D], BF16)
        self.ones_r = sb("ones_r", [1, 128], BF16)
        self.ones_c = sb("ones_c", [128, 1], BF16)
        self.epsc = sb("epsc", [128, 1], F32)
        self.epsT = sb("epsT", [128, M1], F32)
        self.halfT = sb("halfT", [128, M1], F32)
        self.slotA = sb("slotA", [128, 8 * D], BF16)
        self.slotB = sb("slotB", [128, 8 * D], BF16)
        self.lnx_fm = sb("lnx_fm", [128, IT * D], BF16)
        self.z_rm = sb("z_rm", [128, IT * D], BF16)
        self.y_rm = sb("y_rm", [128, IT * D], BF16)
        self.q_rm = sb("q_rm", [128, IT * D], BF16)
        self.gW1a = sb("gW1a", [128, 2 * D], F32)
        self.gW2a = sb("gW2a", [128, 8 * M1], F32)
        self.b1ar = sb("b1ar", [1, M1], F32)
        self.b2ar = sb("b2ar", [1, D], F32)
        self.st6 = sb("st6", [128, 12], F32)
        self.mv = sb("mv", [128, 2], F32)
        self.rstd = sb("rstd", [128, 1], F32)
        self.x01 = sb("x01", [128, 2 * D], F32)
        self.lnxr = sb("lnxr", [128, 2 * D], BF16)
        self.W1n_b = self.lnxr.ap()
        self.zfm = sb("zfm", [128, ITS * D], BF16)
        self.erm = sb("erm", [128, ITS * D], BF16)
        self.efm = sb("efm", [128, ITS * D], BF16)
        self.hrm = sb("hrm", [128, ITS * M1], BF16)
        self.hfm = sb("hfm", [128, ITS * M1], BF16)
        self.gpb = sb("gpb", [128, ITS * M1], BF16)
        self.dab = sb("dab", [128, ITS * M1], BF16)
        self.xa01 = sb("xa01", [128, 2 * M1], BF16)
        self.sq01 = sb("sq01", [128, 2 * M1], BF16)
        self.ex01 = sb("ex01", [128, 2 * M1], BF16)
        self.ms01 = sb("ms01", [128, M1], U8)
        self.xs01 = sb("xs01", [128, M1], F32)
        self.ph01 = sb("ph01", [128, M1], F32)
        self.x2s = sb("x2s", [128, ITS * D], F32)
        self.ln2r = sb("ln2r", [128, ITS * D], BF16)
        self.ln2f = sb("ln2f", [128, ITS * D], BF16)
        self.p1sb = self.erm[:, 0:2048]
        self.g1sb = self.efm[0:4, 0:1024]
        self.g2sb = self.efm[0:4, 1024:1280]

        import os
        nphase = int(os.environ.get("ATH_PHASES", "99"))
        with nc.Block() as block:
            self.block = block
            self.init_phase()
            if nphase >= 1:
                self._init_gates = list(getattr(self, "_stage_gates", []))
                nT = min(T, nphase)
                self.emit_A(0)
                self._mt_gate_once = getattr(self, "_mt_store_gate", None)
                self.emit_q(0)
                p1g = (self.stream_state(self.sg) + self.stream_state(self.sout)
                       + [(self.GP, self.GP.n)])

                def pre_chain_gate(e, p1g=p1g):
                    for sem, v in p1g:
                        e.wait_ge(sem.h, v)
                self.on("vector", pre_chain_gate)
                self.on("scalar", pre_chain_gate)
                for t in range(nT):
                    for s_i in range(self.NS):
                        self.chain_subtile(t, s_i)
                    self.ar_start(t)
                    if t + 1 < nT:
                        self.emit_A(t + 1)
                    self.ar_finish(t)
                    self.emit_D(t)
                    if t + 1 < nT:
                        self.emit_q(t + 1)
            flush = [(self.SY, self.SY.n)]
            for stx in (self.sx, self.sout, self.sslot, self.sg, self.sgr,
                        self.tlnx, self.tsub, self.tder):
                flush += self.stream_state(stx)

            def fin(s, flush=flush):
                for sem, v in flush:
                    if v:
                        s.wait_ge(sem.h, v)
            self.on("sync", fin)
        stack.close()
        return nc

    def init_phase(self):
        import os
        lvl = int(os.environ.get("ATH_INIT", "99"))
        nc = self.nc

        def gp(g):
            self.dma(g, self.GP, out=self.W1n_f.ap().rearrange("p (c f) -> p c f", c=2),
                     in_=self.W1.rearrange("(c p) f -> p c f", p=128))
            self.dma(g, self.GP, out=self.W2n_f.ap().rearrange("p (c f) -> p c f", c=8),
                        in_=self.W2.rearrange("(c p) f -> p c f", p=128))
            self.dma(g, self.GP, out=self.b1r_f[:, :], in_=self.b1[None, :])
            self.dma(g, self.GP, out=self.b2r_f[:, :], in_=self.b2[None, :])
            self.dma(g, self.GP, out=self.fbias_b[:, 0:512], in_=self.br[None, :])
            self.dma(g, self.GP, out=self.fbias_b[:, 512:1024], in_=self.bi[None, :])
        self.on("gpsimd", gp)

        def dv(v):
            v.memset(self.ones_r[:, :], 1.0)
            v.memset(self.ones_c[:, :], 1.0)
            v.memset(self.epsc[:, :], 1e-5)
            v.memset(self.epsT[:, :], 1e-12)
            i = v.memset(self.halfT[:, :], 0.5)
            self.DV.inc(i)
        self.on("vector", dv)

        if lvl < 2:
            return
        for wi, (src, dst) in enumerate([(self.tk, self.tk_h), (self.tv, self.tv_h),
                                         (self.tq, self.tq_h)]):
            st_before = self.stream_state(self.tsub)

            def gpl(g, src=src, wi=wi, st_before=st_before):
                if wi > 0:
                    for tk_ in st_before:
                        g.wait_ge(tk_[0].h, tk_[1])
                self.dma(g, self.GP, out=self.slotA.ap().rearrange("p (c f) -> p c f", c=8),
                            in_=src.rearrange("(c p) f -> p c f", p=128))
            self.on("gpsimd", gpl)
            gp_now = self.GP.n

            def scl(s, gp_now=gp_now):
                s.wait_ge(self.GP.h, gp_now)
                for hc in range(8):
                    self.dma_s(s, self.tsub,
                        out=self.slotB.ap().rearrange("p (dc m) -> p dc m", dc=8)[:, :, hc * 128:(hc + 1) * 128],
                        in_=self.slotA[:, hc * D:(hc + 1) * D], transpose=True)
            self.on("scalar", scl)
            st_t = self.stream_state(self.tsub)

            def syl(s, dst=dst, st_t=st_t):
                for tk_ in st_t:
                    s.wait_ge(tk_[0].h, tk_[1])
                self.dma(s, self.SY, out=dst, in_=self.slotB[:, :])
            self.on("sync", syl)
            sy_now = self.SY.n

            def scw(s, sy_now=sy_now):
                s.wait_ge(self.SY.h, sy_now)
            self.on("scalar", scw)

        self._stage_gates = ([(self.SY, self.SY.n)]
                             + self.stream_state(self.tsub))
        if lvl < 3:
            return
        for part, wdst in enumerate([self.wr_h, self.wi_h]):
            if part == 0:
                def gpl0(g):
                    self.dma(g, self.GP, out=self.g1sb.rearrange("p (b d) -> p b d", b=32),
                             in_=self.G1R.rearrange("a b c d -> (a c) b d"))
                    self.dma(g, self.GP, out=self.g2sb.rearrange("p (e g) -> p e g", e=16),
                             in_=self.G2R.rearrange("c e f g -> (c f) e g"))
                self.on("gpsimd", gpl0)
            if part == 1:
                sy_prev = self.SY.n

                st_p = self.stream_state(self.sg) + self.stream_state(self.sout)

                def gpl2(g, st_p=st_p):
                    for tk_ in st_p:
                        g.wait_ge(tk_[0].h, tk_[1])
                    self.dma(g, self.GP, out=self.g1sb.rearrange("p (b d) -> p b d", b=32),
                                in_=self.G1I.rearrange("a b c d -> (a c) b d"))
                    self.dma(g, self.GP, out=self.g2sb.rearrange("p (e g) -> p e g", e=16),
                                in_=self.G2I.rearrange("c e f g -> (c f) e g"))
                self.on("gpsimd", gpl2)
            gp_now = self.GP.n
            with nc.psum_tensor("ps_init%d" % part, [128, 1024], F32) as ps_init:
                ac_base = self.AC.n
                pe_base = self.PE.n

                def pe(t, gp_now=gp_now, ac_base=ac_base, ps_init=ps_init):
                    t.wait_ge(self.GP.h, gp_now)
                    for m in range(8):
                        if m >= 2:
                            t.wait_ge(self.AC.h, ac_base + m - 1)
                        i = t.matmul(ps_init[:, (m % 2) * 512:(m % 2) * 512 + 256],
                                     self.g1sb[:, m * 128:(m + 1) * 128], self.g2sb[:, :],
                                     start=True, stop=True)
                        self.PE.inc(i)
                self.on("tensor", pe)

                def ac(s, pe_base=pe_base, ps_init=ps_init):
                    for m in range(8):
                        s.wait_ge(self.PE.h, pe_base + m + 1)
                        i = s.copy(self.p1sb[:, m * 256:(m + 1) * 256],
                                   ps_init[:, (m % 2) * 512:(m % 2) * 512 + 256])
                        self.AC.inc(i)
                self.on("scalar", ac)
            ac_now = self.AC.n

            def syp(s, wdst=wdst, ac_now=ac_now):
                import os
                s.wait_ge(self.AC.h, ac_now)
                if os.environ.get("ATH_NOSYP"):
                    return
                wt4 = wdst.rearrange("(b e) (d g) -> b d e g", e=16, g=16)
                k = 0
                for m in range(8):
                    for bl in range(4):
                        b = m * 4 + bl
                        self.dma_s(s, (self.sg, self.sout)[k % 2],
                            out=wt4[b, :, :, :],
                            in_=self.p1sb[bl * 32:(bl + 1) * 32, m * 256:(m + 1) * 256]
                            .rearrange("p (e g) -> p e g", e=16))
                        k += 1
            self.on("sync", syp)

        if lvl < 4:
            return
        st_p2 = self.stream_state(self.sg) + self.stream_state(self.sout)

        def sym(s, st_p2=st_p2):
            for tk_ in st_p2:
                s.wait_ge(tk_[0].h, tk_[1])
            mtb = self.q_rm if self.IT >= 8 else self.slotB
            v = mtb.ap().rearrange("p (kc c) -> p kc c", kc=8)
            for (w, kgrp, half) in ((self.wr_h, 0, 0), (self.wi_h, 1, 0),
                                    (self.wi_h, 0, 1), (self.wr_h, 1, 1)):
                for r in range(4):
                    self.dma_s(s, self.tsub,
                        out=v[:, kgrp * 4:(kgrp + 1) * 4,
                              half * 512 + r * 128: half * 512 + (r + 1) * 128],
                        in_=w[r * 128:(r + 1) * 128, :], transpose=True)
        self.on("scalar", sym)
        st_mt = self.stream_state(self.tsub)

        def dvn(v, st_mt=st_mt):
            for tk_ in st_mt:
                v.wait_ge(tk_[0].h, tk_[1])
            mtb = self.q_rm if self.IT >= 8 else self.slotB
            for kc in range(4, 8):
                i = v.tensor_scalar_mul(mtb[:, kc * D:kc * D + 512],
                                        mtb[:, kc * D:kc * D + 512], -1.0)
                self.DV.inc(i)
        self.on("vector", dvn)
        dv_now = self.DV.n

        def sym2(s, dv_now=dv_now):
            s.wait_ge(self.DV.h, dv_now)
            self.dma(s, self.SY, out=self.mt_h,
                     in_=(self.q_rm if self.IT >= 8 else self.slotB)[:, :])
        self.on("sync", sym2)
        self._mt_store_gate = (self.SY, self.SY.n)
        if self.IT < 8:
            self._stage_gates = ([(self.SY, self.SY.n)]
                                 + self.stream_state(self.tsub)
                                 + self.stream_state(self.sg)
                                 + self.stream_state(self.sout))

        if lvl < 5:
            return
        def dvd(v):
            v.wait_ge(self.GP.h, 16 * 4)
            self.emit_derive_dv(v)
        self.on("vector", dvd)
        dv_now = self.DV.n

        def scd(s, dv_now=dv_now):
            s.wait_ge(self.DV.h, dv_now)
            self.emit_derive_sc(s)
        self.on("scalar", scd)

    def emit_derive_dv(self, v):
        i = v.tensor_copy(self.W1n_b[:, :], self.W1n_f[:, :]); self.DV.inc(i)
        i = v.tensor_copy(self.W2n_b[:, :], self.W2n_f[:, :]); self.DV.inc(i)
        i = v.tensor_copy(self.b1r_b[:, :], self.b1r_f[:, :]); self.DV.inc(i)
        i = v.tensor_copy(self.b2r_b[:, :], self.b2r_f[:, :]); self.DV.inc(i)

    def emit_derive_sc(self, s, init=False, part="all"):
        dma = (lambda *a, **k: self.dma(s, self.SC, *a, **k)) if init else \
              (lambda *a, **k: self.dma_s(s, self.tder, *a, **k))
        if part in ("all", "w1"):
            w1toks = []
            for mc in range(2):
                w1toks.append(dma(
                    out=self.W1T_b.ap().rearrange("p (hc m) -> p hc m", hc=8)[:, :, mc * 128:(mc + 1) * 128],
                    in_=self.W1n_b[:, mc * D:(mc + 1) * D], transpose=True))
            if not init:
                self._w1toks = w1toks
        if part in ("all", "w2"):
            w2toks = []
            for jc in range(8):
                w2toks.append(dma(
                    out=self.W2T_b.ap().rearrange("p (m2c m) -> p m2c m", m2c=2)[:, :, jc * 128:(jc + 1) * 128],
                    in_=self.W2n_b[:, jc * 256:(jc + 1) * 256], transpose=True))
            if not init:
                self._w2toks = w2toks

    def emit_A(self, t):
        nc = self.nc
        R, IT, NS, ITS = self.R, self.IT, self.NS, self.ITS

        pe_slotA = self.pe_qdone
        pe_slotB = self.pe_ffndone

        toks = []

        gates = list(getattr(self, "_init_gates", []) or [])
        self._init_gates = None

        def syA(s, pe_slotA=pe_slotA, pe_slotB=pe_slotB, gates=gates):
            s.wait_ge(self.PE.h, pe_slotA)
            if pe_slotB:
                s.wait_ge(self.PE.h, pe_slotB)
            for sem, v in gates:
                s.wait_ge(sem.h, v)
            toks.append(self.dma_s(s, self.sslot, out=self.slotA[:, :], in_=self.tk_h))
            toks.append(self.dma_s(s, self.sslot, out=self.slotB[:, :], in_=self.tv_h))
        self.on("sync", syA)
        slot_toks = list(toks)

        with nc.psum_tensor("zps0_%d" % t, [128, D], F32) as zps0, \
             nc.psum_tensor("zps1_%d" % t, [128, D], F32) as zps1, \
             nc.psum_tensor("yps0_%d" % t, [128, D], F32) as yps0, \
             nc.psum_tensor("yps1_%d" % t, [128, D], F32) as yps1:
            zps = [zps0, zps1]
            yps = [yps0, yps1]
            a_ac0 = self.AC.n
            evacs = {}
            lnx_toks = []
            for it in range(IT):
                self.emit_ln_x(t, it, self.lnxr, it % 2)

                dv_lnc = self.DV.n
                ttok = []

                def sct(s, it=it, dv_lnc=dv_lnc):
                    s.wait_ge(self.DV.h, dv_lnc)
                    ttok.append(self.dma_s(s, self.tlnx,
                             out=self.lnx_fm.ap().rearrange("p (c m) -> p c m", m=128)[:, it * 8:(it + 1) * 8, :],
                             in_=self.lnxr[:, (it % 2) * D:(it % 2 + 1) * D], transpose=True))
                self.on("scalar", sct)
                self.lngate[it % 2] = ttok[0]
                lnx_toks.append(ttok[0])

            for it in range(IT):
                lnx_tok = lnx_toks[it]
                prev_evac = evacs.get(it - 2)

                def pezy(tt, it=it, lnx_tok=lnx_tok, prev_evac=prev_evac,
                         slot_toks=slot_toks):
                    self.wtok(tt, lnx_tok)
                    for tk_ in slot_toks:
                        self.wtok(tt, tk_)
                    if prev_evac is not None:
                        tt.wait_ge(self.AC.h, prev_evac[0])
                        tt.wait_ge(self.DV.h, prev_evac[1])
                    else:
                        tt.wait_ge(self.AC.h, a_ac0)
                    for dst, w in ((zps[it % 2], self.slotA), (yps[it % 2], self.slotB)):
                        for h in range(2):
                            for dc in range(8):
                                i = tt.matmul(dst[:, h * 512:(h + 1) * 512],
                                              self.lnx_fm[:, (it * 8 + dc) * 128:(it * 8 + dc + 1) * 128],
                                              w[:, dc * D + h * 512: dc * D + (h + 1) * 512],
                                              start=(dc == 0), stop=(dc == 7))
                            self.PE.inc(i)
                self.on("tensor", pezy)
                pe_n = self.PE.n

                def acz(s, it=it, pe_n=pe_n):
                    s.wait_ge(self.PE.h, pe_n - 2)
                    i = s.copy(self.z_rm[:, it * D:(it + 1) * D], zps[it % 2][:, :])
                    self.AC.inc(i)
                self.on("scalar", acz)
                ac_z = self.AC.n

                def dvy(v, it=it, pe_n=pe_n, ac_z=ac_z):
                    v.wait_ge(self.PE.h, pe_n)
                    v.wait_ge(self.AC.h, ac_z)
                    i = v.tensor_tensor(self.y_rm[:, it * D:(it + 1) * D],
                                        self.z_rm[:, it * D:(it + 1) * D],
                                        yps[it % 2][:, :], ALU.subtract)
                    self.DV.inc(i)
                self.on("vector", dvy)
                evacs[it] = (self.AC.n, self.DV.n)

    def emit_q(self, t):
        nc = self.nc
        R, IT = self.R, self.IT
        pe_zy = self.PE.n

        qtk = []

        mtg = getattr(self, "_mt_gate_once", None)
        self._mt_gate_once = None

        def syq(s, pe_zy=pe_zy, mtg=mtg):
            if mtg is not None:
                s.wait_ge(mtg[0].h, mtg[1])
            s.wait_ge(self.PE.h, pe_zy)
            qtk.append(self.dma_s(s, self.sslot, out=self.slotA[:, :], in_=self.tq_h))
        self.on("sync", syq)
        slot_tok = qtk[0]
        lnx_state = self.stream_state(self.tlnx)
        a_ac = self.AC.n
        a_dv = self.DV.n

        with nc.psum_tensor("qps0_%d" % t, [128, D], F32) as qps0, \
             nc.psum_tensor("qps1_%d" % t, [128, D], F32) as qps1:
            qps = [qps0, qps1]
            qevac = {}
            for it in range(IT):
                prev = qevac.get(it - 2)

                def peq(tt, it=it, prev=prev, slot_tok=slot_tok, a_ac=a_ac, a_dv=a_dv,
                        lnx_state=lnx_state):
                    self.wtok(tt, slot_tok)
                    for tk_ in lnx_state:
                        tt.wait_ge(tk_[0].h, tk_[1])
                    if prev is not None:
                        tt.wait_ge(self.AC.h, prev)
                    else:
                        tt.wait_ge(self.AC.h, a_ac)
                        tt.wait_ge(self.DV.h, a_dv)
                    for h in range(2):
                        for dc in range(8):
                            i = tt.matmul(qps[it % 2][:, h * 512:(h + 1) * 512],
                                          self.lnx_fm[:, (it * 8 + dc) * 128:(it * 8 + dc + 1) * 128],
                                          self.slotA[:, dc * D + h * 512: dc * D + (h + 1) * 512],
                                          start=(dc == 0), stop=(dc == 7))
                        self.PE.inc(i)
                self.on("tensor", peq)
                pe_n = self.PE.n

                def acq(s, it=it, pe_n=pe_n):
                    s.wait_ge(self.PE.h, pe_n)
                    i = s.copy(self.q_rm[:, it * D:(it + 1) * D], qps[it % 2][:, :])
                    self.AC.inc(i)
                self.on("scalar", acq)
                qevac[it] = self.AC.n
        self.pe_qdone = self.PE.n

    def emit_D(self, t):
        nc = self.nc
        pe_b = self.PE.n

        mtk = []

        def syd(s, pe_b=pe_b):
            s.wait_ge(self.PE.h, pe_b)
            mtk.append(self.dma_s(s, self.sslot, out=self.slotB[:, :], in_=self.mt_h))
        self.on("sync", syd)
        self.mt_tok = mtk[0]
        for s_i in range(self.NS):
            self.recon_subtile(t, s_i)
        self.pe_ffndone = self.PE.n

    def emit_ln_x(self, t, it, dstbuf, slot):
        R = self.R
        gate = self.xgate[self.xuse % 2]
        xslot = self.xuse % 2
        self.xuse += 1

        xt = []

        def syx(s, t=t, it=it, gate=gate, xslot=xslot):
            if gate:
                s.wait_ge(self.DV.h, gate)
            xt.append(self.dma_s(s, self.sx, out=self.x01[:, xslot * D:(xslot + 1) * D],
                      in_=self.x[t * R + it * 128: t * R + (it + 1) * 128, :]))
        self.on("sync", syx)
        x_tok = xt[0]
        lngate = self.lngate[slot]

        def dvln(v, x_tok=x_tok, xslot=xslot):
            self.wtok(v, x_tok)
            sl = self.x01[:, xslot * D:(xslot + 1) * D]
            v.bn_stats(self.st6[:, 0:6], sl[:, 0:512])
            i = v.bn_stats(self.st6[:, 6:12], sl[:, 512:1024]); self.DV.inc(i)
            v.wait_ge(self.DV.h, self.DV.n)
            i = v.bn_aggr(self.mv[:, :], self.st6[:, :]); self.DV.inc(i)
        self.on("vector", dvln)
        dv_n = self.DV.n

        def acln(s, dv_n=dv_n):
            s.wait_ge(self.DV.h, dv_n)
            i = s.activation(self.rstd[:, :], self.mv[:, 1:2], AF.Sqrt,
                             bias=self.epsc[:, 0:1], scale=1.0)
            self.AC.inc(i)
        self.on("scalar", acln)
        ac_n = self.AC.n

        def dvln2(v, ac_n=ac_n, xslot=xslot, slot=slot, lngate=lngate):
            v.wait_ge(self.AC.h, ac_n)
            self.wtok(v, lngate)
            i = v.reciprocal(self.rstd[:, :], self.rstd[:, :]); self.DV.inc(i)
            v.wait_ge(self.DV.h, self.DV.n)
            sl = self.x01[:, xslot * D:(xslot + 1) * D]
            i = v.tensor_scalar(dstbuf[:, slot * D:(slot + 1) * D],
                                sl, self.mv[:, 0:1], self.rstd[:, 0:1],
                                ALU.subtract, ALU.mult)
            self.DV.inc(i)
        self.on("vector", dvln2)
        self.xgate[xslot] = self.DV.n

    def chain_subtile(self, t, s_i):
        nc = self.nc
        ITS = self.ITS
        base_it = s_i * ITS
        ac_pre = self.AC.n
        dv_pre = self.DV.n
        pe_pre = self.PE.n

        ztoks = []

        def scz(s, base_it=base_it, ac_pre=ac_pre, pe_pre=pe_pre):
            s.wait_ge(self.AC.h, ac_pre)
            s.wait_ge(self.PE.h, pe_pre)
            for il in range(ITS):
                it = base_it + il
                ztoks.append(self.dma_s(s, self.tsub,
                    out=self.zfm.ap().rearrange("p (c m) -> p c m", m=128)[:, il * 8:(il + 1) * 8, :],
                    in_=self.z_rm[:, it * D:(it + 1) * D], transpose=True))
        self.on("scalar", scz)
        st_der = self.stream_state(self.tder)

        with nc.psum_tensor("aps0_%d_%d" % (t, s_i), [128, 512], F32) as aps0, \
             nc.psum_tensor("aps1_%d_%d" % (t, s_i), [128, 512], F32) as aps1, \
             nc.psum_tensor("rps0_%d_%d" % (t, s_i), [128, D], F32) as rps0, \
             nc.psum_tensor("rps1_%d_%d" % (t, s_i), [128, D], F32) as rps1:
            aps = [aps0, aps1]
            rps = [rps0, rps1]
            h_done = {}
            acw = {}
            htoks = []
            for il in range(ITS):
                prev_ac = acw.get(il - 2, 0)

                def pea(tt, il=il, st_der=st_der, prev_ac=prev_ac,
                        ac_pre=ac_pre, dv_pre=dv_pre):
                    for tk_ in [ztoks[il]] + st_der:
                        tt.wait_ge(tk_[0].h, tk_[1])
                    if prev_ac:
                        tt.wait_ge(self.AC.h, prev_ac)
                    else:
                        tt.wait_ge(self.AC.h, ac_pre)
                        tt.wait_ge(self.DV.h, dv_pre)
                    for hc in range(8):
                        tt.matmul(aps[il % 2][:, 0:256],
                                  self.zfm[:, (il * 8 + hc) * 128:(il * 8 + hc + 1) * 128],
                                  self.W1T_b[:, hc * 256:(hc + 1) * 256],
                                  start=(hc == 0), stop=False)
                    i = tt.matmul(aps[il % 2][:, 0:256], self.ones_r[:, :],
                                  self.b1r_b[:, :], start=False, stop=True)
                    self.PE.inc(i)
                self.on("tensor", pea)
                pe_a = self.PE.n
                dv_gate = h_done.get(il - 1, (0, 0))[1]

                def aca(s, il=il, pe_a=pe_a, dv_gate=dv_gate):
                    s.wait_ge(self.PE.h, pe_a)
                    if dv_gate:
                        s.wait_ge(self.DV.h, dv_gate)
                    ap = aps[il % 2][:, 0:256]
                    sl = slice((il % 2) * 256, (il % 2 + 1) * 256)
                    i = s.activation(self.hrm[:, il * 256:(il + 1) * 256], ap, AF.Gelu)
                    self.AC.inc(i)
                    i = s.copy(self.xa01[:, sl], ap); self.AC.inc(i)
                    i = s.activation(self.sq01[:, sl], ap, AF.Square); self.AC.inc(i)
                self.on("scalar", aca)
                acw[il] = self.AC.n
                ac_hil = self.AC.n

                def schil(s, il=il, ac_hil=ac_hil):
                    s.wait_ge(self.AC.h, ac_hil)
                    htoks.append(self.dma_s(s, self.tsub,
                        out=self.hfm.ap().rearrange("p (c m) -> p c m", m=128)[:, il * 2:(il + 1) * 2, :],
                        in_=self.hrm[:, il * 256:(il + 1) * 256], transpose=True))
                self.on("scalar", schil)

            def acE(s):
                s.wait_ge(self.AC.h, self.AC.n)
                for il in range(ITS):
                    sl = slice((il % 2) * 256, (il % 2 + 1) * 256)
                    i = s.activation(self.ex01[:, sl], self.sq01[:, sl], AF.Exp, scale=-0.5)
                    self.AC.inc(i)
            self.on("scalar", acE)
            ac_a = self.AC.n
            for il in range(ITS):
                def dvg(v, il=il, ac_a=ac_a):
                    v.wait_ge(self.AC.h, ac_a)
                    v.wait_ge(self.DV.h, self.DV.n)
                    sl = slice((il % 2) * 256, (il % 2 + 1) * 256)
                    i = v.tensor_single_scalar(self.ms01[:, :], self.sq01[:, sl], 1e-24, ALU.is_lt)
                    self.DV.inc(i)
                    i = v.tensor_copy(self.xs01[:, :], self.xa01[:, sl]); self.DV.inc(i)
                    v.wait_ge(self.DV.h, self.DV.n)
                    i = v.copy_predicated(self.xs01[:, :], self.ms01[:, :], self.epsT[:, :])
                    self.DV.inc(i)
                    v.wait_ge(self.DV.h, self.DV.n)
                    i = v.reciprocal(self.xs01[:, :], self.xs01[:, :]); self.DV.inc(i)
                    v.wait_ge(self.DV.h, self.DV.n)
                    i = v.tensor_tensor(self.ph01[:, :], self.hrm[:, il * 256:(il + 1) * 256],
                                        self.xs01[:, :], ALU.mult)
                    self.DV.inc(i)
                    v.wait_ge(self.DV.h, self.DV.n)
                    i = v.copy_predicated(self.ph01[:, :], self.ms01[:, :], self.halfT[:, :])
                    self.DV.inc(i)
                    i = v.tensor_tensor(self.xs01[:, :], self.xa01[:, sl], self.ex01[:, sl], ALU.mult)
                    self.DV.inc(i)
                    v.wait_ge(self.DV.h, self.DV.n)
                    i = v.scalar_tensor_tensor(self.gpb[:, il * 256:(il + 1) * 256],
                                               self.xs01[:, :], GELU_C, self.ph01[:, :],
                                               ALU.mult, ALU.add)
                    self.DV.inc(i)
                self.on("vector", dvg)
                h_done[il] = (self.AC.n, self.DV.n)



            edv = {}
            etoks = []
            for il in range(ITS):
                it = base_it + il
                prev_dv = edv.get(il - 2, 0)

                def per(tt, il=il, prev_dv=prev_dv):
                    self.wtok(tt, htoks[il])
                    if prev_dv:
                        tt.wait_ge(self.DV.h, prev_dv)
                    for h in range(2):
                        for m2c in range(2):
                            tt.matmul(rps[il % 2][:, h * 512:(h + 1) * 512],
                                      self.hfm[:, (il * 2 + m2c) * 128:(il * 2 + m2c + 1) * 128],
                                      self.W2T_b[:, m2c * D + h * 512:m2c * D + (h + 1) * 512],
                                      start=(m2c == 0), stop=False)
                        i = tt.matmul(rps[il % 2][:, h * 512:(h + 1) * 512],
                                      self.ones_r[:, :], self.b2r_b[:, h * 512:(h + 1) * 512],
                                      start=False, stop=True)
                        self.PE.inc(i)
                self.on("tensor", per)
                pe_r = self.PE.n

                def dve(v, il=il, it=it, pe_r=pe_r):
                    v.wait_ge(self.PE.h, pe_r)
                    i = v.tensor_tensor(self.erm[:, il * D:(il + 1) * D], rps[il % 2][:, :],
                                        self.y_rm[:, it * D:(it + 1) * D], ALU.add)
                    self.DV.inc(i)
                self.on("vector", dve)
                edv[il] = self.DV.n
                dv_eil = self.DV.n

                def sceil(s, il=il, dv_eil=dv_eil):
                    s.wait_ge(self.DV.h, dv_eil)
                    etoks.append(self.dma_s(s, self.tsub,
                        out=self.efm.ap().rearrange("p (c m) -> p c m", m=128)[:, il * 8:(il + 1) * 8, :],
                        in_=self.erm[:, il * D:(il + 1) * D], transpose=True))
                self.on("scalar", sceil)

            e_dv = self.DV.n
            gp_dv = self.DV.n
            ac_all = self.AC.n

        with nc.psum_tensor("gw2ps_%d_%d" % (t, s_i), [128, 8 * 256], F32) as gw2ps, \
             nc.psum_tensor("dps0_%d_%d" % (t, s_i), [128, 512], F32) as dps0, \
             nc.psum_tensor("dps1_%d_%d" % (t, s_i), [128, 512], F32) as dps1:
            dps = [dps0, dps1]

            def peg2(tt, e_dv=e_dv, ac_all=ac_all, gw2ps=gw2ps):
                tt.wait_ge(self.DV.h, e_dv)
                tt.wait_ge(self.AC.h, ac_all)
                for jc in range(8):
                    for il in range(ITS):
                        i = tt.matmul(gw2ps[:, jc * 256:(jc + 1) * 256],
                                      self.erm[:, il * D + jc * 128: il * D + (jc + 1) * 128],
                                      self.hrm[:, il * 256:(il + 1) * 256],
                                      start=(il == 0), stop=(il == ITS - 1))
                    self.PE.inc(i)
            self.on("tensor", peg2)
            pe_g2 = self.PE.n

            def dvg2(v, s_i=s_i, pe_g2=pe_g2, gw2ps=gw2ps):
                v.wait_ge(self.PE.h, pe_g2)
                if s_i == 0:
                    i = v.tensor_copy(self.gW2a[:, :], gw2ps[:, :]); self.DV.inc(i)
                else:
                    i = v.tensor_tensor(self.gW2a[:, :], self.gW2a[:, :], gw2ps[:, :], ALU.add)
                    self.DV.inc(i)
            self.on("vector", dvg2)

            dadv = {}
            for il in range(ITS):
                prev_dv = dadv.get(il - 2, 0)

                def ped(tt, il=il, prev_dv=prev_dv):
                    self.wtok(tt, etoks[il])
                    if prev_dv:
                        tt.wait_ge(self.DV.h, prev_dv)
                    for jc in range(8):
                        i = tt.matmul(dps[il % 2][:, 0:256],
                                      self.efm[:, (il * 8 + jc) * 128:(il * 8 + jc + 1) * 128],
                                      self.W2n_b[:, jc * 256:(jc + 1) * 256],
                                      start=(jc == 0), stop=(jc == 7))
                    self.PE.inc(i)
                self.on("tensor", ped)
                pe_d = self.PE.n

                def dvda(v, il=il, pe_d=pe_d):
                    v.wait_ge(self.PE.h, pe_d)
                    i = v.tensor_tensor(self.dab[:, il * 256:(il + 1) * 256],
                                        dps[il % 2][:, 0:256],
                                        self.gpb[:, il * 256:(il + 1) * 256], ALU.mult)
                    self.DV.inc(i)
                self.on("vector", dvda)
                dadv[il] = self.DV.n
            loops_dv = self.DV.n
            loops_ac = self.AC.n
            gw2_dv = self.DV.n

        with nc.psum_tensor("gw1ps_%d_%d" % (t, s_i), [128, 2 * D], F32) as gw1ps, \
             nc.psum_tensor("b2ps_%d_%d" % (t, s_i), [1, D], F32) as b2ps, \
             nc.psum_tensor("b1ps_%d_%d" % (t, s_i), [1, M1], F32) as b1ps:

            def peg1(tt, base_it=base_it, loops_dv=loops_dv, gw1ps=gw1ps,
                     b2ps=b2ps, b1ps=b1ps):
                tt.wait_ge(self.DV.h, loops_dv)
                for m1c in range(2):
                    for h in range(2):
                        for il in range(ITS):
                            it = base_it + il
                            i = tt.matmul(gw1ps[:, m1c * D + h * 512: m1c * D + (h + 1) * 512],
                                          self.dab[:, il * 256 + m1c * 128: il * 256 + (m1c + 1) * 128],
                                          self.z_rm[:, it * D + h * 512: it * D + (h + 1) * 512],
                                          start=(il == 0), stop=(il == ITS - 1))
                        self.PE.inc(i)
                for h in range(2):
                    for il in range(ITS):
                        i = tt.matmul(b2ps[:, h * 512:(h + 1) * 512], self.ones_c[:, :],
                                      self.erm[:, il * D + h * 512: il * D + (h + 1) * 512],
                                      start=(il == 0), stop=(il == ITS - 1))
                    self.PE.inc(i)
                for il in range(ITS):
                    i = tt.matmul(b1ps[:, :], self.ones_c[:, :],
                                  self.dab[:, il * 256:(il + 1) * 256],
                                  start=(il == 0), stop=(il == ITS - 1))
                self.PE.inc(i)
            self.on("tensor", peg1)
            pe_g1 = self.PE.n

            def dvg1(v, s_i=s_i, pe_g1=pe_g1, gw1ps=gw1ps, b2ps=b2ps, b1ps=b1ps):
                v.wait_ge(self.PE.h, pe_g1)
                if s_i == 0:
                    i = v.tensor_copy(self.gW1a[:, :], gw1ps[:, :]); self.DV.inc(i)
                    i = v.tensor_copy(self.b2ar[:, :], b2ps[:, :]); self.DV.inc(i)
                    i = v.tensor_copy(self.b1ar[:, :], b1ps[:, :]); self.DV.inc(i)
                else:
                    i = v.tensor_tensor(self.gW1a[:, :], self.gW1a[:, :], gw1ps[:, :], ALU.add)
                    self.DV.inc(i)
                    i = v.tensor_tensor(self.b2ar[:, :], self.b2ar[:, :], b2ps[:, :], ALU.add)
                    self.DV.inc(i)
                    i = v.tensor_tensor(self.b1ar[:, :], self.b1ar[:, :], b1ps[:, :], ALU.add)
                    self.DV.inc(i)
            self.on("vector", dvg1)

    def ar_start(self, t):
        dv_g = self.DV.n
        g2d = self.g_in[0:128 * 4096].rearrange("(p c) -> p c", c=2048)

        def sya(s, dv_g=dv_g):
            s.wait_ge(self.DV.h, dv_g)
            self.dma_s(s, self.sg, out=g2d[0:128, :], in_=self.gW1a[:, :])
            self.dma_s(s, self.sg, out=g2d[128:256, :], in_=self.gW2a[:, :])
            self.dma_s(s, self.sg, out=self.g_in[128 * 4096:128 * 4096 + 1024][None, :],
                       in_=self.b2ar[:, :])
            self.dma_s(s, self.sg, out=self.g_in[128 * 4096 + 1024:128 * 4096 + 1280][None, :],
                       in_=self.b1ar[:, :])
        self.on("sync", sya)
        st_g = self.stream_state(self.sg)

        def gpc(g, st_g=st_g):
            for tk_ in st_g:
                g.wait_ge(tk_[0].h, tk_[1])
            i = g.collective_compute("AllReduce", ALU.add,
                                     replica_groups=[list(range(NCORES))],
                                     ins=[self.g_in], outs=[self.g_out])
            self.CC.inc(i)
        self.on("gpsimd", gpc)

    def ar_finish(self, t):
        g2o = self.g_out[0:128 * 4096].rearrange("(p c) -> p c", c=2048)
        cc_n = self.CC.n
        rtoks = []

        def syb(g, cc_n=cc_n):
            g.wait_ge(self.CC.h, cc_n)
            for tk_ in self.stream_state(self.sg):
                g.wait_ge(tk_[0].h, tk_[1])
            rtoks.append(self.dma_s(g, self.sgr, out=self.gW1a[:, :], in_=g2o[0:128, :]))
            rtoks.append(self.dma_s(g, self.sgr, out=self.gW2a[:, :], in_=g2o[128:256, :]))
            rtoks.append(self.dma_s(g, self.sgr, out=self.b2ar[:, :],
                         in_=self.g_out[128 * 4096:128 * 4096 + 1024][None, :]))
            rtoks.append(self.dma_s(g, self.sgr, out=self.b1ar[:, :],
                         in_=self.g_out[128 * 4096 + 1024:128 * 4096 + 1280][None, :]))
        self.on("gpsimd", syb)
        c = self.upd_c

        st_ln = self.stream_state(self.tlnx)

        def dvu1(v, c=c, st_ln=st_ln, rtoks=rtoks):
            for tk_ in rtoks:
                v.wait_ge(tk_[0].h, tk_[1])
            for tk_ in st_ln:
                v.wait_ge(tk_[0].h, tk_[1])
            i = v.scalar_tensor_tensor(self.W1n_f[:, :], self.gW1a[:, :], -c,
                                       self.W1n_f[:, :], ALU.mult, ALU.add)
            self.DV.inc(i)
            i = v.scalar_tensor_tensor(self.b1r_f[:, :], self.b1ar[:, :], -c,
                                       self.b1r_f[:, :], ALU.mult, ALU.add)
            self.DV.inc(i)
            i = v.scalar_tensor_tensor(self.b2r_f[:, :], self.b2ar[:, :], -c,
                                       self.b2r_f[:, :], ALU.mult, ALU.add)
            self.DV.inc(i)
            v.wait_ge(self.DV.h, self.DV.n)
            i = v.tensor_copy(self.W1n_b[:, :], self.W1n_f[:, :]); self.DV.inc(i)
            i = v.tensor_copy(self.b1r_b[:, :], self.b1r_f[:, :]); self.DV.inc(i)
            i = v.tensor_copy(self.b2r_b[:, :], self.b2r_f[:, :]); self.DV.inc(i)
        self.on("vector", dvu1)
        dv_u1 = self.DV.n

        def scu1(s, dv_u1=dv_u1):
            s.wait_ge(self.DV.h, dv_u1)
            self.emit_derive_sc(s, part="w1")
        self.on("scalar", scu1)

        def dvu2(v, c=c):
            i = v.scalar_tensor_tensor(self.W2n_f[:, :], self.gW2a[:, :], -c,
                                       self.W2n_f[:, :], ALU.mult, ALU.add)
            self.DV.inc(i)
            v.wait_ge(self.DV.h, self.DV.n)
            i = v.tensor_copy(self.W2n_b[:, :], self.W2n_f[:, :]); self.DV.inc(i)
        self.on("vector", dvu2)
        dv_u2 = self.DV.n

        def scu2(s, dv_u2=dv_u2):
            s.wait_ge(self.DV.h, dv_u2)
            self.emit_derive_sc(s, part="w2")
        self.on("scalar", scu2)
        self.lngate = [self._w1toks[0], self._w1toks[1]]

    def recon_subtile(self, t, s_i):
        nc = self.nc
        R, ITS = self.R, self.ITS
        base_it = s_i * ITS
        st_der = list(getattr(self, "_w1toks", []))
        st_der2 = list(getattr(self, "_w2toks", []))
        dv_pre = self.DV.n
        st_out_pre = self.stream_state(self.sout)

        qtoks = []

        def scq(s, base_it=base_it):
            for il in range(ITS):
                it = base_it + il
                qtoks.append(self.dma_s(s, self.tsub,
                    out=self.zfm.ap().rearrange("p (c m) -> p c m", m=128)[:, il * 8:(il + 1) * 8, :],
                    in_=self.q_rm[:, it * D:(it + 1) * D], transpose=True))
        self.on("scalar", scq)

        h2toks = []
        with nc.psum_tensor("a2ps0_%d_%d" % (t, s_i), [128, 512], F32) as a2ps0, \
             nc.psum_tensor("a2ps1_%d_%d" % (t, s_i), [128, 512], F32) as a2ps1:
            a2ps = [a2ps0, a2ps1]
            h2ac = {}
            for il in range(ITS):
                prev = h2ac.get(il - 2, 0)

                def pea2(tt, il=il, st_der=st_der, prev=prev, dv_pre=dv_pre,
                         a2ps=a2ps):
                    for tk_ in [qtoks[il]] + st_der:
                        tt.wait_ge(tk_[0].h, tk_[1])
                    if prev:
                        tt.wait_ge(self.AC.h, prev)
                    else:
                        tt.wait_ge(self.DV.h, dv_pre)
                    for hc in range(8):
                        tt.matmul(a2ps[il % 2][:, 0:256],
                                  self.zfm[:, (il * 8 + hc) * 128:(il * 8 + hc + 1) * 128],
                                  self.W1T_b[:, hc * 256:(hc + 1) * 256],
                                  start=(hc == 0), stop=False)
                    i = tt.matmul(a2ps[il % 2][:, 0:256], self.ones_r[:, :],
                                  self.b1r_b[:, :], start=False, stop=True)
                    self.PE.inc(i)
                self.on("tensor", pea2)
                pe_a2 = self.PE.n

                def ach2(s, il=il, pe_a2=pe_a2, a2ps=a2ps):
                    s.wait_ge(self.PE.h, pe_a2)
                    i = s.activation(self.hrm[:, il * 256:(il + 1) * 256],
                                     a2ps[il % 2][:, 0:256], AF.Gelu)
                    self.AC.inc(i)
                self.on("scalar", ach2)
                h2ac[il] = self.AC.n
                ac_h2il = self.AC.n

                def sch2il(s, il=il, ac_h2il=ac_h2il):
                    s.wait_ge(self.AC.h, ac_h2il)
                    h2toks.append(self.dma_s(s, self.tsub,
                        out=self.hfm.ap().rearrange("p (c m) -> p c m", m=128)[:, il * 2:(il + 1) * 2, :],
                        in_=self.hrm[:, il * 256:(il + 1) * 256], transpose=True))
                self.on("scalar", sch2il)

        ac_h2all = self.AC.n
        with nc.psum_tensor("ops0_%d_%d" % (t, s_i), [128, D], F32) as ops0, \
             nc.psum_tensor("ops1_%d_%d" % (t, s_i), [128, D], F32) as ops1, \
             nc.psum_tensor("fps0_%d_%d" % (t, s_i), [128, D], F32) as fps0, \
             nc.psum_tensor("fps1_%d_%d" % (t, s_i), [128, D], F32) as fps1:
            ops = [ops0, ops1]
            fps = [fps0, fps1]
            xr_toks = []
            xslots = []
            for il in range(ITS):
                it = base_it + il
                gate = self.xgate[self.xuse % 2]
                xslot = self.xuse % 2
                self.xuse += 1
                xslots.append(xslot)
                xrt = []

                def syxr(s, t=t, it=it, gate=gate, xslot=xslot):
                    if gate:
                        s.wait_ge(self.DV.h, gate)
                    xrt.append(self.dma_s(s, self.sx, out=self.x01[:, xslot * D:(xslot + 1) * D],
                               in_=self.x[t * R + it * 128: t * R + (it + 1) * 128, :]))
                self.on("sync", syxr)
                xr_toks.append(xrt[0])
            ln2dv = {}
            l2toks = []
            for il in range(ITS):
                it = base_it + il
                xr_tok = xr_toks[il]
                xslot = xslots[il]
                prev = ln2dv.get(il - 2, 0)

                def peo(tt, il=il, prev=prev, ops=ops, st_der2=st_der2,
                        ac_h2all=ac_h2all):
                    for tk_ in [h2toks[il]] + st_der2:
                        tt.wait_ge(tk_[0].h, tk_[1])
                    if prev:
                        tt.wait_ge(self.DV.h, prev)
                    else:
                        tt.wait_ge(self.AC.h, ac_h2all)
                    for h in range(2):
                        for m2c in range(2):
                            tt.matmul(ops[il % 2][:, h * 512:(h + 1) * 512],
                                      self.hfm[:, (il * 2 + m2c) * 128:(il * 2 + m2c + 1) * 128],
                                      self.W2T_b[:, m2c * D + h * 512:m2c * D + (h + 1) * 512],
                                      start=(m2c == 0), stop=False)
                        i = tt.matmul(ops[il % 2][:, h * 512:(h + 1) * 512],
                                      self.ones_r[:, :], self.b2r_b[:, h * 512:(h + 1) * 512],
                                      start=False, stop=True)
                        self.PE.inc(i)
                self.on("tensor", peo)
                pe_o = self.PE.n

                def dvx2(v, il=il, it=it, pe_o=pe_o, xr_tok=xr_tok, xslot=xslot,
                         st_out_pre=st_out_pre, ops=ops):
                    v.wait_ge(self.PE.h, pe_o)
                    for tk_ in st_out_pre:
                        v.wait_ge(tk_[0].h, tk_[1])
                    i = v.tensor_tensor(self.x2s[:, il * D:(il + 1) * D], ops[il % 2][:, :],
                                        self.q_rm[:, it * D:(it + 1) * D], ALU.add)
                    self.DV.inc(i)
                    self.wtok(v, xr_tok)
                    v.wait_ge(self.DV.h, self.DV.n)
                    i = v.tensor_tensor(self.x2s[:, il * D:(il + 1) * D],
                                        self.x2s[:, il * D:(il + 1) * D],
                                        self.x01[:, xslot * D:(xslot + 1) * D], ALU.add)
                    self.DV.inc(i)
                    sl = self.x2s[:, il * D:(il + 1) * D]
                    v.wait_ge(self.DV.h, self.DV.n)
                    v.bn_stats(self.st6[:, 0:6], sl[:, 0:512])
                    i = v.bn_stats(self.st6[:, 6:12], sl[:, 512:1024]); self.DV.inc(i)
                    v.wait_ge(self.DV.h, self.DV.n)
                    i = v.bn_aggr(self.mv[:, :], self.st6[:, :]); self.DV.inc(i)
                self.on("vector", dvx2)
                self.xgate[xslot] = self.DV.n
                dv_st = self.DV.n

                def acl2(s, dv_st=dv_st):
                    s.wait_ge(self.DV.h, dv_st)
                    i = s.activation(self.rstd[:, :], self.mv[:, 1:2], AF.Sqrt,
                                     bias=self.epsc[:, 0:1], scale=1.0)
                    self.AC.inc(i)
                self.on("scalar", acl2)
                ac_l2 = self.AC.n

                def dvl2(v, il=il, ac_l2=ac_l2):
                    v.wait_ge(self.AC.h, ac_l2)
                    i = v.reciprocal(self.rstd[:, :], self.rstd[:, :]); self.DV.inc(i)
                    v.wait_ge(self.DV.h, self.DV.n)
                    i = v.tensor_scalar(self.ln2r[:, il * D:(il + 1) * D],
                                        self.x2s[:, il * D:(il + 1) * D],
                                        self.mv[:, 0:1], self.rstd[:, 0:1],
                                        ALU.subtract, ALU.mult)
                    self.DV.inc(i)
                self.on("vector", dvl2)
                ln2dv[il] = self.DV.n
                dv_l2il = self.DV.n

                def scl2il(s, il=il, dv_l2il=dv_l2il):
                    s.wait_ge(self.DV.h, dv_l2il)
                    l2toks.append(self.dma_s(s, self.tsub,
                        out=self.ln2f.ap().rearrange("p (c m) -> p c m", m=128)[:, il * 8:(il + 1) * 8, :],
                        in_=self.ln2r[:, il * D:(il + 1) * D], transpose=True))
                self.on("scalar", scl2il)

            outdv = {}
            for il in range(ITS):
                it = base_it + il
                prev = outdv.get(il - 2, 0)

                def pef(tt, il=il, prev=prev, mt_tok=self.mt_tok, fps=fps,
                        ac_h2all=ac_h2all):
                    self.wtok(tt, l2toks[il])
                    self.wtok(tt, mt_tok)
                    if prev:
                        tt.wait_ge(self.DV.h, prev)
                    else:
                        tt.wait_ge(self.AC.h, ac_h2all)
                    for h in range(2):
                        for kc in range(8):
                            tt.matmul(fps[il % 2][:, h * 512:(h + 1) * 512],
                                      self.ln2f[:, (il * 8 + kc) * 128:(il * 8 + kc + 1) * 128],
                                      self.slotB[:, kc * D + h * 512: kc * D + (h + 1) * 512],
                                      start=(kc == 0), stop=False)
                        i = tt.matmul(fps[il % 2][:, h * 512:(h + 1) * 512],
                                      self.ones_r[:, :], self.fbias_b[:, h * 512:(h + 1) * 512],
                                      start=False, stop=True)
                        self.PE.inc(i)
                self.on("tensor", pef)
                pe_f = self.PE.n

                def dvo(v, il=il, pe_f=pe_f, fps=fps):
                    v.wait_ge(self.PE.h, pe_f)
                    i = v.tensor_tensor(self.x2s[:, il * D:(il + 1) * D],
                                        fps[il % 2][:, :],
                                        self.x2s[:, il * D:(il + 1) * D], ALU.add)
                    self.DV.inc(i)
                self.on("vector", dvo)
                dv_o = self.DV.n
                outdv[il] = dv_o

                def syo(s, t=t, it=it, il=il, dv_o=dv_o):
                    s.wait_ge(self.DV.h, dv_o)
                    self.dma_s(s, self.sout, out=self.y[t * R + it * 128: t * R + (it + 1) * 128, :],
                               in_=self.x2s[:, il * D:(il + 1) * D])
                self.on("sync", syo)


def build_program(R):
    return Builder(R).build()


def kernel(L0_attn_in_seq, L0_attn_t_k, L0_attn_t_v, L0_attn_t_q, L0_attn_W1,
           L0_attn_b1, L0_attn_W2, L0_attn_b2, L0_ffn_G1_R, L0_ffn_G2_R,
           L0_ffn_G1_I, L0_ffn_G2_I, L0_ffn_bias_real, L0_ffn_bias_imag):
    x = np.asarray(L0_attn_in_seq)
    Tn, N, Dn = x.shape
    R = N // NCORES
    nc = build_program(R)
    shared = {
        "t_k": np.ascontiguousarray(L0_attn_t_k),
        "t_v": np.ascontiguousarray(L0_attn_t_v),
        "t_q": np.ascontiguousarray(L0_attn_t_q),
        "W1": np.ascontiguousarray(L0_attn_W1),
        "b1": np.ascontiguousarray(L0_attn_b1),
        "W2": np.ascontiguousarray(L0_attn_W2),
        "b2": np.ascontiguousarray(L0_attn_b2),
        "G1R": np.ascontiguousarray(L0_ffn_G1_R),
        "G2R": np.ascontiguousarray(L0_ffn_G2_R),
        "G1I": np.ascontiguousarray(L0_ffn_G1_I),
        "G2I": np.ascontiguousarray(L0_ffn_G2_I),
        "br": np.ascontiguousarray(L0_ffn_bias_real),
        "bi": np.ascontiguousarray(L0_ffn_bias_imag),
    }
    in_maps = []
    for c in range(NCORES):
        m = dict(shared)
        m["x"] = np.ascontiguousarray(x[:, c * R:(c + 1) * R, :]).reshape(Tn * R, Dn)
        in_maps.append(m)
    res = run_bass_kernel_spmd(nc, in_maps, core_ids=list(range(NCORES)))
    out = np.empty((Tn, N, Dn), dtype=np.float32)
    for c in range(NCORES):
        out[:, c * R:(c + 1) * R, :] = res.results[c]["y"].reshape(Tn, R, Dn)
    return out
